# version 04
# baseline (speedup 1.0000x reference)
import contextlib
import numpy as np
import concourse.bass as bass
import concourse.mybir as mybir
from concourse.bass_utils import run_bass_kernel_spmd

F32 = mybir.dt.float32
F32R = mybir.dt.float32r
BF16 = mybir.dt.bfloat16
AF = mybir.ActivationFunctionType
ALU = mybir.AluOpType
AX = mybir.AxisListType

B, T, D = 8, 2048, 1024
NDT = D // 128
TC = 512
NTC = T // TC
NL = 3
KCONV = 4
NE0, F0 = 8, 2048
NE1, F1 = 16, 2048
NH, DH = 16, 64

PHASE_GROUPS = [["l0a", "l0f", "l1a", "l1f"]]
_DBG = {"on": False}

M_CW = 0
M_CB = 32
M_WRB = 40
M_WIB = 48
M_LSA = 56
M_LA = 64
M_NW = 72
M_NOW = 80
M_GAM = 88
M_BET = 96
M_B2 = 104


class Ctx:
    def __init__(self, nc, es, phases):
        self.nc = nc
        self.es = es
        self.phases = phases
        sb = lambda name, shape, dt: es.enter_context(nc.sbuf_tensor(name, shape, dt))
        self.XR = sb("XR", [128, NDT * T], F32R)
        self.BA = sb("BA", [128, NDT * T], BF16)
        self.BB = sb("BB", [128, NDT * T], BF16)
        self.SC = sb("SC", [128, 4 * T], F32)
        self.WB = sb("WB", [128, 8192], BF16)
        self.WR = sb("WR", [128, 3072], F32R)
        self.ONES = sb("ONES", [128, T], F32)
        self.MISC = sb("MISC", [128, 1024], F32)
        self.MR = sb("MR", [128, 1664], F32R)
        self.OB = sb("OB", [128, 2], BF16)
        self.IDT = sb("IDT", [128, 128], F32)
        self.PS = [es.enter_context(nc.psum_tensor(f"ps{i}", [128, 512], F32))
                   for i in range(8)]
        self.sem = {}
        self.cnt = {}
        for name in ["T", "V", "S", "G", "Y", "O", "C"]:
            self.sem[name] = es.enter_context(nc.semaphore("sem" + name))
            self.cnt[name] = 0
        self.block = es.enter_context(nc.Block())

    def inc(self, ins, name, n=1):
        self.cnt[name] += n
        ins.then_inc(self.sem[name], n)
        return self.cnt[name]

    def wait(self, eng, *names):
        for n in names:
            if self.cnt[n] > 0:
                eng.wait_ge(self.sem[n], self.cnt[n])

    def wait_at(self, eng, name, val):
        if val > 0:
            eng.wait_ge(self.sem[name], val)

    def pb(self, eng, ins):
        self.inc(ins, "V")
        eng.wait_ge(self.sem["V"], self.cnt["V"])

    @staticmethod
    def dcols(j, c0=0, c1=T):
        return slice(j * T + c0, j * T + c1)

    def t1(self, c0=0, c1=T): return self.SC[:, 0 * T + c0:0 * T + c1]
    def t2(self, c0=0, c1=T): return self.SC[:, 1 * T + c0:1 * T + c1]
    def t3(self, c0=0, c1=T): return self.SC[:, 2 * T + c0:2 * T + c1]
    def t4(self, c0=0, c1=T): return self.SC[:, 3 * T + c0:3 * T + c1]

    def mcol(self, base, j):
        return self.MISC[:, base + j:base + j + 1]



def _col8(v):
    v = np.asarray(v)
    return np.ascontiguousarray(v.reshape(-1, 128).T)


def _prep_host(inp):
    d = {}
    d["wrT"] = np.ascontiguousarray(np.asarray(inp["L0_attn_wr_w"]).transpose(0, 2, 1))
    d["wiT"] = np.ascontiguousarray(np.asarray(inp["L0_attn_wi_w"]).transpose(0, 2, 1))
    d["woT"] = np.ascontiguousarray(np.asarray(inp["L0_attn_wo_w"]).transpose(0, 2, 1))
    cw = np.asarray(inp["L0_attn_conv_w"])
    d["cwc"] = np.ascontiguousarray(
        np.stack([_col8(cw[l].T.reshape(-1)) for l in range(NL)]))
    for k_src, k_dst in [("L0_attn_conv_b", "cb"), ("L0_attn_wr_b", "wrb"),
                         ("L0_attn_wi_b", "wib"), ("L0_attn_log_a", "la"),
                         ("L0_attn_norm_w", "nw")]:
        v = np.asarray(inp[k_src])
        d[k_dst] = np.ascontiguousarray(np.stack([_col8(v[l]) for l in range(NL)]))
    d["now"] = _col8(inp["L0_attn_norm_out_w"])
    d["Wg"] = np.ascontiguousarray(np.asarray(inp["L0_ffn_Wg"]))
    d["W1_0"] = np.ascontiguousarray(np.asarray(inp["L0_ffn_W1"]))
    d["W2_0"] = np.ascontiguousarray(np.asarray(inp["L0_ffn_W2"]))
    d["Wq"] = np.ascontiguousarray(np.asarray(inp["L1_attn_Wq"]))
    d["Wk"] = np.ascontiguousarray(np.asarray(inp["L1_attn_Wk"]))
    d["Wv"] = np.ascontiguousarray(np.asarray(inp["L1_attn_Wv"]))
    d["gamma"] = _col8(inp["L1_attn_gamma"])
    d["beta"] = _col8(inp["L1_attn_beta"])
    rt = np.asarray(inp["L1_attn_rel_table"])
    r = np.arange(T)
    rel_index = np.clip(r[None, :] - r[:, None] + (T - 1), 0, 2 * T - 2)
    rel_index = np.minimum(rel_index, rt.shape[0] - 1)
    bias0 = rt[rel_index]
    biasg = bias0.reshape(-1).reshape(NH, T, T)
    d["biasT"] = np.ascontiguousarray(biasg.transpose(0, 2, 1))
    d["rw"] = np.ascontiguousarray(np.asarray(inp["L1_ffn_router_w"]))
    d["rb"] = np.ascontiguousarray(np.asarray(inp["L1_ffn_router_b"]))
    d["W1_1"] = np.ascontiguousarray(np.asarray(inp["L1_ffn_w1"]))
    b1 = np.asarray(inp["L1_ffn_b1"])
    d["b1c"] = np.ascontiguousarray(np.stack([_col8(b1[e]) for e in range(NE1)]))
    d["W2_1"] = np.ascontiguousarray(np.asarray(inp["L1_ffn_w2"]))
    d["b2"] = np.ascontiguousarray(np.asarray(inp["L1_ffn_b2"]))
    d["ident"] = np.eye(128, dtype=np.float32)
    d["onecol"] = np.ones((128, 1), np.float32)
    d["onerow"] = np.ones((1, 128), np.float32)
    i8 = np.arange(64) % NE0
    d["G64"] = (i8[:, None] == i8[None, :]).astype(np.float32)
    oh8 = np.zeros((NE0, NE0 * 128), np.float32)
    for e in range(NE0):
        oh8[e, e * 128:(e + 1) * 128] = 1.0
    d["OH8"] = oh8
    oh16 = np.zeros((NE1, NE1 * 128), np.float32)
    for e in range(NE1):
        oh16[e, e * 128:(e + 1) * 128] = 1.0
    d["OH16"] = oh16
    return d



def _emit_norm_fm(c, buf, out_buf, eps, subtract_mean=True, scale_col_of=None,
                  gb=False, src_bf16=False):
    eps_col = c.MISC[:, {1e-6: 121, 1e-5: 122, 1e-3: 123}[eps]:
                        {1e-6: 121, 1e-5: 122, 1e-3: 123}[eps] + 1]
    """out = (buf - mu)/sqrt(var+eps) over the 1024-feature (partition) dim.
    scale_col_of(j): per-feature multiplier fused into the apply (rms nw).
    gb: apply gamma/beta columns from MISC. Processes per 512-token chunk.
    Row scratch: SC[0:1, 4096+q*512] for q in (sx, sq, mu, var).
    """
    nc, blk = c.nc, c.block
    onec = c.OB[:, 0:1] if src_bf16 else c.MR[:, 0:1]
    oner = c.MR[0:1, 1:129]
    inv_n = 1.0 / D
    PS_X, PS_Q = c.PS[6], c.PS[7]
    R_SX = c.SC[0:1, 4096:4608]
    R_MU = c.SC[0:1, 5120:5632]
    R_VR = c.SC[0:1, 5632:6144]
    R_RSTD = c.MR[0:1, 256:768]
    R_ROWB = c.MR[0:1, 768:1280]
    SQV = c.SC.bitcast(BF16)
    for tc in range(NTC):
        @blk.scalar
        def _(eng, tc=tc):
            c.wait(eng, "V", "T")
            for j in range(NDT):
                ins = eng.activation(
                    SQV[:, j * TC:(j + 1) * TC],
                    buf[:, c.dcols(j, tc * TC, (tc + 1) * TC)], AF.Square)
            c.inc(ins, "S")

        @blk.tensor
        def _(eng, tc=tc):
            c.wait(eng, "S", "V")
            if subtract_mean:
                for j in range(NDT):
                    eng.matmul(PS_X[0:1, 0:TC], onec,
                               buf[:, c.dcols(j, tc * TC, (tc + 1) * TC)],
                               start=(j == 0), stop=(j == NDT - 1))
            for j in range(NDT):
                ins = eng.matmul(PS_Q[0:1, 0:TC], c.OB[:, 0:1],
                                 SQV[:, j * TC:(j + 1) * TC],
                                 start=(j == 0), stop=(j == NDT - 1))
            c.inc(ins, "T")

        @blk.vector
        def _(eng, tc=tc):
            c.wait(eng, "T", "S")
            if subtract_mean:
                eng.tensor_scalar(R_MU, PS_X[0:1, 0:TC], inv_n, None, ALU.mult)
                eng.tensor_scalar(R_VR, PS_Q[0:1, 0:TC], inv_n, None, ALU.mult)
                eng.tensor_tensor(R_SX, R_MU, R_MU, ALU.mult)
                ins = eng.tensor_tensor(R_VR, R_VR, R_SX, ALU.subtract)
            else:
                ins = eng.tensor_scalar(R_VR, PS_Q[0:1, 0:TC], inv_n, None, ALU.mult)
            c.inc(ins, "V")

        @blk.scalar
        def _(eng, tc=tc):
            c.wait(eng, "V")
            eng.activation(R_VR, R_VR, AF.Ln, bias=eps_col[0:1, 0:1])
            ins = eng.activation(R_RSTD, R_VR, AF.Exp, scale=-0.5)
            c.inc(ins, "S")

        if subtract_mean:
            @blk.vector
            def _(eng, tc=tc):
                c.wait(eng, "S")
                eng.tensor_tensor(R_MU, R_MU, R_RSTD, ALU.mult)
                ins = eng.tensor_scalar(R_ROWB, R_MU, -1.0, None, ALU.mult)
                c.inc(ins, "V")

        @blk.tensor
        def _(eng, tc=tc):
            c.wait(eng, "V", "S")
            ins = eng.matmul(PS_X[:, 0:TC], oner, R_RSTD,
                             start=True, stop=True)
            if subtract_mean:
                ins = eng.matmul(PS_Q[:, 0:TC], oner, R_ROWB,
                                 start=True, stop=True)
            c.inc(ins, "T")

        @blk.scalar
        def _(eng, tc=tc):
            c.wait(eng, "T", "V")
            ins = eng.copy(c.SC[:, 3584:4096], PS_X[:, 0:TC])
            c.inc(ins, "S")

        @blk.vector
        def _(eng, tc=tc):
            c.wait(eng, "T", "S")
            rbc = c.SC[:, 3584:4096]
            tmp = c.SC[:, 3072:3584]
            for j in range(NDT):
                src = buf[:, c.dcols(j, tc * TC, (tc + 1) * TC)]
                dst = out_buf[:, c.dcols(j, tc * TC, (tc + 1) * TC)]
                if scale_col_of is not None:
                    ins = eng.scalar_tensor_tensor(dst, src, scale_col_of(j), rbc,
                                                   ALU.mult, ALU.mult)
                elif subtract_mean:
                    eng.tensor_tensor(tmp, src, rbc, ALU.mult)
                    if gb:
                        eng.tensor_tensor(tmp, tmp, PS_Q[:, 0:TC], ALU.add)
                        ins = eng.tensor_scalar(dst, tmp, c.mcol(M_GAM, j),
                                                c.mcol(M_BET, j), ALU.mult, ALU.add)
                    else:
                        ins = eng.tensor_tensor(dst, tmp, PS_Q[:, 0:TC], ALU.add)
                else:
                    ins = eng.tensor_tensor(dst, src, rbc, ALU.mult)
            c.inc(ins, "V")



def _emit_l0_attn(c, dram):
    nc, blk = c.nc, c.block
    xc_buf = [c.BA, c.BB, c.BA]
    h_buf = [c.BB, c.BA, c.BB]
    DGB = c.WB[:, 7168:7680]

    def load_cols(dst_c0, dst_c1, src):
        @blk.sync
        def _(eng):
            c.wait(eng, "V", "S", "T")
            ins = eng.dma_start(out=c.MISC[:, dst_c0:dst_c1], in_=src)
            c.inc(ins, "Y", 16)

    for l in range(NL):
        XC = xc_buf[l]
        H = h_buf[l]
        if l == 0:
            xin = h_buf[0]
            for tc in range(NTC):
                @blk.vector
                def _(eng, tc=tc):
                    c.wait(eng, "G", "S", "T")
                    for j in range(NDT):
                        ins = eng.tensor_copy(
                            xin[:, c.dcols(j, tc * TC, (tc + 1) * TC)],
                            c.XR[:, c.dcols(j, tc * TC, (tc + 1) * TC)])
                    c.inc(ins, "V")
        else:
            xin = xc_buf[l - 1]
        load_cols(M_CW, M_CW + 32, dram["cwc"][l])
        load_cols(M_CB, M_CB + 8, dram["cb"][l])
        load_cols(M_WRB, M_WRB + 8, dram["wrb"][l])
        load_cols(M_WIB, M_WIB + 8, dram["wib"][l])
        load_cols(M_LA, M_LA + 8, dram["la"][l])
        load_cols(M_NW, M_NW + 8, dram["nw"][l])

        @blk.scalar
        def _(eng, l=l):
            c.wait(eng, "Y", "V")
            eng.activation(c.MISC[:, M_LSA:M_LSA + 8], c.MISC[:, M_LA:M_LA + 8],
                           AF.Sigmoid)
            ins = eng.activation(c.MISC[:, M_LSA:M_LSA + 8], c.MISC[:, M_LSA:M_LSA + 8],
                                 AF.Ln)
            c.inc(ins, "S")

        for j in range(NDT):
            @blk.vector
            def _(eng, l=l, j=j):
                c.wait(eng, "Y", "T")
                dg = DGB
                for k in range(KCONV):
                    ins = eng.tensor_scalar(dg[:, k * 128:(k + 1) * 128], c.IDT[:],
                                            c.mcol(M_CW, k * NDT + j), None, ALU.mult)
                c.inc(ins, "V")

            for tc in range(NTC):
                bank = c.PS[(j * NTC + tc) % 2]

                @blk.tensor
                def _(eng, l=l, j=j, tc=tc, bank=bank, xin=xin):
                    c.wait(eng, "V", "S", "G")
                    dg = DGB
                    for k in (3, 2, 1, 0):
                        off = KCONV - 1 - k
                        lo = tc * TC - off
                        if lo >= 0:
                            ins = eng.matmul(bank[:, 0:TC], dg[:, k * 128:(k + 1) * 128],
                                             xin[:, c.dcols(j, lo, lo + TC)],
                                             start=(k == 3), stop=(k == 0))
                        else:
                            ins = eng.matmul(bank[:, off:TC], dg[:, k * 128:(k + 1) * 128],
                                             xin[:, c.dcols(j, 0, TC - off)],
                                             start=(k == 3), stop=(k == 0))
                    c.inc(ins, "T")

                @blk.scalar
                def _(eng, j=j, tc=tc, bank=bank, XC=XC):
                    c.wait(eng, "T")
                    ins = eng.activation(XC[:, c.dcols(j, tc * TC, (tc + 1) * TC)],
                                         bank[:, 0:TC], AF.Identity,
                                         bias=c.mcol(M_CB, j))
                    c.inc(ins, "S")

        wtok = {}

        def load_wblock(kind, l_, jj, slot):
            src = dram[kind][l_][:, jj * 128:(jj + 1) * 128]
            src = src.rearrange("(kt p) o -> p kt o", p=128)

            @blk.gpsimd
            def _(eng):
                c.wait(eng, "T")
                ins = eng.dma_start(out=c.WB[:, slot * 1024:(slot + 1) * 1024], in_=src)
                wtok[(kind, jj)] = c.inc(ins, "G", 16)

        load_wblock("wrT", l, 0, 0)
        load_wblock("wiT", l, 0, 2)

        for j in range(NDT):
            if j + 1 < NDT:
                load_wblock("wrT", l, j + 1, (j + 1) % 2)
                load_wblock("wiT", l, j + 1, 2 + (j + 1) % 2)
            for kind, slot_base, bank_base, bias_c, dst in (
                    ("wrT", 0, 2, M_WRB, c.t1), ("wiT", 2, 4, M_WIB, c.t4)):
                slot = slot_base + j % 2
                for tc in range(NTC):
                    bank = c.PS[bank_base + tc % 2]

                    @blk.tensor
                    def _(eng, j=j, tc=tc, bank=bank, slot=slot, kind=kind, XC=XC):
                        c.wait_at(eng, "G", wtok[(kind, j)])
                        c.wait(eng, "S", "V")
                        for kt in range(NDT):
                            ins = eng.matmul(
                                bank[:, 0:TC],
                                c.WB[:, slot * 1024 + kt * 128:slot * 1024 + (kt + 1) * 128],
                                XC[:, c.dcols(kt, tc * TC, (tc + 1) * TC)],
                                start=(kt == 0), stop=(kt == NDT - 1))
                        c.inc(ins, "T")

                    @blk.scalar
                    def _(eng, j=j, tc=tc, bank=bank, bias_c=bias_c, dst=dst):
                        c.wait(eng, "T", "V")
                        ins = eng.activation(dst(tc * TC, (tc + 1) * TC), bank[:, 0:TC],
                                             AF.Sigmoid, bias=c.mcol(bias_c, j))
                        c.inc(ins, "S")

            @blk.vector
            def _(eng, j=j):
                c.wait(eng, "S")
                eng.tensor_scalar(c.t2(), c.t1(), c.mcol(M_LSA, j), 8.0,
                                  ALU.mult, ALU.mult)
                eng.tensor_tensor_scan(c.t1(), c.ONES[:], c.t2(), 0.0, ALU.mult, ALU.add)
                ins = eng.tensor_scalar(c.t1(), c.t1(), c.MISC[:, 120:121], None, ALU.max)
                c.inc(ins, "V")

            @blk.scalar
            def _(eng, j=j):
                c.wait(eng, "V")
                eng.activation(c.t3(), c.t1(), AF.Exp)
                eng.activation(c.t1(), c.t1(), AF.Exp, scale=-1.0)
                eng.activation(c.t2(), c.t2(), AF.Exp)
                ins = eng.activation(c.t2(), c.t2(), AF.Square)
                c.inc(ins, "S")

            @blk.vector
            def _(eng, j=j):
                c.wait(eng, "S")
                eng.tensor_scalar(c.t2(), c.t2(), -1.0, 1.0, ALU.mult, ALU.add)
                ins = eng.tensor_scalar(c.t2(), c.t2(), c.MISC[:, 121:122], None, ALU.max)
                c.inc(ins, "V")

            @blk.scalar
            def _(eng, j=j):
                c.wait(eng, "V")
                ins = eng.activation(c.t2(), c.t2(), AF.Sqrt)
                c.inc(ins, "S")

            @blk.vector
            def _(eng, j=j, XC=XC, H=H):
                c.wait(eng, "S")
                eng.tensor_tensor(c.t4(), c.t4(), XC[:, c.dcols(j)], ALU.mult)
                eng.tensor_tensor(c.t4(), c.t4(), c.t2(), ALU.mult)
                eng.tensor_tensor(c.t4(), c.t4(), c.t1(), ALU.mult)
                eng.tensor_tensor_scan(c.t2(), c.ONES[:], c.t4(), 0.0, ALU.mult, ALU.add)
                ins = eng.tensor_tensor(H[:, c.dcols(j)], c.t3(), c.t2(), ALU.mult)
                c.inc(ins, "V")

        load_wblock("woT", l, 0, 4)
        for j in range(NDT):
            if j + 1 < NDT:
                load_wblock("woT", l, j + 1, 4 + (j + 1) % 2)
            slot = 4 + j % 2
            for tc in range(NTC):
                bank = c.PS[tc % 2]

                @blk.tensor
                def _(eng, j=j, tc=tc, bank=bank, slot=slot, H=H):
                    c.wait_at(eng, "G", wtok[("woT", j)])
                    c.wait(eng, "S", "V")
                    for kt in range(NDT):
                        ins = eng.matmul(
                            bank[:, 0:TC],
                            c.WB[:, slot * 1024 + kt * 128:slot * 1024 + (kt + 1) * 128],
                            H[:, c.dcols(kt, tc * TC, (tc + 1) * TC)],
                            start=(kt == 0), stop=(kt == NDT - 1))
                    c.inc(ins, "T")

                @blk.scalar
                def _(eng, j=j, tc=tc, bank=bank, XC=XC):
                    c.wait(eng, "T")
                    ins = eng.copy(XC[:, c.dcols(j, tc * TC, (tc + 1) * TC)], bank[:, 0:TC])
                    c.inc(ins, "S")

        _emit_norm_fm(c, XC, XC, 1e-6, subtract_mean=False,
                      scale_col_of=lambda j: c.mcol(M_NW, j), src_bf16=True)

    load_cols(M_NOW, M_NOW + 8, dram["now"])
    yb = xc_buf[NL - 1]
    _emit_norm_fm(c, yb, yb, 1e-6, subtract_mean=False,
                  scale_col_of=lambda j: c.mcol(M_NOW, j), src_bf16=True)
    for tc in range(NTC):
        @blk.vector
        def _(eng, tc=tc):
            c.wait(eng, "V", "S", "G")
            for j in range(NDT):
                ins = eng.tensor_tensor(c.XR[:, c.dcols(j, tc * TC, (tc + 1) * TC)],
                                        c.XR[:, c.dcols(j, tc * TC, (tc + 1) * TC)],
                                        yb[:, c.dcols(j, tc * TC, (tc + 1) * TC)], ALU.add)
            c.inc(ins, "V")
    _emit_norm_fm(c, c.XR, c.XR, 1e-5, subtract_mean=True)


def _emit_l0_ffn(c, dram):
    nc, blk = c.nc, c.block
    BBF = c.BB.bitcast(F32)
    PR = lambda ch: c.SC[:, ch * 8:(ch + 1) * 8]
    PALL = c.SC[0:64, 128:2176]
    IND = c.SC[0:64, 2176:4224]
    TH = c.SC[0:64, 4224:4225]
    PLOC = c.SC[0:8, 5120:7168]

    @blk.gpsimd
    def _(eng):
        c.wait(eng, "T", "V", "S")
        ins = eng.dma_start(out=c.MR[:, 1472:1536],
                            in_=dram["Wg"].rearrange("(kt p) e -> p kt e", p=128))
        c.inc(ins, "G", 16)

        ins = eng.dma_start(out=c.BB[0:8, 2560:3584], in_=dram["OH8"])
        c.inc(ins, "G", 16)

    for ch in range(16):
        @blk.tensor
        def _(eng, ch=ch):
            c.wait(eng, "G", "V", "S")
            for kt in range(NDT):
                ins = eng.matmul(c.PS[2][:, 0:8],
                                 c.XR[:, c.dcols(kt, ch * 128, (ch + 1) * 128)],
                                 c.MR[:, 1472 + kt * 8:1472 + (kt + 1) * 8],
                                 start=(kt == 0), stop=(kt == NDT - 1))
            c.inc(ins, "T")

        @blk.vector
        def _(eng, ch=ch):
            c.wait(eng, "T", "S")
            mx = c.SC[:, 4288:4289]
            c.pb(eng, eng.tensor_reduce(mx, c.PS[2][:, 0:8], AX.X, ALU.max))
            ins = eng.tensor_scalar(c.SC[:, 4272:4280], c.PS[2][:, 0:8], mx, None,
                                    ALU.subtract)
            c.inc(ins, "V")

        @blk.scalar
        def _(eng, ch=ch):
            c.wait(eng, "V")
            ins = eng.activation(c.SC[:, 4272:4280], c.SC[:, 4272:4280], AF.Exp)
            c.inc(ins, "S")

        @blk.vector
        def _(eng, ch=ch):
            c.wait(eng, "S")
            sm = c.SC[:, 4289:4290]
            c.pb(eng, eng.tensor_reduce(sm, c.SC[:, 4272:4280], AX.X, ALU.add))
            c.pb(eng, eng.reciprocal(sm, sm))
            ins = eng.tensor_scalar(PR(ch), c.SC[:, 4272:4280], sm, None, ALU.mult)
            c.inc(ins, "V")

        @blk.tensor
        def _(eng, ch=ch):
            c.wait(eng, "V")
            ins = eng.transpose(c.PS[3][0:8, 0:128], PR(ch), c.IDT[:])
            c.inc(ins, "T")

        @blk.vector
        def _(eng, ch=ch):
            c.wait(eng, "T")
            ins = eng.tensor_copy(PLOC[:, ch * 128:(ch + 1) * 128], c.PS[3][0:8, 0:128])
            c.inc(ins, "V")

    @blk.sync
    def _(eng):
        c.wait(eng, "V")
        ins = eng.dma_start(out=dram["cc_in"], in_=PLOC)
        c.inc(ins, "Y", 16)
        ins = eng.dma_start(out=c.SC[0:64, 4320:4384], in_=dram["G64"])
        c.inc(ins, "Y", 16)

    @blk.gpsimd
    def _(eng):
        c.wait(eng, "Y")
        ins = eng.collective_compute(
            "AllGather", ALU.bypass, replica_groups=[list(range(B))],
            ins=[dram["cc_in"].opt()], outs=[dram["cc_out"].opt()])
        c.inc(ins, "C")
        eng.wait_ge(c.sem["C"], c.cnt["C"])
        ins = eng.dma_start(out=PALL, in_=dram["cc_out"])
        c.inc(ins, "G", 16)

    @blk.vector
    def _(eng):
        c.wait(eng, "G")
        ins = eng.memset(TH, 0.5)
        c.inc(ins, "V")
    NIT = 26
    for it in range(NIT):
        step = 0.5 ** (it + 2)

        @blk.vector
        def _(eng, it=it):
            c.wait(eng, "T", "V")
            c.pb(eng, eng.tensor_scalar(IND, PALL, TH, None, ALU.is_gt))
            ins = eng.tensor_reduce(c.SC[0:64, 4384:4385], IND, AX.X, ALU.add)
            c.inc(ins, "V")

        @blk.tensor
        def _(eng, it=it):
            c.wait(eng, "V")
            if it == 0:
                c.wait(eng, "Y")
            ins = eng.matmul(c.PS[2][0:64, 0:1], c.SC[0:64, 4320:4384],
                             c.SC[0:64, 4384:4385], start=True, stop=True)
            c.inc(ins, "T")

        @blk.vector
        def _(eng, it=it, step=step):
            c.wait(eng, "T")
            d = c.SC[0:64, 4226:4227]
            c.pb(eng, eng.tensor_scalar(d, c.PS[2][0:64, 0:1], 2048.0, None, ALU.is_gt))
            c.pb(eng, eng.tensor_scalar(d, d, 2.0, -1.0, ALU.mult, ALU.add))
            ins = eng.scalar_tensor_tensor(TH, d, float(step), TH, ALU.mult, ALU.add)
            c.inc(ins, "V")

    @blk.vector
    def _(eng):
        ins = eng.tensor_scalar(TH, TH, 4.0 * (0.5 ** (NIT + 1)), None, ALU.add)
        c.inc(ins, "V")

    @blk.tensor
    def _(eng):
        c.wait(eng, "V")
        ins = eng.transpose(c.PS[2][0:1, 0:8], TH[0:8, 0:1], c.IDT[0:8, 0:8])
        c.inc(ins, "T")

    @blk.vector
    def _(eng):
        c.wait(eng, "T")
        ins = eng.tensor_copy(c.SC[0:1, 4400:4408], c.PS[2][0:1, 0:8])
        c.inc(ins, "V")

    @blk.tensor
    def _(eng):
        c.wait(eng, "V")
        ins = eng.matmul(c.PS[2][:, 0:8], c.ONES[0:1, 0:128], c.SC[0:1, 4400:4408],
                         start=True, stop=True)
        c.inc(ins, "T")

    @blk.scalar
    def _(eng):
        c.wait(eng, "T")
        ins = eng.copy(c.SC[:, 4240:4248], c.PS[2][:, 0:8])
        c.inc(ins, "S")

    for ch in range(16):
        @blk.vector
        def _(eng, ch=ch):
            c.wait(eng, "S", "T")
            thb = c.SC[:, 4240:4248]
            sel = c.SC[:, 4256:4264]
            tB = c.SC[:, 4272:4280]
            nselc = c.SC[:, 4288:4289]
            asg = c.SC[:, 4289:4290]
            mxs = c.SC[:, 4290:4291]
            mxa = c.SC[:, 4291:4292]
            wv = c.SC[:, 4292:4293]
            inda = c.SC[:, 4304:4312]
            indb = c.SC[:, 4312:4320]
            c.pb(eng, eng.tensor_tensor(sel, PR(ch), thb, ALU.is_gt))
            c.pb(eng, eng.tensor_reduce(nselc, sel, AX.X, ALU.add))
            c.pb(eng, eng.tensor_scalar(asg, nselc, 0.5, None, ALU.is_gt))
            c.pb(eng, eng.tensor_scalar(tB, PR(ch), 10000.0, None, ALU.add))
            c.pb(eng, eng.tensor_tensor(tB, tB, sel, ALU.mult))
            c.pb(eng, eng.tensor_scalar(tB, tB, 10000.0, None, ALU.subtract))
            c.pb(eng, eng.tensor_reduce(mxs, tB, AX.X, ALU.max))
            c.pb(eng, eng.tensor_reduce(mxa, PR(ch), AX.X, ALU.max))
            c.pb(eng, eng.tensor_tensor(wv, mxs, mxa, ALU.subtract))
            c.pb(eng, eng.scalar_tensor_tensor(wv, wv, asg, mxa, ALU.mult, ALU.add))
            c.pb(eng, eng.tensor_scalar(inda, tB, mxs, None, ALU.is_equal))
            c.pb(eng, eng.tensor_scalar(indb, PR(ch), mxa, None, ALU.is_equal))
            c.pb(eng, eng.tensor_tensor(inda, inda, indb, ALU.subtract))
            c.pb(eng, eng.scalar_tensor_tensor(inda, inda, asg, indb, ALU.mult, ALU.add))
            ins = eng.tensor_scalar(inda, inda, wv, None, ALU.mult)
            c.inc(ins, "V")

        @blk.tensor
        def _(eng, ch=ch):
            c.wait(eng, "V")
            ins = eng.transpose(c.PS[3][0:8, 0:128], c.SC[:, 4304:4312], c.IDT[:])
            c.inc(ins, "T")

        @blk.vector
        def _(eng, ch=ch):
            c.wait(eng, "T")
            ins = eng.tensor_copy(c.BB[0:8, ch * 128:(ch + 1) * 128],
                                  c.PS[3][0:8, 0:128])
            c.inc(ins, "V")

    if _DBG["on"]:
        @blk.sync
        def _(eng):
            c.wait(eng, "V", "S", "T")
            eng.dma_start(out=dram["dbg_pr"], in_=c.SC[:, 0:128]).then_inc(c.sem["O"], 16)
            eng.dma_start(out=dram["dbg_th"], in_=TH).then_inc(c.sem["O"], 16)
            c.cnt["O"] += 32

        @blk.gpsimd
        def _(eng):
            c.wait(eng, "V")
            ins = eng.dma_start(out=dram["dbg_gate"], in_=c.BB[0:8, 0:2048])
            c.inc(ins, "G", 16)

    wtok = {}

    def load_w1(e, fc, slot):
        src = dram["W1_0"][e][:, fc * 128:(fc + 1) * 128]
        src = src.rearrange("(kt p) f -> p kt f", p=128)

        @blk.gpsimd
        def _(eng):
            c.wait(eng, "T")
            ins = eng.dma_start(out=c.WR[:, slot * 1024:(slot + 1) * 1024], in_=src)
            wtok[("w1", e, fc)] = c.inc(ins, "G", 16)

    def load_w2(e, oc, slot):
        src = dram["W2_0"][e][:, oc * 128:(oc + 1) * 128]
        src = src.rearrange("(fc p) o -> p fc o", p=128)

        @blk.gpsimd
        def _(eng):
            c.wait(eng, "T")
            ins = eng.dma_start(out=c.WB[:, slot * 2048:slot * 2048 + 2048], in_=src)
            wtok[("w2", e, oc)] = c.inc(ins, "G", 16)

    for half in range(2):
        t0 = half * 1024
        load_w1(0, 0, 0)
        for e in range(NE0):
            for fc in range(16):
                nxt = (e, fc + 1) if fc + 1 < 16 else (e + 1, 0)
                if nxt[0] < NE0:
                    load_w1(nxt[0], nxt[1], (fc + 1) % 2)
                slot = fc % 2
                for tq in range(2):
                    bank = c.PS[tq]

                    @blk.tensor
                    def _(eng, e=e, fc=fc, tq=tq, bank=bank, slot=slot, t0=t0):
                        c.wait_at(eng, "G", wtok[("w1", e, fc)])
                        c.wait(eng, "S", "V")
                        for kt in range(NDT):
                            ins = eng.matmul(
                                bank[:, 0:TC],
                                c.WR[:, slot * 1024 + kt * 128:slot * 1024 + (kt + 1) * 128],
                                c.XR[:, c.dcols(kt, t0 + tq * TC, t0 + (tq + 1) * TC)],
                                start=(kt == 0), stop=(kt == NDT - 1))
                        c.inc(ins, "T")

                    @blk.scalar
                    def _(eng, fc=fc, tq=tq, bank=bank):
                        c.wait(eng, "T")
                        ins = eng.activation(c.BA[:, fc * 1024 + tq * TC:fc * 1024 + (tq + 1) * TC],
                                             bank[:, 0:TC], AF.Gelu)
                        c.inc(ins, "S")

            load_w2(e, 0, 0)
            for oc in range(NDT):
                nxt = (e, oc + 1) if oc + 1 < NDT else (e + 1, 0)
                if nxt[0] < NE0:
                    load_w2(nxt[0], nxt[1], (oc + 1) % 2)
                slot = oc % 2
                for tq in range(2):
                    bank = c.PS[2 + tq]

                    @blk.tensor
                    def _(eng, e=e, oc=oc, tq=tq, bank=bank, slot=slot):
                        c.wait_at(eng, "G", wtok[("w2", e, oc)])
                        c.wait(eng, "S", "V")
                        for fc in range(16):
                            ins = eng.matmul(
                                bank[:, 0:TC],
                                c.WB[:, slot * 2048 + fc * 128:slot * 2048 + (fc + 1) * 128],
                                c.BA[:, fc * 1024 + tq * TC:fc * 1024 + (tq + 1) * TC],
                                start=(fc == 0), stop=(fc == 15))
                        c.inc(ins, "T")
                        ins = eng.matmul(c.PS[4][:, 0:TC],
                                         c.BB[0:8, 2560 + e * 128:2560 + (e + 1) * 128],
                                         c.BB[0:8, half * 1024 + tq * TC:half * 1024 + (tq + 1) * TC],
                                         start=True, stop=True)
                        c.inc(ins, "T")

                    @blk.scalar
                    def _(eng, bank=bank):
                        c.wait(eng, "T")
                        ins = eng.copy(c.BB[:, 2048:2560], c.PS[4][:, 0:TC])
                        c.inc(ins, "S")

                    @blk.vector
                    def _(eng, e=e, oc=oc, tq=tq, bank=bank):
                        c.wait(eng, "T", "S")
                        acc = c.SC[:, oc * 1024 + tq * TC:oc * 1024 + (tq + 1) * TC]
                        gb = c.BB[:, 2048:2560]
                        if e == 0:
                            ins = eng.tensor_tensor(acc, bank[:, 0:TC], gb, ALU.mult)
                        else:
                            tmp = c.BB[:, 3584:4096]
                            eng.tensor_tensor(tmp, bank[:, 0:TC], gb, ALU.mult)
                            ins = eng.tensor_tensor(acc, acc, tmp, ALU.add)
                        c.inc(ins, "V")

        if _DBG["on"] and half == 1:
            @blk.sync
            def _(eng, half=half):
                c.wait(eng, "V", "S")
                ins = eng.dma_start(out=dram["dbg_acc"], in_=c.SC[:, 0:8192])
                c.inc(ins, "O", 16)

        for j in range(NDT):
            @blk.vector
            def _(eng, j=j, t0=t0):
                c.wait(eng, "V", "S")
                ins = eng.tensor_tensor(
                    c.XR[:, c.dcols(j, t0, t0 + 1024)],
                    c.XR[:, c.dcols(j, t0, t0 + 1024)],
                    c.SC[:, j * 1024:(j + 1) * 1024], ALU.add)
                c.inc(ins, "V")

    _emit_norm_fm(c, c.XR, c.XR, 1e-5, subtract_mean=True)


def _emit_l1_attn(c, dram):
    nc, blk = c.nc, c.block
    QP = lambda c0, c1: c.BB[:, c0:c1]
    VW0 = 4096
    EXS = 6176
    BTR = 8224
    @blk.sync
    def _(eng):
        c.wait(eng, "V", "S", "T")
        ins = eng.dma_start(out=c.MISC[:, M_GAM:M_GAM + 8], in_=dram["gamma"])
        c.inc(ins, "Y", 16)
        ins = eng.dma_start(out=c.MISC[:, M_BET:M_BET + 8], in_=dram["beta"])
        c.inc(ins, "Y", 16)

    wtok = {}

    def load_wpair(kind, gp, slot):
        src = dram[kind][:, gp * 128:(gp + 1) * 128]
        src = src.rearrange("(kt p) o -> p kt o", p=128)

        @blk.gpsimd
        def _(eng):
            c.wait(eng, "T")
            ins = eng.dma_start(out=c.WR[:, slot * 1024:(slot + 1) * 1024], in_=src)
            wtok[(kind, gp)] = c.inc(ins, "G", 16)

    btok = {}

    def load_bias(g, kc, qc, slot):
        src = dram["biasT"][g][kc * 128:(kc + 1) * 128, qc * TC:(qc + 1) * TC]

        @blk.gpsimd
        def _(eng, slot=slot):
            c.wait_at(eng, "T", btok.get(("used", slot), 0))
            ins = eng.dma_start(out=c.BB[:, BTR + slot * TC:BTR + (slot + 1) * TC],
                                in_=src)
            btok[(g, kc, qc)] = c.inc(ins, "G", 16)

    for gp in range(NH // 2):
        for idx, kind in enumerate(("Wq", "Wk", "Wv")):
            load_wpair(kind, gp, idx)
        for idx, kind in enumerate(("Wq", "Wk")):
            for tc in range(NTC):
                bank = c.PS[tc % 2]

                @blk.tensor
                def _(eng, kind=kind, idx=idx, tc=tc, bank=bank, gp=gp):
                    c.wait_at(eng, "G", wtok[(kind, gp)])
                    c.wait(eng, "S", "V")
                    for kt in range(NDT):
                        ins = eng.matmul(
                            bank[:, 0:TC],
                            c.WR[:, idx * 1024 + kt * 128:idx * 1024 + (kt + 1) * 128],
                            c.XR[:, c.dcols(kt, tc * TC, (tc + 1) * TC)],
                            start=(kt == 0), stop=(kt == NDT - 1))
                    c.inc(ins, "T")

                @blk.scalar
                def _(eng, idx=idx, tc=tc, bank=bank):
                    c.wait(eng, "T")
                    ins = eng.copy(c.BB[:, idx * 2048 + tc * TC:idx * 2048 + (tc + 1) * TC],
                                   bank[:, 0:TC])
                    c.inc(ins, "S")

        @blk.vector
        def _(eng, gp=gp):
            c.wait(eng, "S")
            for tt in range(16):
                for h in range(2):
                    ins = eng.memset(c.BB[:, VW0 + (tt * 2 + h) * 65 + 64:VW0 + (tt * 2 + h) * 65 + 65], 1.0)
            c.inc(ins, "V")

        for tt in range(16):
            bank = c.PS[2 + tt % 2]

            @blk.tensor
            def _(eng, tt=tt, bank=bank, gp=gp):
                c.wait_at(eng, "G", wtok[("Wv", gp)])
                c.wait(eng, "V", "S")
                for kt in range(NDT):
                    ins = eng.matmul(
                        bank[:, 0:128],
                        c.XR[:, c.dcols(kt, tt * 128, (tt + 1) * 128)],
                        c.WR[:, 2 * 1024 + kt * 128:2 * 1024 + (kt + 1) * 128],
                        start=(kt == 0), stop=(kt == NDT - 1))
                c.inc(ins, "T")

            @blk.vector
            def _(eng, tt=tt, bank=bank):
                c.wait(eng, "T")
                eng.tensor_copy(c.BB[:, VW0 + (tt * 2) * 65:VW0 + (tt * 2) * 65 + 64],
                                bank[:, 0:64])
                ins = eng.tensor_copy(c.BB[:, VW0 + (tt * 2 + 1) * 65:VW0 + (tt * 2 + 1) * 65 + 64],
                                      bank[:, 64:128])
                c.inc(ins, "V")

        for hh in range(2):
            g = gp * 2 + hh
            rb = hh * 64

            @blk.tensor
            def _(eng):
                eng.drain()

            for qc in range(NTC):
                load_bias(g, 0, qc, 0)
                load_bias(g, 1, qc, 1)
                PV = c.PS[4]
                BV = c.PS[5]
                for kc in range(16):
                    if kc + 2 < 16:
                        load_bias(g, kc + 2, qc, (kc + 2) % 4)
                    sbank = c.PS[2 + kc % 2]

                    @blk.tensor
                    def _(eng, g=g, rb=rb, qc=qc, kc=kc, sbank=sbank, PV=PV, BV=BV, hh=hh):
                        c.wait(eng, "S", "V")
                        ins = eng.matmul(
                            sbank[:, 0:TC],
                            c.BB[rb:rb + 64, 2048 + kc * 128:2048 + (kc + 1) * 128],
                            c.BB[rb:rb + 64, qc * TC:(qc + 1) * TC],
                            start=True, stop=True)
                        c.inc(ins, "T")

                    @blk.scalar
                    def _(eng, kc=kc, sbank=sbank):
                        c.wait(eng, "T")
                        ins = eng.activation(
                            c.BB[:, EXS + (kc % 4) * TC:EXS + (kc % 4 + 1) * TC],
                            sbank[:, 0:TC], AF.Exp, scale=float(D) ** -0.5)
                        c.inc(ins, "S")

                    @blk.tensor
                    def _(eng, g=g, qc=qc, kc=kc, PV=PV, BV=BV):
                        c.wait(eng, "S")
                        c.wait_at(eng, "G", btok[(g, kc, qc)])
                        vw = c.BB[:, VW0 + (kc * 2 + g % 2) * 65:VW0 + (kc * 2 + g % 2) * 65 + 65]
                        ins = eng.matmul(PV[0:65, 0:TC], vw,
                                         c.BB[:, EXS + (kc % 4) * TC:EXS + (kc % 4 + 1) * TC],
                                         start=(kc == 0), stop=(kc == 15))
                        ins = eng.matmul(BV[0:65, 0:TC], vw,
                                         c.BB[:, BTR + (kc % 4) * TC:BTR + (kc % 4 + 1) * TC],
                                         start=(kc == 0), stop=(kc == 15))
                        btok[("used", kc % 4)] = c.inc(ins, "T")

                @blk.vector
                def _(eng, PV=PV):
                    c.wait(eng, "T")
                    with nc.allow_low_precision(reason="f32r recip row for bcast matmul"):
                        ins = eng.reciprocal(c.MR[0:1, 256:768][0:1, 0:TC], PV[64:65, 0:TC])
                    c.inc(ins, "V")

                @blk.tensor
                def _(eng):
                    c.wait(eng, "V")
                    ins = eng.matmul(c.PS[6][0:64, 0:TC], c.MR[0:1, 65:129],
                                     c.MR[0:1, 256:768][0:1, 0:TC], start=True, stop=True)
                    c.inc(ins, "T")

                @blk.scalar
                def _(eng):
                    c.wait(eng, "T")
                    ins = eng.copy(c.SC[0:64, 0:TC], c.PS[6][0:64, 0:TC])
                    c.inc(ins, "S")

                @blk.vector
                def _(eng, g=g, qc=qc, PV=PV, BV=BV):
                    c.wait(eng, "S")
                    tmp = c.SC[0:64, 512:1024]
                    eng.tensor_tensor(tmp, PV[0:64, 0:TC], c.SC[0:64, 0:TC], ALU.mult)
                    dst = c.BA[(g % 2) * 64:(g % 2) * 64 + 64,
                               c.dcols(g // 2, qc * TC, (qc + 1) * TC)]
                    ins = eng.tensor_tensor(dst, tmp, BV[0:64, 0:TC], ALU.add)
                    c.inc(ins, "V")

    if _DBG["on"]:
        @blk.gpsimd
        def _(eng):
            c.wait(eng, "V", "S")
            ins = eng.dma_start(out=dram["dbg_attn"], in_=c.BA[:])
            c.inc(ins, "G", 16)

    _emit_norm_fm(c, c.BA, c.BA, 1e-3, subtract_mean=True, gb=True, src_bf16=True)
    for tc in range(NTC):
        @blk.vector
        def _(eng, tc=tc):
            c.wait(eng, "V", "S")
            for j in range(NDT):
                ins = eng.tensor_tensor(c.XR[:, c.dcols(j, tc * TC, (tc + 1) * TC)],
                                        c.XR[:, c.dcols(j, tc * TC, (tc + 1) * TC)],
                                        c.BA[:, c.dcols(j, tc * TC, (tc + 1) * TC)], ALU.add)
            c.inc(ins, "V")
    _emit_norm_fm(c, c.XR, c.XR, 1e-5, subtract_mean=True)


def _emit_l1_ffn(c, dram):
    nc, blk = c.nc, c.block
    P0 = lambda ch: c.SC[:, ch * 16:(ch + 1) * 16]

    @blk.gpsimd
    def _(eng):
        c.wait(eng, "T", "V", "S")
        ins = eng.dma_start(out=c.MR[:, 1280:1408],
                            in_=dram["rw"].rearrange("(kt p) e -> p kt e", p=128))
        c.inc(ins, "G", 16)
        ins = eng.dma_start(out=c.MR[0:1, 1552:1568], in_=dram["rb"].unsqueeze(0))
        c.inc(ins, "G", 16)
        ins = eng.dma_start(out=c.BB[0:16, 2560:4608], in_=dram["OH16"])
        c.inc(ins, "G", 16)
        ins = eng.dma_start(out=c.WB[0:16, 4096:5120], in_=dram["b2"])
        c.inc(ins, "G", 16)

    @blk.tensor
    def _(eng):
        c.wait(eng, "G", "V", "S")
        ins = eng.matmul(c.PS[3][:, 0:16], c.MR[0:1, 1:129], c.MR[0:1, 1552:1568],
                         start=True, stop=True)
        c.inc(ins, "T")

    @blk.scalar
    def _(eng):
        c.wait(eng, "T")
        ins = eng.copy(c.SC[:, 4352:4368], c.PS[3][:, 0:16])
        c.inc(ins, "S")

    for ch in range(16):
        @blk.tensor
        def _(eng, ch=ch):
            c.wait(eng, "G", "V", "S")
            for kt in range(NDT):
                ins = eng.matmul(c.PS[2][:, 0:16],
                                 c.XR[:, c.dcols(kt, ch * 128, (ch + 1) * 128)],
                                 c.MR[:, 1280 + kt * 16:1280 + (kt + 1) * 16],
                                 start=(kt == 0), stop=(kt == NDT - 1))
            c.inc(ins, "T")

        @blk.vector
        def _(eng, ch=ch):
            c.wait(eng, "T", "S")
            pm = c.SC[:, 4368:4384]
            mx = c.SC[:, 4401:4402]
            eng.tensor_tensor(pm, c.PS[2][:, 0:16], c.SC[:, 4352:4368], ALU.add)
            c.pb(eng, eng.tensor_reduce(mx, pm, AX.X, ALU.max))
            ins = eng.tensor_scalar(pm, pm, mx, None, ALU.subtract)
            c.inc(ins, "V")

        @blk.scalar
        def _(eng, ch=ch):
            c.wait(eng, "V")
            ins = eng.activation(c.SC[:, 4368:4384], c.SC[:, 4368:4384], AF.Exp)
            c.inc(ins, "S")

        @blk.vector
        def _(eng, ch=ch):
            c.wait(eng, "S")
            pm = c.SC[:, 4368:4384]
            sm = c.SC[:, 4401:4402]
            c.pb(eng, eng.tensor_reduce(sm, pm, AX.X, ALU.add))
            c.pb(eng, eng.reciprocal(sm, sm))
            c.pb(eng, eng.tensor_scalar(P0(ch), pm, sm, None, ALU.mult))
            indsum = c.SC[:, 4384:4400]
            wsum = c.SC[:, 4400:4401]
            tmu = c.SC[:, 4402:4418]
            mm_ = c.SC[:, 4418:4419]
            eng.tensor_copy(pm, P0(ch))
            eng.memset(indsum, 0.0)
            c.pb(eng, eng.memset(wsum, 0.0))
            for k4 in range(4):
                c.pb(eng, eng.tensor_reduce(mm_, pm, AX.X, ALU.max))
                c.pb(eng, eng.tensor_scalar(tmu, pm, mm_, None, ALU.is_equal))
                c.pb(eng, eng.tensor_tensor(indsum, indsum, tmu, ALU.add))
                c.pb(eng, eng.tensor_tensor(wsum, wsum, mm_, ALU.add))
                c.pb(eng, eng.tensor_tensor(tmu, tmu, pm, ALU.mult))
                c.pb(eng, eng.tensor_tensor(pm, pm, tmu, ALU.subtract))
            c.pb(eng, eng.reciprocal(wsum, wsum))
            c.pb(eng, eng.tensor_tensor(tmu, P0(ch), indsum, ALU.mult))
            ins = eng.tensor_scalar(tmu, tmu, wsum, None, ALU.mult)
            c.inc(ins, "V")

        @blk.tensor
        def _(eng, ch=ch):
            c.wait(eng, "V")
            ins = eng.transpose(c.PS[3][0:16, 0:128], c.SC[:, 4402:4418], c.IDT[:])
            c.inc(ins, "T")

        @blk.vector
        def _(eng, ch=ch):
            c.wait(eng, "T")
            ins = eng.tensor_copy(c.BB[0:16, ch * 128:(ch + 1) * 128],
                                  c.PS[3][0:16, 0:128])
            c.inc(ins, "V")

    wtok = {}

    def load_w1(e, fc, slot):
        src = dram["W1_1"][e][:, fc * 128:(fc + 1) * 128]
        src = src.rearrange("(kt p) f -> p kt f", p=128)

        @blk.gpsimd
        def _(eng):
            c.wait(eng, "T")
            ins = eng.dma_start(out=c.WR[:, slot * 1024:(slot + 1) * 1024], in_=src)
            wtok[("w1", e, fc)] = c.inc(ins, "G", 16)

    def load_w2(e, oc, slot):
        src = dram["W2_1"][e][:, oc * 128:(oc + 1) * 128]
        src = src.rearrange("(fc p) o -> p fc o", p=128)

        @blk.gpsimd
        def _(eng):
            c.wait(eng, "T")
            ins = eng.dma_start(out=c.WB[:, slot * 2048:slot * 2048 + 2048], in_=src)
            wtok[("w2", e, oc)] = c.inc(ins, "G", 16)

    def load_b1(e):
        @blk.sync
        def _(eng):
            c.wait(eng, "S")
            ins = eng.dma_start(out=c.MISC[:, 104:120], in_=dram["b1c"][e])
            wtok[("b1", e)] = c.inc(ins, "Y", 16)

    for half in range(2):
        t0 = half * 1024
        load_w1(0, 0, 0)
        for e in range(NE1):
            load_b1(e)
            for fc in range(16):
                nxt = (e, fc + 1) if fc + 1 < 16 else (e + 1, 0)
                if nxt[0] < NE1:
                    load_w1(nxt[0], nxt[1], (fc + 1) % 2)
                slot = fc % 2
                for tq in range(2):
                    bank = c.PS[tq]

                    @blk.tensor
                    def _(eng, e=e, fc=fc, tq=tq, bank=bank, slot=slot, t0=t0):
                        c.wait_at(eng, "G", wtok[("w1", e, fc)])
                        c.wait(eng, "S", "V")
                        for kt in range(NDT):
                            ins = eng.matmul(
                                bank[:, 0:TC],
                                c.WR[:, slot * 1024 + kt * 128:slot * 1024 + (kt + 1) * 128],
                                c.XR[:, c.dcols(kt, t0 + tq * TC, t0 + (tq + 1) * TC)],
                                start=(kt == 0), stop=(kt == NDT - 1))
                        c.inc(ins, "T")

                    @blk.scalar
                    def _(eng, e=e, fc=fc, tq=tq, bank=bank):
                        c.wait(eng, "T")
                        c.wait_at(eng, "Y", wtok[("b1", e)])
                        ins = eng.activation(c.BA[:, fc * 1024 + tq * TC:fc * 1024 + (tq + 1) * TC],
                                             bank[:, 0:TC], AF.Gelu,
                                             bias=c.mcol(104, fc))
                        c.inc(ins, "S")

            load_w2(e, 0, 0)
            for oc in range(NDT):
                nxt = (e, oc + 1) if oc + 1 < NDT else (e + 1, 0)
                if nxt[0] < NE1:
                    load_w2(nxt[0], nxt[1], (oc + 1) % 2)
                slot = oc % 2
                for tq in range(2):
                    bank = c.PS[2 + tq]

                    @blk.tensor
                    def _(eng, e=e, oc=oc, tq=tq, bank=bank, slot=slot, half=half):
                        c.wait_at(eng, "G", wtok[("w2", e, oc)])
                        c.wait(eng, "S", "V")
                        for fc in range(16):
                            ins = eng.matmul(
                                bank[:, 0:TC],
                                c.WB[:, slot * 2048 + fc * 128:slot * 2048 + (fc + 1) * 128],
                                c.BA[:, fc * 1024 + tq * TC:fc * 1024 + (tq + 1) * TC],
                                start=(fc == 0), stop=(fc == 15))
                        c.inc(ins, "T")
                        ins = eng.matmul(c.PS[4][:, 0:TC],
                                         c.BB[0:16, 2560 + e * 128:2560 + (e + 1) * 128],
                                         c.BB[0:16, half * 1024 + tq * TC:half * 1024 + (tq + 1) * TC],
                                         start=True, stop=True)
                        c.inc(ins, "T")

                    @blk.scalar
                    def _(eng, bank=bank):
                        c.wait(eng, "T")
                        ins = eng.copy(c.BB[:, 2048:2560], c.PS[4][:, 0:TC])
                        c.inc(ins, "S")

                    @blk.vector
                    def _(eng, e=e, oc=oc, tq=tq, bank=bank):
                        c.wait(eng, "T", "S")
                        acc = c.SC[:, oc * 1024 + tq * TC:oc * 1024 + (tq + 1) * TC]
                        gb = c.BB[:, 2048:2560]
                        if e == 0:
                            ins = eng.tensor_tensor(acc, bank[:, 0:TC], gb, ALU.mult)
                        else:
                            tmp = c.BB[:, 4608:5120]
                            eng.tensor_tensor(tmp, bank[:, 0:TC], gb, ALU.mult)
                            ins = eng.tensor_tensor(acc, acc, tmp, ALU.add)
                        c.inc(ins, "V")

        for oc in range(NDT):
            for tq in range(2):
                @blk.tensor
                def _(eng, oc=oc, tq=tq, half=half):
                    c.wait(eng, "V", "S")
                    ins = eng.matmul(c.PS[5][:, 0:TC],
                                     c.WB[0:16, 4096 + oc * 128:4096 + (oc + 1) * 128],
                                     c.BB[0:16, half * 1024 + tq * TC:half * 1024 + (tq + 1) * TC],
                                     start=True, stop=True)
                    c.inc(ins, "T")

                @blk.vector
                def _(eng, oc=oc, tq=tq):
                    c.wait(eng, "T")
                    acc = c.SC[:, oc * 1024 + tq * TC:oc * 1024 + (tq + 1) * TC]
                    ins = eng.tensor_tensor(acc, acc, c.PS[5][:, 0:TC], ALU.add)
                    c.inc(ins, "V")

        for j in range(NDT):
            @blk.vector
            def _(eng, j=j, t0=t0):
                c.wait(eng, "V", "S")
                ins = eng.tensor_tensor(
                    c.XR[:, c.dcols(j, t0, t0 + 1024)],
                    c.XR[:, c.dcols(j, t0, t0 + 1024)],
                    c.SC[:, j * 1024:(j + 1) * 1024], ALU.add)
                c.inc(ins, "V")

    _emit_norm_fm(c, c.XR, c.XR, 1e-5, subtract_mean=True)






def _emit_prologue(c, dram):
    blk = c.block

    @blk.sync
    def _(eng):
        ins = eng.dma_start(out=c.IDT[:], in_=dram["ident"])
        c.inc(ins, "Y", 16)

    @blk.gpsimd
    def _(eng):
        ins = eng.dma_start(out=c.MR[:, 0:1], in_=dram["onecol"])
        c.inc(ins, "G", 16)
        ins = eng.dma_start(out=c.MR[0:1, 1:129], in_=dram["onerow"])
        c.inc(ins, "G", 16)

    @blk.vector
    def _(eng):
        eng.memset(c.ONES[:], 1.0)
        eng.memset(c.OB[:], 1.0)
        eng.memset(c.MISC[:, 120:121], -80.0)
        eng.memset(c.MISC[:, 121:122], 1e-6)
        eng.memset(c.MISC[:, 122:123], 1e-5)
        ins = eng.memset(c.MISC[:, 123:124], 1e-3)
        c.inc(ins, "V")

    if c.phases[0] == "l0a":
        for tt in range(T // 128):
            @blk.sync
            def _(eng, tt=tt):
                c.wait(eng, "T")
                ins = eng.dma_start(out=c.SC[:, 0:1024],
                                    in_=dram["x0"][tt * 128:(tt + 1) * 128, :])
                c.inc(ins, "Y", 16)

            vbase = c.cnt["V"]

            @blk.tensor
            def _(eng, tt=tt, vbase=vbase):
                c.wait(eng, "Y")
                for j in range(NDT):
                    if j == 0:
                        c.wait_at(eng, "V", vbase)
                    if j >= 2:
                        c.wait_at(eng, "V", vbase + j - 1)
                    ins = eng.transpose(c.PS[4 + j % 2][:, 0:128],
                                        c.SC[:, j * 128:(j + 1) * 128], c.IDT[:])
                    c.inc(ins, "T")

            @blk.vector
            def _(eng, tt=tt):
                tbase = c.cnt["T"] - NDT
                for j in range(NDT):
                    c.wait_at(eng, "T", tbase + j + 1)
                    ins = eng.tensor_copy(c.XR[:, c.dcols(j, tt * 128, tt * 128 + 128)],
                                          c.PS[4 + j % 2][:, 0:128])
                    c.inc(ins, "V")
    else:
        @blk.gpsimd
        def _(eng):
            ins = eng.dma_start(out=c.XR[:], in_=dram["xin"])
            c.inc(ins, "G", 16)


def _emit_epilogue(c, dram):
    nc, blk = c.nc, c.block
    if c.phases[-1] == "l1f":
        OTM = c.SC[:, 0:1024]
        for tt in range(T // 128):
            vbase = c.cnt["V"]
            otok = c.cnt["O"]

            @blk.tensor
            def _(eng, tt=tt, vbase=vbase):
                c.wait(eng, "G", "S")
                for j in range(NDT):
                    if j == 0:
                        c.wait_at(eng, "V", vbase)
                    if j >= 2:
                        c.wait_at(eng, "V", vbase + j - 1)
                    ins = eng.transpose(c.PS[4 + j % 2][:, 0:128],
                                        c.XR[:, c.dcols(j, tt * 128, tt * 128 + 128)].bitcast(F32),
                                        c.IDT[:])
                    c.inc(ins, "T")

            @blk.vector
            def _(eng, tt=tt, otok=otok):
                tbase = c.cnt["T"] - NDT
                c.wait_at(eng, "O", otok)
                for j in range(NDT):
                    c.wait_at(eng, "T", tbase + j + 1)
                    ins = eng.tensor_copy(OTM[:, j * 128:(j + 1) * 128],
                                          c.PS[4 + j % 2][:, 0:128])
                    c.inc(ins, "V")

            @blk.sync
            def _(eng, tt=tt):
                c.wait(eng, "V")
                ins = eng.dma_start(out=dram["y"][tt * 128:(tt + 1) * 128, :], in_=OTM)
                c.inc(ins, "O", 16)
    else:
        @blk.sync
        def _(eng):
            c.wait(eng, "V", "S", "T", "G")
            ins = eng.dma_start(out=dram["xout"], in_=c.XR[:].bitcast(F32))
            c.inc(ins, "O", 16)

    @blk.sync
    def _(eng):
        c.wait_at(eng, "O", c.cnt["O"])



def build_program(phases):
    nc = bass.Bass("TRN2", target_bir_lowering=False, debug=False)
    dram = {}

    def din(name, shape, dt=F32):
        dram[name] = nc.dram_tensor(name, list(shape), dt, kind="ExternalInput").ap()

    def dout(name, shape, dt=F32):
        dram[name] = nc.dram_tensor(name, list(shape), dt, kind="ExternalOutput").ap()

    if phases[0] == "l0a":
        din("x0", (T, D))
    else:
        din("xin", (128, NDT * T))
    if phases[-1] == "l1f":
        dout("y", (T, D))
    else:
        dout("xout", (128, NDT * T))
    if "l0a" in phases:
        din("wrT", (NL, D, D)); din("wiT", (NL, D, D)); din("woT", (NL, D, D))
        din("cwc", (NL, 128, 32)); din("cb", (NL, 128, 8)); din("wrb", (NL, 128, 8))
        din("wib", (NL, 128, 8)); din("la", (NL, 128, 8)); din("nw", (NL, 128, 8))
        din("now", (128, 8))
    if "l0f" in phases:
        din("Wg", (D, NE0)); din("W1_0", (NE0, D, F0)); din("W2_0", (NE0, F0, D))
        din("G64", (64, 64)); din("OH8", (NE0, NE0 * 128))
        if _DBG["on"]:
            dout("dbg_pr", (128, 128)); dout("dbg_th", (64, 1))
            dout("dbg_gate", (8, 2048)); dout("dbg_acc", (128, 8192))
    if "l0f" in phases:
        dram["cc_in"] = nc.dram_tensor("cc_in", [NE0, T], F32, kind="Internal").ap()
        dram["cc_out"] = nc.dram_tensor("cc_out", [NE0 * B, T], F32, kind="Internal").ap()
    if "l1a" in phases and _DBG["on"]:
        dout("dbg_attn", (128, NDT * T))
    if "l1a" in phases:
        din("Wq", (D, D)); din("Wk", (D, D)); din("Wv", (D, D))
        din("gamma", (128, 8)); din("beta", (128, 8)); din("biasT", (NH, T, T))
    if "l1f" in phases:
        din("rw", (D, NE1)); din("rb", (NE1,))
        din("W1_1", (NE1, D, F1)); din("b1c", (NE1, 128, 16))
        din("W2_1", (NE1, F1, D)); din("b2", (NE1, D))
        din("OH16", (NE1, NE1 * 128))
    din("ident", (128, 128)); din("onecol", (128, 1)); din("onerow", (1, 128))

    es = contextlib.ExitStack()
    with es:
        c = Ctx(nc, es, phases)
        _emit_prologue(c, dram)
        for ph in phases:
            {"l0a": _emit_l0_attn, "l0f": _emit_l0_ffn,
             "l1a": _emit_l1_attn, "l1f": _emit_l1_ffn}[ph](c, dram)
        _emit_epilogue(c, dram)
    return nc


_INPUT_KEYS = {
    "l0a": ["wrT", "wiT", "woT", "cwc", "cb", "wrb", "wib", "la", "nw", "now"],
    "l0f": ["Wg", "W1_0", "W2_0", "G64", "OH8"],
    "l1a": ["Wq", "Wk", "Wv", "gamma", "beta", "biasT"],
    "l1f": ["rw", "rb", "W1_1", "b1c", "W2_1", "b2", "OH16"],
}


def kernel(**inputs):
    host = _prep_host(inputs)
    x0 = np.ascontiguousarray(np.asarray(inputs["L0_attn_x"], dtype=np.float32))
    cur = [np.ascontiguousarray(x0[b]) for b in range(B)]
    for phases in PHASE_GROUPS:
        nc = build_program(phases)
        shared = {"ident": host["ident"], "onecol": host["onecol"],
                  "onerow": host["onerow"]}
        for ph in phases:
            for k in _INPUT_KEYS[ph]:
                shared[k] = host[k]
        in_maps = []
        for b in range(B):
            m = dict(shared)
            if phases[0] == "l0a":
                m["x0"] = cur[b]
            else:
                m["xin"] = cur[b]
            in_maps.append(m)
        res = run_bass_kernel_spmd(nc, in_maps, core_ids=list(range(B)))
        if phases[-1] == "l1f":
            return np.stack([res.results[b]["y"] for b in range(B)], axis=0)
        cur = [np.ascontiguousarray(res.results[b]["xout"]) for b in range(B)]
    raise RuntimeError("phase groups must end with l1f")


# version 07
# speedup vs baseline: 2.2686x; 2.2686x over previous
import contextlib
import numpy as np
import concourse.bass as bass
import concourse.mybir as mybir
from concourse.bass_utils import run_bass_kernel_spmd

F32 = mybir.dt.float32
F32R = mybir.dt.float32r
BF16 = mybir.dt.bfloat16
AF = mybir.ActivationFunctionType
ALU = mybir.AluOpType
AX = mybir.AxisListType

B, T, D = 8, 2048, 1024
NDT = D // 128
TC = 512
NTC = T // TC
NL = 3
KCONV = 4
NE0, F0 = 8, 2048
NE1, F1 = 16, 2048
NH, DH = 16, 64

PHASE_GROUPS = [["l0a", "l0f", "l1a", "l1f"]]
_DBG = {"on": False}

M_CW = 0
M_CB = 32
M_WRB = 40
M_WIB = 48
M_LSA = 56
M_LA = 64
M_NW = 72
M_NOW = 80
M_GAM = 88
M_BET = 96
M_B2 = 104


class Ctx:
    def __init__(self, nc, es, phases):
        self.nc = nc
        self.es = es
        self.phases = phases
        sb = lambda name, shape, dt: es.enter_context(nc.sbuf_tensor(name, shape, dt))
        self.XR = sb("XR", [128, NDT * T], F32R)
        self.BA = sb("BA", [128, NDT * T], BF16)
        self.BB = sb("BB", [128, NDT * T], BF16)
        self.SC = sb("SC", [128, 4 * T], F32)
        self.WB = sb("WB", [128, 8192], BF16)
        self.WR = sb("WR", [128, 3072], F32R)
        self.ONES = sb("ONES", [128, T], F32)
        self.MISC = sb("MISC", [128, 1024], F32)
        self.MR = sb("MR", [128, 1664], F32R)
        self.OB = sb("OB", [128, 2], BF16)
        self.IDT = sb("IDT", [128, 128], F32)
        self.PSA = es.enter_context(nc.psum_tensor("psA", [128, 2048], F32))
        self.PSB = es.enter_context(nc.psum_tensor("psB", [128, 2048], F32))
        self.PS = [self.PSA[:, i * 512:(i + 1) * 512] for i in range(4)] + \
                  [self.PSB[:, i * 512:(i + 1) * 512] for i in range(4)]
        self.sem = {}
        self.cnt = {}
        for name in ["T", "V", "S", "G", "Y", "O", "C"]:
            self.sem[name] = es.enter_context(nc.semaphore("sem" + name))
            self.cnt[name] = 0
        self.block = es.enter_context(nc.Block())

    def inc(self, ins, name, n=1):
        self.cnt[name] += n
        ins.then_inc(self.sem[name], n)
        return self.cnt[name]

    def wait(self, eng, *names):
        for n in names:
            if self.cnt[n] > 0:
                eng.wait_ge(self.sem[n], self.cnt[n])

    def wait_at(self, eng, name, val):
        if val > 0:
            eng.wait_ge(self.sem[name], val)

    def pb(self, eng, ins):
        self.inc(ins, "V")
        eng.wait_ge(self.sem["V"], self.cnt["V"])

    @staticmethod
    def dcols(j, c0=0, c1=T):
        return slice(j * T + c0, j * T + c1)

    def t1(self, c0=0, c1=T): return self.SC[:, 0 * T + c0:0 * T + c1]
    def t2(self, c0=0, c1=T): return self.SC[:, 1 * T + c0:1 * T + c1]
    def t3(self, c0=0, c1=T): return self.SC[:, 2 * T + c0:2 * T + c1]
    def t4(self, c0=0, c1=T): return self.SC[:, 3 * T + c0:3 * T + c1]

    def mcol(self, base, j):
        return self.MISC[:, base + j:base + j + 1]



def _col8(v):
    v = np.asarray(v)
    return np.ascontiguousarray(v.reshape(-1, 128).T)


def _prep_host(inp):
    d = {}
    d["wrT"] = np.ascontiguousarray(np.asarray(inp["L0_attn_wr_w"]).transpose(0, 2, 1))
    d["wiT"] = np.ascontiguousarray(np.asarray(inp["L0_attn_wi_w"]).transpose(0, 2, 1))
    d["woT"] = np.ascontiguousarray(np.asarray(inp["L0_attn_wo_w"]).transpose(0, 2, 1))
    cw = np.asarray(inp["L0_attn_conv_w"])
    d["cwc"] = np.ascontiguousarray(
        np.stack([_col8(cw[l].T.reshape(-1)) for l in range(NL)]))
    for k_src, k_dst in [("L0_attn_conv_b", "cb"), ("L0_attn_wr_b", "wrb"),
                         ("L0_attn_wi_b", "wib"), ("L0_attn_log_a", "la"),
                         ("L0_attn_norm_w", "nw")]:
        v = np.asarray(inp[k_src])
        d[k_dst] = np.ascontiguousarray(np.stack([_col8(v[l]) for l in range(NL)]))
    d["now"] = _col8(inp["L0_attn_norm_out_w"])
    d["Wg"] = np.ascontiguousarray(np.asarray(inp["L0_ffn_Wg"]))
    d["W1_0"] = np.ascontiguousarray(np.asarray(inp["L0_ffn_W1"]))
    d["W2_0"] = np.ascontiguousarray(np.asarray(inp["L0_ffn_W2"]))
    d["Wq"] = np.ascontiguousarray(np.asarray(inp["L1_attn_Wq"]))
    d["Wk"] = np.ascontiguousarray(np.asarray(inp["L1_attn_Wk"]))
    d["Wv"] = np.ascontiguousarray(np.asarray(inp["L1_attn_Wv"]))
    d["gamma"] = _col8(inp["L1_attn_gamma"])
    d["beta"] = _col8(inp["L1_attn_beta"])
    rt = np.asarray(inp["L1_attn_rel_table"])
    r = np.arange(T)
    rel_index = np.clip(r[None, :] - r[:, None] + (T - 1), 0, 2 * T - 2)
    rel_index = np.minimum(rel_index, rt.shape[0] - 1)
    bias0 = rt[rel_index]
    biasg = bias0.reshape(-1).reshape(NH, T, T)
    d["biasT"] = np.ascontiguousarray(biasg.transpose(0, 2, 1))
    d["rw"] = np.ascontiguousarray(np.asarray(inp["L1_ffn_router_w"]))
    d["rb"] = np.ascontiguousarray(np.asarray(inp["L1_ffn_router_b"]))
    d["W1_1"] = np.ascontiguousarray(np.asarray(inp["L1_ffn_w1"]))
    b1 = np.asarray(inp["L1_ffn_b1"])
    d["b1c"] = np.ascontiguousarray(np.stack([_col8(b1[e]) for e in range(NE1)]))
    d["W2_1"] = np.ascontiguousarray(np.asarray(inp["L1_ffn_w2"]))
    d["b2"] = np.ascontiguousarray(np.asarray(inp["L1_ffn_b2"]))
    d["ident"] = np.eye(128, dtype=np.float32)
    d["onecol"] = np.ones((128, 1), np.float32)
    d["onerow"] = np.ones((1, 128), np.float32)
    i8 = np.arange(64) % NE0
    d["G64"] = (i8[:, None] == i8[None, :]).astype(np.float32)
    oh8 = np.zeros((NE0, NE0 * 128), np.float32)
    for e in range(NE0):
        oh8[e, e * 128:(e + 1) * 128] = 1.0
    d["OH8"] = oh8
    oh16 = np.zeros((NE1, NE1 * 128), np.float32)
    for e in range(NE1):
        oh16[e, e * 128:(e + 1) * 128] = 1.0
    d["OH16"] = oh16
    return d



def _emit_norm_fm(c, buf, out_buf, eps, subtract_mean=True, scale_col_of=None,
                  gb=False, src_bf16=False):
    eps_col = c.MISC[:, {1e-6: 121, 1e-5: 122, 1e-3: 123}[eps]:
                        {1e-6: 121, 1e-5: 122, 1e-3: 123}[eps] + 1]
    """out = (buf - mu)/sqrt(var+eps) over the 1024-feature (partition) dim.
    scale_col_of(j): per-feature multiplier fused into the apply (rms nw).
    gb: apply gamma/beta columns from MISC. Processes per 512-token chunk.
    Row scratch: SC[0:1, 4096+q*512] for q in (sx, sq, mu, var).
    """
    nc, blk = c.nc, c.block
    onec = c.OB[:, 0:1] if src_bf16 else c.MR[:, 0:1]
    oner = c.MR[0:1, 1:129]
    inv_n = 1.0 / D
    PS_X, PS_Q = c.PS[6], c.PS[7]
    R_SX = c.SC[0:1, 4096:4608]
    R_MU = c.SC[0:1, 5120:5632]
    R_VR = c.SC[0:1, 5632:6144]
    R_RSTD = c.MR[0:1, 256:768]
    R_ROWB = c.MR[0:1, 768:1280]
    SQV = c.SC.bitcast(BF16)
    for tc in range(NTC):
        @blk.scalar
        def _(eng, tc=tc):
            c.wait(eng, "V", "T")
            for j in range(NDT):
                ins = eng.activation(
                    SQV[:, j * TC:(j + 1) * TC],
                    buf[:, c.dcols(j, tc * TC, (tc + 1) * TC)], AF.Square)
            c.inc(ins, "S")

        @blk.tensor
        def _(eng, tc=tc):
            c.wait(eng, "S", "V")
            if subtract_mean:
                for j in range(NDT):
                    eng.matmul(PS_X[0:1, 0:TC], onec,
                               buf[:, c.dcols(j, tc * TC, (tc + 1) * TC)],
                               start=(j == 0), stop=(j == NDT - 1))
            for j in range(NDT):
                ins = eng.matmul(PS_Q[0:1, 0:TC], c.OB[:, 0:1],
                                 SQV[:, j * TC:(j + 1) * TC],
                                 start=(j == 0), stop=(j == NDT - 1))
            c.inc(ins, "T")

        @blk.vector
        def _(eng, tc=tc):
            c.wait(eng, "T", "S")
            if subtract_mean:
                eng.tensor_scalar(R_MU, PS_X[0:1, 0:TC], inv_n, None, ALU.mult)
                eng.tensor_scalar(R_VR, PS_Q[0:1, 0:TC], inv_n, None, ALU.mult)
                eng.tensor_tensor(R_SX, R_MU, R_MU, ALU.mult)
                ins = eng.tensor_tensor(R_VR, R_VR, R_SX, ALU.subtract)
            else:
                ins = eng.tensor_scalar(R_VR, PS_Q[0:1, 0:TC], inv_n, None, ALU.mult)
            c.inc(ins, "V")

        @blk.scalar
        def _(eng, tc=tc):
            c.wait(eng, "V")
            eng.activation(R_VR, R_VR, AF.Ln, bias=eps_col[0:1, 0:1])
            ins = eng.activation(R_RSTD, R_VR, AF.Exp, scale=-0.5)
            c.inc(ins, "S")

        if subtract_mean:
            @blk.vector
            def _(eng, tc=tc):
                c.wait(eng, "S")
                eng.tensor_tensor(R_MU, R_MU, R_RSTD, ALU.mult)
                ins = eng.tensor_scalar(R_ROWB, R_MU, -1.0, None, ALU.mult)
                c.inc(ins, "V")

        @blk.tensor
        def _(eng, tc=tc):
            c.wait(eng, "V", "S")
            ins = eng.matmul(PS_X[:, 0:TC], oner, R_RSTD,
                             start=True, stop=True)
            if subtract_mean:
                ins = eng.matmul(PS_Q[:, 0:TC], oner, R_ROWB,
                                 start=True, stop=True)
            c.inc(ins, "T")

        @blk.scalar
        def _(eng, tc=tc):
            c.wait(eng, "T", "V")
            ins = eng.copy(c.SC[:, 3584:4096], PS_X[:, 0:TC])
            c.inc(ins, "S")

        @blk.vector
        def _(eng, tc=tc):
            c.wait(eng, "T", "S")
            rbc = c.SC[:, 3584:4096]
            tmp = c.SC[:, 3072:3584]
            for j in range(NDT):
                src = buf[:, c.dcols(j, tc * TC, (tc + 1) * TC)]
                dst = out_buf[:, c.dcols(j, tc * TC, (tc + 1) * TC)]
                if scale_col_of is not None:
                    ins = eng.scalar_tensor_tensor(dst, src, scale_col_of(j), rbc,
                                                   ALU.mult, ALU.mult)
                elif subtract_mean:
                    eng.tensor_tensor(tmp, src, rbc, ALU.mult)
                    if gb:
                        eng.tensor_tensor(tmp, tmp, PS_Q[:, 0:TC], ALU.add)
                        ins = eng.tensor_scalar(dst, tmp, c.mcol(M_GAM, j),
                                                c.mcol(M_BET, j), ALU.mult, ALU.add)
                    else:
                        ins = eng.tensor_tensor(dst, tmp, PS_Q[:, 0:TC], ALU.add)
                else:
                    ins = eng.tensor_tensor(dst, src, rbc, ALU.mult)
            c.inc(ins, "V")



def _emit_l0_attn(c, dram):
    nc, blk = c.nc, c.block
    xc_buf = [c.BA, c.BB, c.BA]
    h_buf = [c.BB, c.BA, c.BB]
    DGB = c.WB[:, 7168:7680]

    def load_cols(dst_c0, dst_c1, src):
        @blk.sync
        def _(eng):
            c.wait(eng, "V", "S", "T")
            ins = eng.dma_start(out=c.MISC[:, dst_c0:dst_c1], in_=src)
            c.inc(ins, "Y", 16)

    ltok = {}
    for l in range(NL):
        XC = xc_buf[l]
        H = h_buf[l]
        if l == 0:
            xin = h_buf[0]
            for tc in range(NTC):
                @blk.vector
                def _(eng, tc=tc):
                    c.wait(eng, "G", "S", "T")
                    for j in range(NDT):
                        ins = eng.tensor_copy(
                            xin[:, c.dcols(j, tc * TC, (tc + 1) * TC)],
                            c.XR[:, c.dcols(j, tc * TC, (tc + 1) * TC)])
                    c.inc(ins, "V")
        else:
            xin = xc_buf[l - 1]
        load_cols(M_CW, M_CW + 32, dram["cwc"][l])
        load_cols(M_CB, M_CB + 8, dram["cb"][l])
        load_cols(M_WRB, M_WRB + 8, dram["wrb"][l])
        load_cols(M_WIB, M_WIB + 8, dram["wib"][l])
        load_cols(M_LA, M_LA + 8, dram["la"][l])
        load_cols(M_NW, M_NW + 8, dram["nw"][l])

        @blk.scalar
        def _(eng, l=l):
            c.wait(eng, "Y", "V")
            eng.activation(c.MISC[:, M_LSA:M_LSA + 8], c.MISC[:, M_LA:M_LA + 8],
                           AF.Sigmoid)
            ins = eng.activation(c.MISC[:, M_LSA:M_LSA + 8], c.MISC[:, M_LSA:M_LSA + 8],
                                 AF.Ln)
            c.inc(ins, "S")

        for j in range(NDT):
            @blk.vector
            def _(eng, l=l, j=j):
                c.wait(eng, "Y")
                c.wait_at(eng, "T", ltok.get("convt", 0))
                dg = DGB
                for k in range(KCONV):
                    ins = eng.tensor_scalar(dg[:, k * 128:(k + 1) * 128], c.IDT[:],
                                            c.mcol(M_CW, k * NDT + j), None, ALU.mult)
                ltok["diag"] = c.inc(ins, "V")

            @blk.tensor
            def _(eng, l=l, j=j, xin=xin):
                c.wait_at(eng, "V", ltok["diag"])
                c.wait_at(eng, "S", ltok.get("convs", 0))
                c.wait_at(eng, "S", ltok.get("wos", 0))
                if l == 0 and j == 0:
                    c.wait(eng, "V", "G")
                dg = DGB
                for tc in range(NTC):
                    bank = c.PS[tc]
                    for k in (3, 2, 1, 0):
                        off = KCONV - 1 - k
                        lo = tc * TC - off
                        if lo >= 0:
                            ins = eng.matmul(bank[:, 0:TC], dg[:, k * 128:(k + 1) * 128],
                                             xin[:, c.dcols(j, lo, lo + TC)],
                                             start=(k == 3), stop=(k == 0))
                        else:
                            ins = eng.matmul(bank[:, off:TC], dg[:, k * 128:(k + 1) * 128],
                                             xin[:, c.dcols(j, 0, TC - off)],
                                             start=(k == 3), stop=(k == 0))
                ltok["convt"] = c.inc(ins, "T")

            @blk.scalar
            def _(eng, j=j, XC=XC):
                c.wait_at(eng, "T", ltok["convt"])
                ins = eng.activation(XC[:, c.dcols(j)], c.PSA[:, 0:2048],
                                     AF.Identity, bias=c.mcol(M_CB, j))
                ltok["convs"] = c.inc(ins, "S")

        wtok = {}

        def load_wblock(kind, l_, jj, slot):
            src = dram[kind][l_][:, jj * 128:(jj + 1) * 128]
            src = src.rearrange("(kt p) o -> p kt o", p=128)

            @blk.gpsimd
            def _(eng):
                c.wait(eng, "T")
                ins = eng.dma_start(out=c.WB[:, slot * 1024:(slot + 1) * 1024], in_=src)
                wtok[(kind, jj)] = c.inc(ins, "G", 16)

        load_wblock("wrT", l, 0, 0)
        load_wblock("wiT", l, 0, 2)

        for j in range(NDT):
            if j + 1 < NDT:
                load_wblock("wrT", l, j + 1, (j + 1) % 2)
                load_wblock("wiT", l, j + 1, 2 + (j + 1) % 2)
            for kind, slot_base, bank_base, bias_c, dst, tk in (
                    ("wrT", 0, 4, M_WRB, c.t1, "r"), ("wiT", 2, 0, M_WIB, c.t4, "i")):
                slot = slot_base + j % 2

                @blk.tensor
                def _(eng, j=j, slot=slot, kind=kind, bank_base=bank_base, XC=XC, tk=tk):
                    c.wait_at(eng, "G", wtok[(kind, j)])
                    c.wait_at(eng, "S", ltok.get(tk + "s", 0))
                    if j == 0:
                        c.wait(eng, "S", "V")
                    for tc in range(NTC):
                        bank = c.PS[bank_base + tc]
                        for kt in range(NDT):
                            ins = eng.matmul(
                                bank[:, 0:TC],
                                c.WB[:, slot * 1024 + kt * 128:slot * 1024 + (kt + 1) * 128],
                                XC[:, c.dcols(kt, tc * TC, (tc + 1) * TC)],
                                start=(kt == 0), stop=(kt == NDT - 1))
                    ltok[tk + "t"] = c.inc(ins, "T")

                @blk.scalar
                def _(eng, j=j, bias_c=bias_c, dst=dst, bank_base=bank_base, tk=tk):
                    c.wait_at(eng, "T", ltok[tk + "t"])
                    c.wait_at(eng, "V", ltok.get("chainv", 0))
                    psrc = c.PSB if bank_base == 4 else c.PSA
                    ins = eng.activation(dst(), psrc[:, 0:2048],
                                         AF.Sigmoid, bias=c.mcol(bias_c, j))
                    ltok[tk + "s"] = c.inc(ins, "S")

            @blk.vector
            def _(eng, j=j):
                c.wait(eng, "S")
                eng.tensor_scalar(c.t2(), c.t1(), c.mcol(M_LSA, j), 8.0,
                                  ALU.mult, ALU.mult)
                eng.tensor_tensor_scan(c.t1(), c.ONES[:], c.t2(), 0.0, ALU.mult, ALU.add)
                ins = eng.tensor_scalar(c.t1(), c.t1(), c.MISC[:, 120:121], None, ALU.max)
                c.inc(ins, "V")

            @blk.scalar
            def _(eng, j=j):
                c.wait(eng, "V")
                eng.activation(c.t3(), c.t1(), AF.Exp)
                eng.activation(c.t1(), c.t1(), AF.Exp, scale=-1.0)
                eng.activation(c.t2(), c.t2(), AF.Exp)
                ins = eng.activation(c.t2(), c.t2(), AF.Square)
                c.inc(ins, "S")

            @blk.vector
            def _(eng, j=j):
                c.wait(eng, "S")
                eng.tensor_scalar(c.t2(), c.t2(), -1.0, 1.0, ALU.mult, ALU.add)
                ins = eng.tensor_scalar(c.t2(), c.t2(), c.MISC[:, 121:122], None, ALU.max)
                c.inc(ins, "V")

            @blk.scalar
            def _(eng, j=j):
                c.wait(eng, "V")
                eng.activation(c.t2(), c.t2(), AF.Ln)
                ins = eng.activation(c.t2(), c.t2(), AF.Exp, scale=0.5)
                c.inc(ins, "S")

            @blk.vector
            def _(eng, j=j, XC=XC, H=H):
                c.wait(eng, "S")
                eng.tensor_tensor(c.t4(), c.t4(), XC[:, c.dcols(j)], ALU.mult)
                eng.tensor_tensor(c.t4(), c.t4(), c.t2(), ALU.mult)
                eng.tensor_tensor(c.t4(), c.t4(), c.t1(), ALU.mult)
                eng.tensor_tensor_scan(c.t2(), c.ONES[:], c.t4(), 0.0, ALU.mult, ALU.add)
                ins = eng.tensor_tensor(H[:, c.dcols(j)], c.t3(), c.t2(), ALU.mult)
                ltok["chainv"] = c.inc(ins, "V")

        load_wblock("woT", l, 0, 4)
        for j in range(NDT):
            if j + 1 < NDT:
                load_wblock("woT", l, j + 1, 4 + (j + 1) % 2)
            slot = 4 + j % 2

            @blk.tensor
            def _(eng, j=j, slot=slot, H=H):
                c.wait_at(eng, "G", wtok[("woT", j)])
                c.wait_at(eng, "S", ltok.get("wos", 0))
                if j == 0:
                    c.wait(eng, "V", "S")
                for tc in range(NTC):
                    bank = c.PS[tc]
                    for kt in range(NDT):
                        ins = eng.matmul(
                            bank[:, 0:TC],
                            c.WB[:, slot * 1024 + kt * 128:slot * 1024 + (kt + 1) * 128],
                            H[:, c.dcols(kt, tc * TC, (tc + 1) * TC)],
                            start=(kt == 0), stop=(kt == NDT - 1))
                ltok["wot"] = c.inc(ins, "T")

            @blk.scalar
            def _(eng, j=j, XC=XC):
                c.wait_at(eng, "T", ltok["wot"])
                ins = eng.copy(XC[:, c.dcols(j)], c.PSA[:, 0:2048])
                ltok["wos"] = c.inc(ins, "S")

        _emit_norm_fm(c, XC, XC, 1e-6, subtract_mean=False,
                      scale_col_of=lambda j: c.mcol(M_NW, j), src_bf16=True)

    load_cols(M_NOW, M_NOW + 8, dram["now"])
    yb = xc_buf[NL - 1]
    _emit_norm_fm(c, yb, yb, 1e-6, subtract_mean=False,
                  scale_col_of=lambda j: c.mcol(M_NOW, j), src_bf16=True)
    for tc in range(NTC):
        @blk.vector
        def _(eng, tc=tc):
            c.wait(eng, "V", "S", "G")
            for j in range(NDT):
                ins = eng.tensor_tensor(c.XR[:, c.dcols(j, tc * TC, (tc + 1) * TC)],
                                        c.XR[:, c.dcols(j, tc * TC, (tc + 1) * TC)],
                                        yb[:, c.dcols(j, tc * TC, (tc + 1) * TC)], ALU.add)
            c.inc(ins, "V")
    _emit_norm_fm(c, c.XR, c.XR, 1e-5, subtract_mean=True)


def _emit_l0_ffn(c, dram):
    nc, blk = c.nc, c.block
    BBF = c.BB.bitcast(F32)
    PR = lambda ch: c.SC[:, ch * 8:(ch + 1) * 8]
    PALL = c.SC[0:64, 128:2176]
    IND = c.SC[0:64, 2176:4224]
    TH = c.SC[0:64, 4224:4225]
    PLOC = c.SC[0:8, 5120:7168]

    @blk.gpsimd
    def _(eng):
        c.wait(eng, "T", "V", "S")
        ins = eng.dma_start(out=c.MR[:, 1472:1536],
                            in_=dram["Wg"].rearrange("(kt p) e -> p kt e", p=128))
        c.inc(ins, "G", 16)

        ins = eng.dma_start(out=c.BB[0:8, 2560:3584], in_=dram["OH8"])
        c.inc(ins, "G", 16)

    for ch in range(16):
        @blk.tensor
        def _(eng, ch=ch):
            c.wait(eng, "G", "V", "S")
            for kt in range(NDT):
                ins = eng.matmul(c.PS[2][:, 0:8],
                                 c.XR[:, c.dcols(kt, ch * 128, (ch + 1) * 128)],
                                 c.MR[:, 1472 + kt * 8:1472 + (kt + 1) * 8],
                                 start=(kt == 0), stop=(kt == NDT - 1))
            c.inc(ins, "T")

        @blk.vector
        def _(eng, ch=ch):
            c.wait(eng, "T", "S")
            mx = c.SC[:, 4288:4289]
            c.pb(eng, eng.tensor_reduce(mx, c.PS[2][:, 0:8], AX.X, ALU.max))
            ins = eng.tensor_scalar(c.SC[:, 4272:4280], c.PS[2][:, 0:8], mx, None,
                                    ALU.subtract)
            c.inc(ins, "V")

        @blk.scalar
        def _(eng, ch=ch):
            c.wait(eng, "V")
            ins = eng.activation(c.SC[:, 4272:4280], c.SC[:, 4272:4280], AF.Exp)
            c.inc(ins, "S")

        @blk.vector
        def _(eng, ch=ch):
            c.wait(eng, "S")
            sm = c.SC[:, 4289:4290]
            c.pb(eng, eng.tensor_reduce(sm, c.SC[:, 4272:4280], AX.X, ALU.add))
            c.pb(eng, eng.reciprocal(sm, sm))
            ins = eng.tensor_scalar(PR(ch), c.SC[:, 4272:4280], sm, None, ALU.mult)
            c.inc(ins, "V")

        @blk.tensor
        def _(eng, ch=ch):
            c.wait(eng, "V")
            ins = eng.transpose(c.PS[3][0:8, 0:128], PR(ch), c.IDT[:])
            c.inc(ins, "T")

        @blk.vector
        def _(eng, ch=ch):
            c.wait(eng, "T")
            ins = eng.tensor_copy(PLOC[:, ch * 128:(ch + 1) * 128], c.PS[3][0:8, 0:128])
            c.inc(ins, "V")

    @blk.sync
    def _(eng):
        c.wait(eng, "V")
        ins = eng.dma_start(out=dram["cc_in"], in_=PLOC)
        c.inc(ins, "Y", 16)
        ins = eng.dma_start(out=c.SC[0:64, 4320:4384], in_=dram["G64"])
        c.inc(ins, "Y", 16)

    @blk.gpsimd
    def _(eng):
        c.wait(eng, "Y")
        ins = eng.collective_compute(
            "AllGather", ALU.bypass, replica_groups=[list(range(B))],
            ins=[dram["cc_in"].opt()], outs=[dram["cc_out"].opt()])
        c.inc(ins, "C")
        eng.wait_ge(c.sem["C"], c.cnt["C"])
        ins = eng.dma_start(out=PALL, in_=dram["cc_out"])
        c.inc(ins, "G", 16)

    @blk.vector
    def _(eng):
        c.wait(eng, "G")
        ins = eng.memset(TH, 0.5)
        c.inc(ins, "V")
    NIT = 26
    for it in range(NIT):
        step = 0.5 ** (it + 2)

        @blk.vector
        def _(eng, it=it):
            c.wait(eng, "T", "V")
            c.pb(eng, eng.tensor_scalar(IND, PALL, TH, None, ALU.is_gt))
            ins = eng.tensor_reduce(c.SC[0:64, 4384:4385], IND, AX.X, ALU.add)
            c.inc(ins, "V")

        @blk.tensor
        def _(eng, it=it):
            c.wait(eng, "V")
            if it == 0:
                c.wait(eng, "Y")
            ins = eng.matmul(c.PS[2][0:64, 0:1], c.SC[0:64, 4320:4384],
                             c.SC[0:64, 4384:4385], start=True, stop=True)
            c.inc(ins, "T")

        @blk.vector
        def _(eng, it=it, step=step):
            c.wait(eng, "T")
            d = c.SC[0:64, 4226:4227]
            c.pb(eng, eng.tensor_scalar(d, c.PS[2][0:64, 0:1], 2048.0, None, ALU.is_gt))
            c.pb(eng, eng.tensor_scalar(d, d, 2.0, -1.0, ALU.mult, ALU.add))
            ins = eng.scalar_tensor_tensor(TH, d, float(step), TH, ALU.mult, ALU.add)
            c.inc(ins, "V")

    @blk.vector
    def _(eng):
        ins = eng.tensor_scalar(TH, TH, 4.0 * (0.5 ** (NIT + 1)), None, ALU.add)
        c.inc(ins, "V")

    @blk.tensor
    def _(eng):
        c.wait(eng, "V")
        ins = eng.transpose(c.PS[2][0:1, 0:8], TH[0:8, 0:1], c.IDT[0:8, 0:8])
        c.inc(ins, "T")

    @blk.vector
    def _(eng):
        c.wait(eng, "T")
        ins = eng.tensor_copy(c.SC[0:1, 4400:4408], c.PS[2][0:1, 0:8])
        c.inc(ins, "V")

    @blk.tensor
    def _(eng):
        c.wait(eng, "V")
        ins = eng.matmul(c.PS[2][:, 0:8], c.ONES[0:1, 0:128], c.SC[0:1, 4400:4408],
                         start=True, stop=True)
        c.inc(ins, "T")

    @blk.scalar
    def _(eng):
        c.wait(eng, "T")
        ins = eng.copy(c.SC[:, 4240:4248], c.PS[2][:, 0:8])
        c.inc(ins, "S")

    for ch in range(16):
        @blk.vector
        def _(eng, ch=ch):
            c.wait(eng, "S", "T")
            thb = c.SC[:, 4240:4248]
            sel = c.SC[:, 4256:4264]
            tB = c.SC[:, 4272:4280]
            nselc = c.SC[:, 4288:4289]
            asg = c.SC[:, 4289:4290]
            mxs = c.SC[:, 4290:4291]
            mxa = c.SC[:, 4291:4292]
            wv = c.SC[:, 4292:4293]
            inda = c.SC[:, 4304:4312]
            indb = c.SC[:, 4312:4320]
            c.pb(eng, eng.tensor_tensor(sel, PR(ch), thb, ALU.is_gt))
            c.pb(eng, eng.tensor_reduce(nselc, sel, AX.X, ALU.add))
            c.pb(eng, eng.tensor_scalar(asg, nselc, 0.5, None, ALU.is_gt))
            c.pb(eng, eng.tensor_scalar(tB, PR(ch), 10000.0, None, ALU.add))
            c.pb(eng, eng.tensor_tensor(tB, tB, sel, ALU.mult))
            c.pb(eng, eng.tensor_scalar(tB, tB, 10000.0, None, ALU.subtract))
            c.pb(eng, eng.tensor_reduce(mxs, tB, AX.X, ALU.max))
            c.pb(eng, eng.tensor_reduce(mxa, PR(ch), AX.X, ALU.max))
            c.pb(eng, eng.tensor_tensor(wv, mxs, mxa, ALU.subtract))
            c.pb(eng, eng.scalar_tensor_tensor(wv, wv, asg, mxa, ALU.mult, ALU.add))
            c.pb(eng, eng.tensor_scalar(inda, tB, mxs, None, ALU.is_equal))
            c.pb(eng, eng.tensor_scalar(indb, PR(ch), mxa, None, ALU.is_equal))
            c.pb(eng, eng.tensor_tensor(inda, inda, indb, ALU.subtract))
            c.pb(eng, eng.scalar_tensor_tensor(inda, inda, asg, indb, ALU.mult, ALU.add))
            ins = eng.tensor_scalar(inda, inda, wv, None, ALU.mult)
            c.inc(ins, "V")

        @blk.tensor
        def _(eng, ch=ch):
            c.wait(eng, "V")
            ins = eng.transpose(c.PS[3][0:8, 0:128], c.SC[:, 4304:4312], c.IDT[:])
            c.inc(ins, "T")

        @blk.vector
        def _(eng, ch=ch):
            c.wait(eng, "T")
            ins = eng.tensor_copy(c.BB[0:8, ch * 128:(ch + 1) * 128],
                                  c.PS[3][0:8, 0:128])
            c.inc(ins, "V")

    if _DBG["on"]:
        @blk.sync
        def _(eng):
            c.wait(eng, "V", "S", "T")
            eng.dma_start(out=dram["dbg_pr"], in_=c.SC[:, 0:128]).then_inc(c.sem["O"], 16)
            eng.dma_start(out=dram["dbg_th"], in_=TH).then_inc(c.sem["O"], 16)
            c.cnt["O"] += 32

        @blk.gpsimd
        def _(eng):
            c.wait(eng, "V")
            ins = eng.dma_start(out=dram["dbg_gate"], in_=c.BB[0:8, 0:2048])
            c.inc(ins, "G", 16)

    wtok = {}
    tok = {"gbt": 0, "gbc": 0, "comb": 0}

    ctr = {"w1": 0, "w2": 0}
    for s_ in range(3):
        tok[("w1u", s_)] = c.cnt["T"]
        tok[("w2u", s_)] = c.cnt["T"]

    def load_w1(e, fc, _slot_unused=None):
        slot = ctr["w1"] % 3
        ctr["w1"] += 1
        wtok[("w1s", e, fc)] = slot
        src = dram["W1_0"][e][:, fc * 128:(fc + 1) * 128]
        src = src.rearrange("(kt p) f -> p kt f", p=128)

        @blk.gpsimd
        def _(eng, slot=slot):
            c.wait_at(eng, "T", tok.get(("w1u", slot), 0))
            ins = eng.dma_start(out=c.WR[:, slot * 1024:(slot + 1) * 1024], in_=src)
            wtok[("w1", e, fc)] = c.inc(ins, "G", 16)

    def load_w2(e, oc, _slot_unused=None):
        slot = ctr["w2"] % 3
        ctr["w2"] += 1
        wtok[("w2s", e, oc)] = slot
        src = dram["W2_0"][e][:, oc * 128:(oc + 1) * 128]
        src = src.rearrange("(fc p) o -> p fc o", p=128)

        @blk.gpsimd
        def _(eng, slot=slot):
            c.wait_at(eng, "T", tok.get(("w2u", slot), 0))
            ins = eng.dma_start(out=c.WB[:, slot * 2048:slot * 2048 + 2048], in_=src)
            wtok[("w2", e, oc)] = c.inc(ins, "G", 16)

    pf_s = {0: c.cnt["S"], 1: c.cnt["S"]}
    pf_v = {0: c.cnt["V"], 1: c.cnt["V"]}
    tok["gbc"] = c.cnt["S"]
    tok["comb"] = c.cnt["V"]
    for half in range(2):
        t0 = half * 1024
        load_w1(0, 0, 0)
        load_w1(0, 1, 1)
        for e in range(NE0):
            for fc in range(16):
                la, lb = (e, fc + 2) if fc + 2 < 16 else (e + 1, fc + 2 - 16)
                if la < NE0:
                    load_w1(la, lb, (fc + 2) % 3)
                pr = fc % 2
                bk0, bk1 = c.PS[2 * pr], c.PS[2 * pr + 1]

                @blk.tensor
                def _(eng, e=e, fc=fc, pr=pr, bk0=bk0, bk1=bk1, t0=t0):
                    c.wait_at(eng, "G", wtok[("w1", e, fc)])
                    c.wait_at(eng, "S", pf_s[pr])
                    c.wait_at(eng, "V", pf_v[pr])
                    slot = wtok[("w1s", e, fc)]
                    for tq, bank in ((0, bk0), (1, bk1)):
                        for kt in range(NDT):
                            ins = eng.matmul(
                                bank[:, 0:TC],
                                c.WR[:, slot * 1024 + kt * 128:slot * 1024 + (kt + 1) * 128],
                                c.XR[:, c.dcols(kt, t0 + tq * TC, t0 + (tq + 1) * TC)],
                                start=(kt == 0), stop=(kt == NDT - 1))
                    tok[("h1t", pr)] = tok[("w1u", slot)] = c.inc(ins, "T")

                @blk.scalar
                def _(eng, fc=fc, pr=pr, bk0=bk0):
                    c.wait_at(eng, "T", tok[("h1t", pr)])
                    ins = eng.activation(c.BA[:, fc * 1024:(fc + 1) * 1024],
                                         c.PSA[:, pr * 1024:(pr + 1) * 1024], AF.Gelu)
                    pf_s[pr] = c.inc(ins, "S")

            load_w2(e, 0, 0)
            load_w2(e, 1, 1)
            for oc in range(NDT):
                la, lb = (e, oc + 2) if oc + 2 < NDT else (e + 1, oc + 2 - NDT)
                if la < NE0:
                    load_w2(la, lb, (oc + 2) % 3)
                pr = oc % 2
                bk0, bk1 = c.PS[2 * pr], c.PS[2 * pr + 1]

                @blk.tensor
                def _(eng, e=e, oc=oc, pr=pr, bk0=bk0, bk1=bk1, half=half):
                    c.wait_at(eng, "G", wtok[("w2", e, oc)])
                    c.wait_at(eng, "S", max(pf_s[0], pf_s[1]))
                    c.wait_at(eng, "V", pf_v[pr])
                    slot = wtok[("w2s", e, oc)]
                    for tq, bank in ((0, bk0), (1, bk1)):
                        for fc in range(16):
                            ins = eng.matmul(
                                bank[:, 0:TC],
                                c.WB[:, slot * 2048 + fc * 128:slot * 2048 + (fc + 1) * 128],
                                c.BA[:, fc * 1024 + tq * TC:fc * 1024 + (tq + 1) * TC],
                                start=(fc == 0), stop=(fc == 15))
                    tok[("h2t", pr)] = tok[("w2u", slot)] = c.inc(ins, "T")
                    c.wait_at(eng, "S", tok["gbc"])
                    eng.matmul(c.PS[6][:, 0:TC],
                               c.BB[0:8, 2560 + e * 128:2560 + (e + 1) * 128],
                               c.BB[0:8, half * 1024:half * 1024 + TC],
                               start=True, stop=True)
                    ins = eng.matmul(c.PS[7][:, 0:TC],
                                     c.BB[0:8, 2560 + e * 128:2560 + (e + 1) * 128],
                                     c.BB[0:8, half * 1024 + TC:half * 1024 + 1024],
                                     start=True, stop=True)
                    tok["gbt"] = c.inc(ins, "T")

                @blk.scalar
                def _(eng):
                    c.wait_at(eng, "T", tok["gbt"])
                    c.wait_at(eng, "V", tok["comb"])
                    eng.copy(c.BB[:, 2048:2560], c.PS[6][:, 0:TC])
                    ins = eng.copy(c.BB[:, 3584:4096], c.PS[7][:, 0:TC])
                    tok["gbc"] = c.inc(ins, "S")

                @blk.vector
                def _(eng, e=e, oc=oc, pr=pr, bk0=bk0, bk1=bk1):
                    c.wait_at(eng, "T", tok[("h2t", pr)])
                    c.wait_at(eng, "S", tok["gbc"])
                    acc = c.SC[:, oc * 1024:(oc + 1) * 1024]
                    for tq, bank, gc in ((0, bk0, 2048), (1, bk1, 3584)):
                        a_ = acc[:, tq * TC:(tq + 1) * TC]
                        gb = c.BB[:, gc:gc + 512]
                        if e == 0:
                            ins = eng.tensor_tensor(a_, bank[:, 0:TC], gb, ALU.mult)
                        else:
                            tmp = c.BB[:, 4608:5120]
                            eng.tensor_tensor(tmp, bank[:, 0:TC], gb, ALU.mult)
                            ins = eng.tensor_tensor(a_, a_, tmp, ALU.add)
                    pf_v[pr] = tok["comb"] = c.inc(ins, "V")

        for j in range(NDT):
            @blk.vector
            def _(eng, j=j, t0=t0):
                c.wait(eng, "V", "S")
                ins = eng.tensor_tensor(
                    c.XR[:, c.dcols(j, t0, t0 + 1024)],
                    c.XR[:, c.dcols(j, t0, t0 + 1024)],
                    c.SC[:, j * 1024:(j + 1) * 1024], ALU.add)
                c.inc(ins, "V")

    _emit_norm_fm(c, c.XR, c.XR, 1e-5, subtract_mean=True)


def _emit_l1_attn(c, dram):
    nc, blk = c.nc, c.block
    QP = lambda c0, c1: c.BB[:, c0:c1]
    VW0 = 4096
    EXS = 6176
    BTR = 8224
    @blk.sync
    def _(eng):
        c.wait(eng, "V", "S", "T")
        ins = eng.dma_start(out=c.MISC[:, M_GAM:M_GAM + 8], in_=dram["gamma"])
        c.inc(ins, "Y", 16)
        ins = eng.dma_start(out=c.MISC[:, M_BET:M_BET + 8], in_=dram["beta"])
        c.inc(ins, "Y", 16)

    wtok = {}

    def load_wpair(kind, gp, slot):
        src = dram[kind][:, gp * 128:(gp + 1) * 128]
        src = src.rearrange("(kt p) o -> p kt o", p=128)

        @blk.gpsimd
        def _(eng):
            c.wait(eng, "T")
            ins = eng.dma_start(out=c.WR[:, slot * 1024:(slot + 1) * 1024], in_=src)
            wtok[(kind, gp)] = c.inc(ins, "G", 16)

    btok = {}

    def load_bias(g, kc, qc):
        src = dram["biasT"][g][kc * 128:(kc + 1) * 128, qc * TC:(qc + 1) * TC]

        @blk.gpsimd
        def _(eng, kc=kc):
            c.wait_at(eng, "T", bslot_tok.get(kc, 0))
            ins = eng.dma_start(out=c.WB[:, kc * TC:(kc + 1) * TC], in_=src)
            btok[(g, kc, qc)] = c.inc(ins, "G", 16)

    tok = {"qkc": c.cnt["S"], "vv": 0, "pv": 0, "cmb": c.cnt["V"],
           "rcp": c.cnt["V"], "bcc": c.cnt["S"]}
    exp_tok = {0: c.cnt["S"], 1: c.cnt["S"]}
    pvu_tok = {0: c.cnt["T"], 1: c.cnt["T"]}
    bslot_tok = {}
    for kc_ in range(16):
        bslot_tok[kc_] = c.cnt["T"]
    for gp in range(NH // 2):
        for idx, kind in enumerate(("Wq", "Wk", "Wv")):
            load_wpair(kind, gp, idx)
        for idx, kind in enumerate(("Wq", "Wk")):
            @blk.tensor
            def _(eng, kind=kind, idx=idx, gp=gp):
                c.wait_at(eng, "G", wtok[(kind, gp)])
                c.wait_at(eng, "S", tok["qkc"])
                c.wait_at(eng, "S", exp_tok[0])
                c.wait_at(eng, "S", exp_tok[1])
                c.wait_at(eng, "V", tok["cmb"])
                for tc in range(NTC):
                    for kt in range(NDT):
                        ins = eng.matmul(
                            c.PS[tc][:, 0:TC],
                            c.WR[:, idx * 1024 + kt * 128:idx * 1024 + (kt + 1) * 128],
                            c.XR[:, c.dcols(kt, tc * TC, (tc + 1) * TC)],
                            start=(kt == 0), stop=(kt == NDT - 1))
                tok["qkt"] = c.inc(ins, "T")

            @blk.scalar
            def _(eng, idx=idx):
                c.wait_at(eng, "T", tok["qkt"])
                ins = eng.copy(c.BB[:, idx * 2048:(idx + 1) * 2048], c.PSA[:, 0:2048])
                tok["qkc"] = c.inc(ins, "S")

        @blk.vector
        def _(eng, gp=gp):
            c.wait_at(eng, "S", tok["qkc"])
            for tt in range(16):
                for h in range(2):
                    ins = eng.memset(c.BB[:, VW0 + (tt * 2 + h) * 65 + 64:VW0 + (tt * 2 + h) * 65 + 65], 1.0)
            tok["vz"] = c.inc(ins, "V")

        for tt in range(16):
            bank = c.PS[tt % 2]

            @blk.tensor
            def _(eng, tt=tt, bank=bank, gp=gp):
                c.wait_at(eng, "G", wtok[("Wv", gp)])
                if tt == 0:
                    c.wait_at(eng, "S", tok["qkc"])
                    c.wait_at(eng, "V", tok["vz"])
                else:
                    c.wait_at(eng, "V", tok.get(("vc", tt % 2), 0))
                for kt in range(NDT):
                    ins = eng.matmul(
                        bank[:, 0:128],
                        c.XR[:, c.dcols(kt, tt * 128, (tt + 1) * 128)],
                        c.WR[:, 2 * 1024 + kt * 128:2 * 1024 + (kt + 1) * 128],
                        start=(kt == 0), stop=(kt == NDT - 1))
                tok[("vt", tt % 2)] = c.inc(ins, "T")

            @blk.vector
            def _(eng, tt=tt, bank=bank):
                c.wait_at(eng, "T", tok[("vt", tt % 2)])
                eng.tensor_copy(c.BB[:, VW0 + (tt * 2) * 65:VW0 + (tt * 2) * 65 + 64],
                                bank[:, 0:64])
                ins = eng.tensor_copy(c.BB[:, VW0 + (tt * 2 + 1) * 65:VW0 + (tt * 2 + 1) * 65 + 64],
                                      bank[:, 64:128])
                tok[("vc", tt % 2)] = c.inc(ins, "V")

        for hh in range(2):
            g = gp * 2 + hh
            rb = hh * 64

            @blk.tensor
            def _(eng):
                eng.drain()

            for qc in range(NTC):
                PV = c.PS[4]
                BV = c.PS[5]
                for g2 in range(8):
                    load_bias(g, g2 * 2, qc)
                    load_bias(g, g2 * 2 + 1, qc)
                    pr = g2 % 2
                    bk0, bk1 = c.PS[2 * pr], c.PS[2 * pr + 1]

                    @blk.tensor
                    def _(eng, g=g, rb=rb, qc=qc, g2=g2, pr=pr, bk0=bk0, bk1=bk1):
                        c.wait_at(eng, "S", exp_tok[pr])
                        if g2 < 2:
                            c.wait_at(eng, "V", tok.get(("vc", 0), 0))
                            c.wait_at(eng, "V", tok.get(("vc", 1), 0))
                        for i, bank in ((0, bk0), (1, bk1)):
                            kc = g2 * 2 + i
                            ins = eng.matmul(
                                bank[:, 0:TC],
                                c.BB[rb:rb + 64, 2048 + kc * 128:2048 + (kc + 1) * 128],
                                c.BB[rb:rb + 64, qc * TC:(qc + 1) * TC],
                                start=True, stop=True)
                        tok[("sct", pr)] = c.inc(ins, "T")

                    @blk.scalar
                    def _(eng, g2=g2, pr=pr, bk0=bk0):
                        c.wait_at(eng, "T", tok[("sct", pr)])
                        c.wait_at(eng, "T", pvu_tok[pr])
                        ins = eng.activation(
                            c.BB[:, EXS + pr * 1024:EXS + (pr + 1) * 1024],
                            c.PSA[:, pr * 1024:(pr + 1) * 1024], AF.Exp,
                            scale=float(D) ** -0.5)
                        exp_tok[pr] = c.inc(ins, "S")

                    @blk.tensor
                    def _(eng, g=g, qc=qc, g2=g2, pr=pr, PV=PV):
                        c.wait_at(eng, "S", exp_tok[pr])
                        if g2 == 0:
                            c.wait_at(eng, "V", tok["cmb"])
                        for i in range(2):
                            kc = g2 * 2 + i
                            vw = c.BB[:, VW0 + (kc * 2 + g % 2) * 65:VW0 + (kc * 2 + g % 2) * 65 + 65]
                            ins = eng.matmul(PV[0:65, 0:TC], vw,
                                             c.BB[:, EXS + pr * 1024 + i * TC:EXS + pr * 1024 + (i + 1) * TC],
                                             start=(kc == 0), stop=(kc == 15))
                        pvu_tok[pr] = c.inc(ins, "T")

                @blk.vector
                def _(eng, PV=PV):
                    c.wait_at(eng, "T", c.cnt["T"])
                    with nc.allow_low_precision(reason="f32r recip row for bcast matmul"):
                        ins = eng.reciprocal(c.MR[0:1, 256:768][0:1, 0:TC], PV[64:65, 0:TC])
                    tok["rcp"] = c.inc(ins, "V")

                @blk.tensor
                def _(eng, g=g, qc=qc, BV=BV):
                    for kc in range(16):
                        c.wait_at(eng, "G", btok[(g, kc, qc)])
                        vw = c.BB[:, VW0 + (kc * 2 + g % 2) * 65:VW0 + (kc * 2 + g % 2) * 65 + 65]
                        ins = eng.matmul(BV[0:65, 0:TC], vw,
                                         c.WB[:, kc * TC:(kc + 1) * TC],
                                         start=(kc == 0), stop=(kc == 15))
                        bslot_tok[kc] = c.inc(ins, "T")

                @blk.tensor
                def _(eng):
                    c.wait_at(eng, "V", tok["rcp"])
                    ins = eng.matmul(c.PS[6][0:64, 0:TC], c.MR[0:1, 65:129],
                                     c.MR[0:1, 256:768][0:1, 0:TC], start=True, stop=True)
                    tok["bct"] = c.inc(ins, "T")

                @blk.scalar
                def _(eng):
                    c.wait_at(eng, "T", tok["bct"])
                    c.wait_at(eng, "V", tok["cmb"])
                    ins = eng.copy(c.SC[0:64, 0:TC], c.PS[6][0:64, 0:TC])
                    tok["bcc"] = c.inc(ins, "S")

                @blk.vector
                def _(eng, g=g, qc=qc, PV=PV, BV=BV):
                    c.wait_at(eng, "S", tok["bcc"])
                    c.wait_at(eng, "T", c.cnt["T"])
                    tmp = c.SC[0:64, 512:1024]
                    eng.tensor_tensor(tmp, PV[0:64, 0:TC], c.SC[0:64, 0:TC], ALU.mult)
                    dst = c.BA[(g % 2) * 64:(g % 2) * 64 + 64,
                               c.dcols(g // 2, qc * TC, (qc + 1) * TC)]
                    ins = eng.tensor_tensor(dst, tmp, BV[0:64, 0:TC], ALU.add)
                    tok["cmb"] = c.inc(ins, "V")

    _emit_norm_fm(c, c.BA, c.BA, 1e-3, subtract_mean=True, gb=True, src_bf16=True)
    for tc in range(NTC):
        @blk.vector
        def _(eng, tc=tc):
            c.wait(eng, "V", "S")
            for j in range(NDT):
                ins = eng.tensor_tensor(c.XR[:, c.dcols(j, tc * TC, (tc + 1) * TC)],
                                        c.XR[:, c.dcols(j, tc * TC, (tc + 1) * TC)],
                                        c.BA[:, c.dcols(j, tc * TC, (tc + 1) * TC)], ALU.add)
            c.inc(ins, "V")
    _emit_norm_fm(c, c.XR, c.XR, 1e-5, subtract_mean=True)


def _emit_l1_ffn(c, dram):
    nc, blk = c.nc, c.block
    P0 = lambda ch: c.SC[:, ch * 16:(ch + 1) * 16]

    @blk.gpsimd
    def _(eng):
        c.wait(eng, "T", "V", "S")
        ins = eng.dma_start(out=c.MR[:, 1280:1408],
                            in_=dram["rw"].rearrange("(kt p) e -> p kt e", p=128))
        c.inc(ins, "G", 16)
        ins = eng.dma_start(out=c.MR[0:1, 1552:1568], in_=dram["rb"].unsqueeze(0))
        c.inc(ins, "G", 16)
        ins = eng.dma_start(out=c.BB[0:16, 2560:4608], in_=dram["OH16"])
        c.inc(ins, "G", 16)
        ins = eng.dma_start(out=c.WB[0:16, 6144:7168], in_=dram["b2"])
        c.inc(ins, "G", 16)

    @blk.tensor
    def _(eng):
        c.wait(eng, "G", "V", "S")
        ins = eng.matmul(c.PS[3][:, 0:16], c.MR[0:1, 1:129], c.MR[0:1, 1552:1568],
                         start=True, stop=True)
        c.inc(ins, "T")

    @blk.scalar
    def _(eng):
        c.wait(eng, "T")
        ins = eng.copy(c.SC[:, 4352:4368], c.PS[3][:, 0:16])
        c.inc(ins, "S")

    for ch in range(16):
        @blk.tensor
        def _(eng, ch=ch):
            c.wait(eng, "G", "V", "S")
            for kt in range(NDT):
                ins = eng.matmul(c.PS[2][:, 0:16],
                                 c.XR[:, c.dcols(kt, ch * 128, (ch + 1) * 128)],
                                 c.MR[:, 1280 + kt * 16:1280 + (kt + 1) * 16],
                                 start=(kt == 0), stop=(kt == NDT - 1))
            c.inc(ins, "T")

        @blk.vector
        def _(eng, ch=ch):
            c.wait(eng, "T", "S")
            pm = c.SC[:, 4368:4384]
            mx = c.SC[:, 4401:4402]
            eng.tensor_tensor(pm, c.PS[2][:, 0:16], c.SC[:, 4352:4368], ALU.add)
            c.pb(eng, eng.tensor_reduce(mx, pm, AX.X, ALU.max))
            ins = eng.tensor_scalar(pm, pm, mx, None, ALU.subtract)
            c.inc(ins, "V")

        @blk.scalar
        def _(eng, ch=ch):
            c.wait(eng, "V")
            ins = eng.activation(c.SC[:, 4368:4384], c.SC[:, 4368:4384], AF.Exp)
            c.inc(ins, "S")

        @blk.vector
        def _(eng, ch=ch):
            c.wait(eng, "S")
            pm = c.SC[:, 4368:4384]
            sm = c.SC[:, 4401:4402]
            c.pb(eng, eng.tensor_reduce(sm, pm, AX.X, ALU.add))
            c.pb(eng, eng.reciprocal(sm, sm))
            c.pb(eng, eng.tensor_scalar(P0(ch), pm, sm, None, ALU.mult))
            indsum = c.SC[:, 4384:4400]
            wsum = c.SC[:, 4400:4401]
            tmu = c.SC[:, 4402:4418]
            mm_ = c.SC[:, 4418:4419]
            eng.tensor_copy(pm, P0(ch))
            eng.memset(indsum, 0.0)
            c.pb(eng, eng.memset(wsum, 0.0))
            for k4 in range(4):
                c.pb(eng, eng.tensor_reduce(mm_, pm, AX.X, ALU.max))
                c.pb(eng, eng.tensor_scalar(tmu, pm, mm_, None, ALU.is_equal))
                c.pb(eng, eng.tensor_tensor(indsum, indsum, tmu, ALU.add))
                c.pb(eng, eng.tensor_tensor(wsum, wsum, mm_, ALU.add))
                c.pb(eng, eng.tensor_tensor(tmu, tmu, pm, ALU.mult))
                c.pb(eng, eng.tensor_tensor(pm, pm, tmu, ALU.subtract))
            c.pb(eng, eng.reciprocal(wsum, wsum))
            c.pb(eng, eng.tensor_tensor(tmu, P0(ch), indsum, ALU.mult))
            ins = eng.tensor_scalar(tmu, tmu, wsum, None, ALU.mult)
            c.inc(ins, "V")

        @blk.tensor
        def _(eng, ch=ch):
            c.wait(eng, "V")
            ins = eng.transpose(c.PS[3][0:16, 0:128], c.SC[:, 4402:4418], c.IDT[:])
            c.inc(ins, "T")

        @blk.vector
        def _(eng, ch=ch):
            c.wait(eng, "T")
            ins = eng.tensor_copy(c.BB[0:16, ch * 128:(ch + 1) * 128],
                                  c.PS[3][0:16, 0:128])
            c.inc(ins, "V")

    wtok = {}

    tok = {"gbt": 0, "gbc": 0, "comb": 0}

    ctr = {"w1": 0, "w2": 0}
    for s_ in range(3):
        tok[("w1u", s_)] = c.cnt["T"]
        tok[("w2u", s_)] = c.cnt["T"]

    def load_w1(e, fc, _slot_unused=None):
        slot = ctr["w1"] % 3
        ctr["w1"] += 1
        wtok[("w1s", e, fc)] = slot
        src = dram["W1_1"][e][:, fc * 128:(fc + 1) * 128]
        src = src.rearrange("(kt p) f -> p kt f", p=128)

        @blk.gpsimd
        def _(eng, slot=slot):
            c.wait_at(eng, "T", tok.get(("w1u", slot), 0))
            ins = eng.dma_start(out=c.WR[:, slot * 1024:(slot + 1) * 1024], in_=src)
            wtok[("w1", e, fc)] = c.inc(ins, "G", 16)

    def load_w2(e, oc, _slot_unused=None):
        slot = ctr["w2"] % 3
        ctr["w2"] += 1
        wtok[("w2s", e, oc)] = slot
        src = dram["W2_1"][e][:, oc * 128:(oc + 1) * 128]
        src = src.rearrange("(fc p) o -> p fc o", p=128)

        @blk.gpsimd
        def _(eng, slot=slot):
            c.wait_at(eng, "T", tok.get(("w2u", slot), 0))
            ins = eng.dma_start(out=c.WB[:, slot * 2048:slot * 2048 + 2048], in_=src)
            wtok[("w2", e, oc)] = c.inc(ins, "G", 16)

    def load_b1(e):
        @blk.sync
        def _(eng):
            c.wait(eng, "S")
            ins = eng.dma_start(out=c.MISC[:, 104:120], in_=dram["b1c"][e])
            wtok[("b1", e)] = c.inc(ins, "Y", 16)

    pf_s = {0: c.cnt["S"], 1: c.cnt["S"]}
    pf_v = {0: c.cnt["V"], 1: c.cnt["V"]}
    tok["gbc"] = c.cnt["S"]
    tok["comb"] = c.cnt["V"]
    for half in range(2):
        t0 = half * 1024
        load_w1(0, 0, 0)
        load_w1(0, 1, 1)
        for e in range(NE1):
            load_b1(e)
            for fc in range(16):
                la, lb = (e, fc + 2) if fc + 2 < 16 else (e + 1, fc + 2 - 16)
                if la < NE1:
                    load_w1(la, lb, (fc + 2) % 3)
                pr = fc % 2
                bk0, bk1 = c.PS[2 * pr], c.PS[2 * pr + 1]

                @blk.tensor
                def _(eng, e=e, fc=fc, pr=pr, bk0=bk0, bk1=bk1, t0=t0):
                    c.wait_at(eng, "G", wtok[("w1", e, fc)])
                    c.wait_at(eng, "S", pf_s[pr])
                    c.wait_at(eng, "V", pf_v[pr])
                    slot = wtok[("w1s", e, fc)]
                    for tq, bank in ((0, bk0), (1, bk1)):
                        for kt in range(NDT):
                            ins = eng.matmul(
                                bank[:, 0:TC],
                                c.WR[:, slot * 1024 + kt * 128:slot * 1024 + (kt + 1) * 128],
                                c.XR[:, c.dcols(kt, t0 + tq * TC, t0 + (tq + 1) * TC)],
                                start=(kt == 0), stop=(kt == NDT - 1))
                    tok[("h1t", pr)] = tok[("w1u", slot)] = c.inc(ins, "T")

                @blk.scalar
                def _(eng, e=e, fc=fc, pr=pr, bk0=bk0):
                    c.wait_at(eng, "T", tok[("h1t", pr)])
                    c.wait_at(eng, "Y", wtok[("b1", e)])
                    ins = eng.activation(c.BA[:, fc * 1024:(fc + 1) * 1024],
                                         c.PSA[:, pr * 1024:(pr + 1) * 1024], AF.Gelu,
                                         bias=c.mcol(104, fc))
                    pf_s[pr] = c.inc(ins, "S")

            load_w2(e, 0, 0)
            load_w2(e, 1, 1)
            for oc in range(NDT):
                la, lb = (e, oc + 2) if oc + 2 < NDT else (e + 1, oc + 2 - NDT)
                if la < NE1:
                    load_w2(la, lb, (oc + 2) % 3)
                pr = oc % 2
                bk0, bk1 = c.PS[2 * pr], c.PS[2 * pr + 1]

                @blk.tensor
                def _(eng, e=e, oc=oc, pr=pr, bk0=bk0, bk1=bk1, half=half):
                    c.wait_at(eng, "G", wtok[("w2", e, oc)])
                    c.wait_at(eng, "S", max(pf_s[0], pf_s[1]))
                    c.wait_at(eng, "V", pf_v[pr])
                    slot = wtok[("w2s", e, oc)]
                    for tq, bank in ((0, bk0), (1, bk1)):
                        for fc in range(16):
                            ins = eng.matmul(
                                bank[:, 0:TC],
                                c.WB[:, slot * 2048 + fc * 128:slot * 2048 + (fc + 1) * 128],
                                c.BA[:, fc * 1024 + tq * TC:fc * 1024 + (tq + 1) * TC],
                                start=(fc == 0), stop=(fc == 15))
                    tok[("h2t", pr)] = tok[("w2u", slot)] = c.inc(ins, "T")
                    c.wait_at(eng, "S", tok["gbc"])
                    eng.matmul(c.PS[6][:, 0:TC],
                               c.BB[0:16, 2560 + e * 128:2560 + (e + 1) * 128],
                               c.BB[0:16, half * 1024:half * 1024 + TC],
                               start=True, stop=True)
                    ins = eng.matmul(c.PS[7][:, 0:TC],
                                     c.BB[0:16, 2560 + e * 128:2560 + (e + 1) * 128],
                                     c.BB[0:16, half * 1024 + TC:half * 1024 + 1024],
                                     start=True, stop=True)
                    tok["gbt"] = c.inc(ins, "T")

                @blk.scalar
                def _(eng):
                    c.wait_at(eng, "T", tok["gbt"])
                    c.wait_at(eng, "V", tok["comb"])
                    eng.copy(c.BB[:, 2048:2560], c.PS[6][:, 0:TC])
                    ins = eng.copy(c.BB[:, 4608:5120], c.PS[7][:, 0:TC])
                    tok["gbc"] = c.inc(ins, "S")

                @blk.vector
                def _(eng, e=e, oc=oc, pr=pr, bk0=bk0, bk1=bk1):
                    c.wait_at(eng, "T", tok[("h2t", pr)])
                    c.wait_at(eng, "S", tok["gbc"])
                    acc = c.SC[:, oc * 1024:(oc + 1) * 1024]
                    for tq, bank, gc in ((0, bk0, 2048), (1, bk1, 4608)):
                        a_ = acc[:, tq * TC:(tq + 1) * TC]
                        gb = c.BB[:, gc:gc + 512]
                        if e == 0:
                            ins = eng.tensor_tensor(a_, bank[:, 0:TC], gb, ALU.mult)
                        else:
                            tmp = c.BB[:, 5120:5632]
                            eng.tensor_tensor(tmp, bank[:, 0:TC], gb, ALU.mult)
                            ins = eng.tensor_tensor(a_, a_, tmp, ALU.add)
                    pf_v[pr] = tok["comb"] = c.inc(ins, "V")

        for oc in range(NDT):
            for tq in range(2):
                @blk.tensor
                def _(eng, oc=oc, tq=tq, half=half):
                    c.wait(eng, "V", "S")
                    ins = eng.matmul(c.PS[5][:, 0:TC],
                                     c.WB[0:16, 6144 + oc * 128:6144 + (oc + 1) * 128],
                                     c.BB[0:16, half * 1024 + tq * TC:half * 1024 + (tq + 1) * TC],
                                     start=True, stop=True)
                    c.inc(ins, "T")

                @blk.vector
                def _(eng, oc=oc, tq=tq):
                    c.wait(eng, "T")
                    acc = c.SC[:, oc * 1024 + tq * TC:oc * 1024 + (tq + 1) * TC]
                    ins = eng.tensor_tensor(acc, acc, c.PS[5][:, 0:TC], ALU.add)
                    c.inc(ins, "V")

        for j in range(NDT):
            @blk.vector
            def _(eng, j=j, t0=t0):
                c.wait(eng, "V", "S")
                ins = eng.tensor_tensor(
                    c.XR[:, c.dcols(j, t0, t0 + 1024)],
                    c.XR[:, c.dcols(j, t0, t0 + 1024)],
                    c.SC[:, j * 1024:(j + 1) * 1024], ALU.add)
                c.inc(ins, "V")

    _emit_norm_fm(c, c.XR, c.XR, 1e-5, subtract_mean=True)






def _emit_barrier(c):
    blk = c.block

    @blk.tensor
    def _(eng):
        c.wait(eng, "V", "S", "G", "Y")
        ins = eng.nop()
        c.inc(ins, "T")

    @blk.vector
    def _(eng):
        c.wait(eng, "T", "S", "G", "Y")
        ins = eng.engine_nop()
        c.inc(ins, "V")

    @blk.scalar
    def _(eng):
        c.wait(eng, "T", "V", "G", "Y")
        ins = eng.nop()
        c.inc(ins, "S")

    @blk.gpsimd
    def _(eng):
        c.wait(eng, "T", "V", "S")

    @blk.sync
    def _(eng):
        c.wait(eng, "T", "V", "S")


def _emit_prologue(c, dram):
    blk = c.block

    @blk.sync
    def _(eng):
        ins = eng.dma_start(out=c.IDT[:], in_=dram["ident"])
        c.inc(ins, "Y", 16)

    @blk.gpsimd
    def _(eng):
        ins = eng.dma_start(out=c.MR[:, 0:1], in_=dram["onecol"])
        c.inc(ins, "G", 16)
        ins = eng.dma_start(out=c.MR[0:1, 1:129], in_=dram["onerow"])
        c.inc(ins, "G", 16)

    @blk.vector
    def _(eng):
        eng.memset(c.ONES[:], 1.0)
        eng.memset(c.OB[:], 1.0)
        eng.memset(c.MISC[:, 120:121], -80.0)
        eng.memset(c.MISC[:, 121:122], 1e-6)
        eng.memset(c.MISC[:, 122:123], 1e-5)
        ins = eng.memset(c.MISC[:, 123:124], 1e-3)
        c.inc(ins, "V")

    if c.phases[0] == "l0a":
        for tt in range(T // 128):
            @blk.sync
            def _(eng, tt=tt):
                c.wait(eng, "T")
                ins = eng.dma_start(out=c.SC[:, 0:1024],
                                    in_=dram["x0"][tt * 128:(tt + 1) * 128, :])
                c.inc(ins, "Y", 16)

            vbase = c.cnt["V"]

            @blk.tensor
            def _(eng, tt=tt, vbase=vbase):
                c.wait(eng, "Y")
                for j in range(NDT):
                    if j == 0:
                        c.wait_at(eng, "V", vbase)
                    if j >= 2:
                        c.wait_at(eng, "V", vbase + j - 1)
                    ins = eng.transpose(c.PS[4 + j % 2][:, 0:128],
                                        c.SC[:, j * 128:(j + 1) * 128], c.IDT[:])
                    c.inc(ins, "T")

            @blk.vector
            def _(eng, tt=tt):
                tbase = c.cnt["T"] - NDT
                for j in range(NDT):
                    c.wait_at(eng, "T", tbase + j + 1)
                    ins = eng.tensor_copy(c.XR[:, c.dcols(j, tt * 128, tt * 128 + 128)],
                                          c.PS[4 + j % 2][:, 0:128])
                    c.inc(ins, "V")
    else:
        @blk.gpsimd
        def _(eng):
            ins = eng.dma_start(out=c.XR[:], in_=dram["xin"])
            c.inc(ins, "G", 16)


def _emit_epilogue(c, dram):
    nc, blk = c.nc, c.block
    if c.phases[-1] == "l1f":
        OTM = c.SC[:, 0:1024]
        for tt in range(T // 128):
            vbase = c.cnt["V"]
            otok = c.cnt["O"]

            @blk.tensor
            def _(eng, tt=tt, vbase=vbase):
                c.wait(eng, "G", "S")
                for j in range(NDT):
                    if j == 0:
                        c.wait_at(eng, "V", vbase)
                    if j >= 2:
                        c.wait_at(eng, "V", vbase + j - 1)
                    ins = eng.transpose(c.PS[4 + j % 2][:, 0:128],
                                        c.XR[:, c.dcols(j, tt * 128, tt * 128 + 128)].bitcast(F32),
                                        c.IDT[:])
                    c.inc(ins, "T")

            @blk.vector
            def _(eng, tt=tt, otok=otok):
                tbase = c.cnt["T"] - NDT
                c.wait_at(eng, "O", otok)
                for j in range(NDT):
                    c.wait_at(eng, "T", tbase + j + 1)
                    ins = eng.tensor_copy(OTM[:, j * 128:(j + 1) * 128],
                                          c.PS[4 + j % 2][:, 0:128])
                    c.inc(ins, "V")

            @blk.sync
            def _(eng, tt=tt):
                c.wait(eng, "V")
                ins = eng.dma_start(out=dram["y"][tt * 128:(tt + 1) * 128, :], in_=OTM)
                c.inc(ins, "O", 16)
    else:
        @blk.sync
        def _(eng):
            c.wait(eng, "V", "S", "T", "G")
            ins = eng.dma_start(out=dram["xout"], in_=c.XR[:].bitcast(F32))
            c.inc(ins, "O", 16)

    @blk.sync
    def _(eng):
        c.wait_at(eng, "O", c.cnt["O"])



def build_program(phases):
    nc = bass.Bass("TRN2", target_bir_lowering=False, debug=False)
    dram = {}

    def din(name, shape, dt=F32):
        dram[name] = nc.dram_tensor(name, list(shape), dt, kind="ExternalInput").ap()

    def dout(name, shape, dt=F32):
        dram[name] = nc.dram_tensor(name, list(shape), dt, kind="ExternalOutput").ap()

    if phases[0] == "l0a":
        din("x0", (T, D))
    else:
        din("xin", (128, NDT * T))
    if phases[-1] == "l1f":
        dout("y", (T, D))
    else:
        dout("xout", (128, NDT * T))
    if "l0a" in phases:
        din("wrT", (NL, D, D)); din("wiT", (NL, D, D)); din("woT", (NL, D, D))
        din("cwc", (NL, 128, 32)); din("cb", (NL, 128, 8)); din("wrb", (NL, 128, 8))
        din("wib", (NL, 128, 8)); din("la", (NL, 128, 8)); din("nw", (NL, 128, 8))
        din("now", (128, 8))
    if "l0f" in phases:
        din("Wg", (D, NE0)); din("W1_0", (NE0, D, F0)); din("W2_0", (NE0, F0, D))
        din("G64", (64, 64)); din("OH8", (NE0, NE0 * 128))
        if _DBG["on"]:
            dout("dbg_pr", (128, 128)); dout("dbg_th", (64, 1))
            dout("dbg_gate", (8, 2048)); dout("dbg_acc", (128, 8192))
    if "l0f" in phases:
        dram["cc_in"] = nc.dram_tensor("cc_in", [NE0, T], F32, kind="Internal").ap()
        dram["cc_out"] = nc.dram_tensor("cc_out", [NE0 * B, T], F32, kind="Internal").ap()
    if "l1a" in phases and _DBG["on"]:
        dout("dbg_attn", (128, NDT * T))
    if "l1a" in phases:
        din("Wq", (D, D)); din("Wk", (D, D)); din("Wv", (D, D))
        din("gamma", (128, 8)); din("beta", (128, 8)); din("biasT", (NH, T, T))
    if "l1f" in phases:
        din("rw", (D, NE1)); din("rb", (NE1,))
        din("W1_1", (NE1, D, F1)); din("b1c", (NE1, 128, 16))
        din("W2_1", (NE1, F1, D)); din("b2", (NE1, D))
        din("OH16", (NE1, NE1 * 128))
    din("ident", (128, 128)); din("onecol", (128, 1)); din("onerow", (1, 128))

    es = contextlib.ExitStack()
    with es:
        c = Ctx(nc, es, phases)
        _emit_prologue(c, dram)
        for ph in phases:
            _emit_barrier(c)
            {"l0a": _emit_l0_attn, "l0f": _emit_l0_ffn,
             "l1a": _emit_l1_attn, "l1f": _emit_l1_ffn}[ph](c, dram)
        _emit_barrier(c)
        _emit_epilogue(c, dram)
    return nc


_INPUT_KEYS = {
    "l0a": ["wrT", "wiT", "woT", "cwc", "cb", "wrb", "wib", "la", "nw", "now"],
    "l0f": ["Wg", "W1_0", "W2_0", "G64", "OH8"],
    "l1a": ["Wq", "Wk", "Wv", "gamma", "beta", "biasT"],
    "l1f": ["rw", "rb", "W1_1", "b1c", "W2_1", "b2", "OH16"],
}


def kernel(**inputs):
    host = _prep_host(inputs)
    x0 = np.ascontiguousarray(np.asarray(inputs["L0_attn_x"], dtype=np.float32))
    cur = [np.ascontiguousarray(x0[b]) for b in range(B)]
    for phases in PHASE_GROUPS:
        nc = build_program(phases)
        shared = {"ident": host["ident"], "onecol": host["onecol"],
                  "onerow": host["onerow"]}
        for ph in phases:
            for k in _INPUT_KEYS[ph]:
                shared[k] = host[k]
        in_maps = []
        for b in range(B):
            m = dict(shared)
            if phases[0] == "l0a":
                m["x0"] = cur[b]
            else:
                m["xin"] = cur[b]
            in_maps.append(m)
        res = run_bass_kernel_spmd(nc, in_maps, core_ids=list(range(B)))
        if phases[-1] == "l1f":
            return np.stack([res.results[b]["y"] for b in range(B)], axis=0)
        cur = [np.ascontiguousarray(res.results[b]["xout"]) for b in range(B)]
    raise RuntimeError("phase groups must end with l1f")


# version 08
# speedup vs baseline: 2.4400x; 1.0755x over previous
import contextlib
import numpy as np
import concourse.bass as bass
import concourse.mybir as mybir
from concourse.bass_utils import run_bass_kernel_spmd

F32 = mybir.dt.float32
F32R = mybir.dt.float32r
BF16 = mybir.dt.bfloat16
AF = mybir.ActivationFunctionType
ALU = mybir.AluOpType
AX = mybir.AxisListType

B, T, D = 8, 2048, 1024
NDT = D // 128
TC = 512
NTC = T // TC
NL = 3
KCONV = 4
NE0, F0 = 8, 2048
NE1, F1 = 16, 2048
NH, DH = 16, 64

PHASE_GROUPS = [["l0a", "l0f", "l1a", "l1f"]]
_DBG = {"on": False}

M_CW = 0
M_CB = 32
M_WRB = 40
M_WIB = 48
M_LSA = 56
M_LA = 64
M_NW = 72
M_NOW = 80
M_GAM = 88
M_BET = 96
M_B2 = 104


class Ctx:
    def __init__(self, nc, es, phases):
        self.nc = nc
        self.es = es
        self.phases = phases
        sb = lambda name, shape, dt: es.enter_context(nc.sbuf_tensor(name, shape, dt))
        self.XR = sb("XR", [128, NDT * T], F32R)
        self.BA = sb("BA", [128, NDT * T], BF16)
        self.BB = sb("BB", [128, NDT * T], BF16)
        self.SC = sb("SC", [128, 4 * T], F32)
        self.WB = sb("WB", [128, 8192], BF16)
        self.WR = sb("WR", [128, 3072], F32R)
        self.ONES = sb("ONES", [128, T], F32)
        self.MISC = sb("MISC", [128, 1024], F32)
        self.MR = sb("MR", [128, 1664], F32R)
        self.OB = sb("OB", [128, 2], BF16)
        self.IDT = sb("IDT", [128, 128], F32)
        self.PSA = es.enter_context(nc.psum_tensor("psA", [128, 2048], F32))
        self.PSB = es.enter_context(nc.psum_tensor("psB", [128, 2048], F32))
        self.PS = [self.PSA[:, i * 512:(i + 1) * 512] for i in range(4)] + \
                  [self.PSB[:, i * 512:(i + 1) * 512] for i in range(4)]
        self.sem = {}
        self.cnt = {}
        for name in ["T", "V", "S", "G", "Y", "O", "C"]:
            self.sem[name] = es.enter_context(nc.semaphore("sem" + name))
            self.cnt[name] = 0
        self.block = es.enter_context(nc.Block())

    def inc(self, ins, name, n=1):
        self.cnt[name] += n
        ins.then_inc(self.sem[name], n)
        return self.cnt[name]

    def wait(self, eng, *names):
        for n in names:
            if self.cnt[n] > 0:
                eng.wait_ge(self.sem[n], self.cnt[n])

    def wait_at(self, eng, name, val):
        if val > 0:
            eng.wait_ge(self.sem[name], val)

    def pb(self, eng, ins):
        self.inc(ins, "V")
        eng.wait_ge(self.sem["V"], self.cnt["V"])

    @staticmethod
    def dcols(j, c0=0, c1=T):
        return slice(j * T + c0, j * T + c1)

    def t1(self, c0=0, c1=T): return self.SC[:, 0 * T + c0:0 * T + c1]
    def t2(self, c0=0, c1=T): return self.SC[:, 1 * T + c0:1 * T + c1]
    def t3(self, c0=0, c1=T): return self.SC[:, 2 * T + c0:2 * T + c1]
    def t4(self, c0=0, c1=T): return self.SC[:, 3 * T + c0:3 * T + c1]

    def mcol(self, base, j):
        return self.MISC[:, base + j:base + j + 1]



def _col8(v):
    v = np.asarray(v)
    return np.ascontiguousarray(v.reshape(-1, 128).T)


def _prep_host(inp):
    d = {}
    d["wrT"] = np.ascontiguousarray(np.asarray(inp["L0_attn_wr_w"]).transpose(0, 2, 1))
    d["wiT"] = np.ascontiguousarray(np.asarray(inp["L0_attn_wi_w"]).transpose(0, 2, 1))
    d["woT"] = np.ascontiguousarray(np.asarray(inp["L0_attn_wo_w"]).transpose(0, 2, 1))
    cw = np.asarray(inp["L0_attn_conv_w"])
    d["cwc"] = np.ascontiguousarray(
        np.stack([_col8(cw[l].T.reshape(-1)) for l in range(NL)]))
    for k_src, k_dst in [("L0_attn_conv_b", "cb"), ("L0_attn_wr_b", "wrb"),
                         ("L0_attn_wi_b", "wib"), ("L0_attn_log_a", "la"),
                         ("L0_attn_norm_w", "nw")]:
        v = np.asarray(inp[k_src])
        d[k_dst] = np.ascontiguousarray(np.stack([_col8(v[l]) for l in range(NL)]))
    d["now"] = _col8(inp["L0_attn_norm_out_w"])
    d["Wg"] = np.ascontiguousarray(np.asarray(inp["L0_ffn_Wg"]))
    d["W1_0"] = np.ascontiguousarray(np.asarray(inp["L0_ffn_W1"]))
    d["W2_0"] = np.ascontiguousarray(np.asarray(inp["L0_ffn_W2"]))
    d["Wq"] = np.ascontiguousarray(np.asarray(inp["L1_attn_Wq"]))
    d["Wk"] = np.ascontiguousarray(np.asarray(inp["L1_attn_Wk"]))
    d["Wv"] = np.ascontiguousarray(np.asarray(inp["L1_attn_Wv"]))
    d["gamma"] = _col8(inp["L1_attn_gamma"])
    d["beta"] = _col8(inp["L1_attn_beta"])
    rt = np.asarray(inp["L1_attn_rel_table"])
    r = np.arange(T)
    rel_index = np.clip(r[None, :] - r[:, None] + (T - 1), 0, 2 * T - 2)
    rel_index = np.minimum(rel_index, rt.shape[0] - 1)
    bias0 = rt[rel_index]
    biasg = bias0.reshape(-1).reshape(NH, T, T)
    d["biasT"] = np.ascontiguousarray(biasg.transpose(0, 2, 1))
    d["rw"] = np.ascontiguousarray(np.asarray(inp["L1_ffn_router_w"]))
    d["rb"] = np.ascontiguousarray(np.asarray(inp["L1_ffn_router_b"]))
    d["W1_1"] = np.ascontiguousarray(np.asarray(inp["L1_ffn_w1"]))
    b1 = np.asarray(inp["L1_ffn_b1"])
    d["b1c"] = np.ascontiguousarray(np.stack([_col8(b1[e]) for e in range(NE1)]))
    d["W2_1"] = np.ascontiguousarray(np.asarray(inp["L1_ffn_w2"]))
    d["b2"] = np.ascontiguousarray(np.asarray(inp["L1_ffn_b2"]))
    d["ident"] = np.eye(128, dtype=np.float32)
    d["onecol"] = np.ones((128, 1), np.float32)
    d["onerow"] = np.ones((1, 128), np.float32)
    i8 = np.arange(64) % NE0
    d["G64"] = (i8[:, None] == i8[None, :]).astype(np.float32)
    oh8 = np.zeros((NE0, NE0 * 128), np.float32)
    for e in range(NE0):
        oh8[e, e * 128:(e + 1) * 128] = 1.0
    d["OH8"] = oh8
    oh16 = np.zeros((NE1, NE1 * 128), np.float32)
    for e in range(NE1):
        oh16[e, e * 128:(e + 1) * 128] = 1.0
    d["OH16"] = oh16
    return d



def _emit_norm_fm(c, buf, out_buf, eps, subtract_mean=True, scale_col_of=None,
                  gb=False, src_bf16=False):
    eps_col = c.MISC[:, {1e-6: 121, 1e-5: 122, 1e-3: 123}[eps]:
                        {1e-6: 121, 1e-5: 122, 1e-3: 123}[eps] + 1]
    """out = (buf - mu)/sqrt(var+eps) over the 1024-feature (partition) dim.
    scale_col_of(j): per-feature multiplier fused into the apply (rms nw).
    gb: apply gamma/beta columns from MISC. Processes per 512-token chunk.
    Row scratch: SC[0:1, 4096+q*512] for q in (sx, sq, mu, var).
    """
    nc, blk = c.nc, c.block
    onec = c.OB[:, 0:1] if src_bf16 else c.MR[:, 0:1]
    oner = c.MR[0:1, 1:129]
    inv_n = 1.0 / D
    PS_X, PS_Q = c.PS[6], c.PS[7]
    R_SX = c.SC[0:1, 4096:4608]
    R_MU = c.SC[0:1, 5120:5632]
    R_VR = c.SC[0:1, 5632:6144]
    R_RSTD = c.MR[0:1, 256:768]
    R_ROWB = c.MR[0:1, 768:1280]
    SQV = c.SC.bitcast(BF16)
    for tc in range(NTC):
        @blk.scalar
        def _(eng, tc=tc):
            c.wait(eng, "V", "T")
            for j in range(NDT):
                ins = eng.activation(
                    SQV[:, j * TC:(j + 1) * TC],
                    buf[:, c.dcols(j, tc * TC, (tc + 1) * TC)], AF.Square)
            c.inc(ins, "S")

        @blk.tensor
        def _(eng, tc=tc):
            c.wait(eng, "S", "V")
            if subtract_mean:
                for j in range(NDT):
                    eng.matmul(PS_X[0:1, 0:TC], onec,
                               buf[:, c.dcols(j, tc * TC, (tc + 1) * TC)],
                               start=(j == 0), stop=(j == NDT - 1))
            for j in range(NDT):
                ins = eng.matmul(PS_Q[0:1, 0:TC], c.OB[:, 0:1],
                                 SQV[:, j * TC:(j + 1) * TC],
                                 start=(j == 0), stop=(j == NDT - 1))
            c.inc(ins, "T")

        @blk.vector
        def _(eng, tc=tc):
            c.wait(eng, "T", "S")
            if subtract_mean:
                eng.tensor_scalar(R_MU, PS_X[0:1, 0:TC], inv_n, None, ALU.mult)
                eng.tensor_scalar(R_VR, PS_Q[0:1, 0:TC], inv_n, None, ALU.mult)
                eng.tensor_tensor(R_SX, R_MU, R_MU, ALU.mult)
                ins = eng.tensor_tensor(R_VR, R_VR, R_SX, ALU.subtract)
            else:
                ins = eng.tensor_scalar(R_VR, PS_Q[0:1, 0:TC], inv_n, None, ALU.mult)
            c.inc(ins, "V")

        @blk.scalar
        def _(eng, tc=tc):
            c.wait(eng, "V")
            eng.activation(R_VR, R_VR, AF.Ln, bias=eps_col[0:1, 0:1])
            ins = eng.activation(R_RSTD, R_VR, AF.Exp, scale=-0.5)
            c.inc(ins, "S")

        if subtract_mean:
            @blk.vector
            def _(eng, tc=tc):
                c.wait(eng, "S")
                eng.tensor_tensor(R_MU, R_MU, R_RSTD, ALU.mult)
                ins = eng.tensor_scalar(R_ROWB, R_MU, -1.0, None, ALU.mult)
                c.inc(ins, "V")

        @blk.tensor
        def _(eng, tc=tc):
            c.wait(eng, "V", "S")
            ins = eng.matmul(PS_X[:, 0:TC], oner, R_RSTD,
                             start=True, stop=True)
            if subtract_mean:
                ins = eng.matmul(PS_Q[:, 0:TC], oner, R_ROWB,
                                 start=True, stop=True)
            c.inc(ins, "T")

        @blk.scalar
        def _(eng, tc=tc):
            c.wait(eng, "T", "V")
            ins = eng.activation(c.SC[:, 3584:4096], PS_X[:, 0:TC], AF.Identity)
            c.inc(ins, "S")

        @blk.vector
        def _(eng, tc=tc):
            c.wait(eng, "T", "S")
            rbc = c.SC[:, 3584:4096]
            tmp = c.SC[:, 3072:3584]
            for j in range(NDT):
                src = buf[:, c.dcols(j, tc * TC, (tc + 1) * TC)]
                dst = out_buf[:, c.dcols(j, tc * TC, (tc + 1) * TC)]
                if scale_col_of is not None:
                    ins = eng.scalar_tensor_tensor(dst, src, scale_col_of(j), rbc,
                                                   ALU.mult, ALU.mult)
                elif subtract_mean:
                    eng.tensor_tensor(tmp, src, rbc, ALU.mult)
                    if gb:
                        eng.tensor_tensor(tmp, tmp, PS_Q[:, 0:TC], ALU.add)
                        ins = eng.tensor_scalar(dst, tmp, c.mcol(M_GAM, j),
                                                c.mcol(M_BET, j), ALU.mult, ALU.add)
                    else:
                        ins = eng.tensor_tensor(dst, tmp, PS_Q[:, 0:TC], ALU.add)
                else:
                    ins = eng.tensor_tensor(dst, src, rbc, ALU.mult)
            c.inc(ins, "V")



def _emit_l0_attn(c, dram):
    nc, blk = c.nc, c.block
    xc_buf = [c.BA, c.BB, c.BA]
    h_buf = [c.BB, c.BA, c.BB]
    DGB = c.WB[:, 7168:7680]

    def load_cols(dst_c0, dst_c1, src):
        @blk.sync
        def _(eng):
            c.wait(eng, "V", "S", "T")
            ins = eng.dma_start(out=c.MISC[:, dst_c0:dst_c1], in_=src)
            c.inc(ins, "Y", 16)

    ltok = {}
    for l in range(NL):
        XC = xc_buf[l]
        H = h_buf[l]
        if l == 0:
            xin = h_buf[0]
            for tc in range(NTC):
                @blk.vector
                def _(eng, tc=tc):
                    c.wait(eng, "G", "S", "T")
                    for j in range(NDT):
                        ins = eng.tensor_copy(
                            xin[:, c.dcols(j, tc * TC, (tc + 1) * TC)],
                            c.XR[:, c.dcols(j, tc * TC, (tc + 1) * TC)])
                    c.inc(ins, "V")
        else:
            xin = xc_buf[l - 1]
        load_cols(M_CW, M_CW + 32, dram["cwc"][l])
        load_cols(M_CB, M_CB + 8, dram["cb"][l])
        load_cols(M_WRB, M_WRB + 8, dram["wrb"][l])
        load_cols(M_WIB, M_WIB + 8, dram["wib"][l])
        load_cols(M_LA, M_LA + 8, dram["la"][l])
        load_cols(M_NW, M_NW + 8, dram["nw"][l])

        @blk.scalar
        def _(eng, l=l):
            c.wait(eng, "Y", "V")
            eng.activation(c.MISC[:, M_LSA:M_LSA + 8], c.MISC[:, M_LA:M_LA + 8],
                           AF.Sigmoid)
            ins = eng.activation(c.MISC[:, M_LSA:M_LSA + 8], c.MISC[:, M_LSA:M_LSA + 8],
                                 AF.Ln)
            c.inc(ins, "S")

        for j in range(NDT):
            @blk.vector
            def _(eng, l=l, j=j):
                c.wait(eng, "Y")
                c.wait_at(eng, "T", ltok.get("convt", 0))
                dg = DGB
                for k in range(KCONV):
                    ins = eng.tensor_scalar(dg[:, k * 128:(k + 1) * 128], c.IDT[:],
                                            c.mcol(M_CW, k * NDT + j), None, ALU.mult)
                ltok["diag"] = c.inc(ins, "V")

            @blk.tensor
            def _(eng, l=l, j=j, xin=xin):
                c.wait_at(eng, "V", ltok["diag"])
                c.wait_at(eng, "S", ltok.get("convs", 0))
                c.wait_at(eng, "S", ltok.get("wos", 0))
                if l == 0 and j == 0:
                    c.wait(eng, "V", "G")
                dg = DGB
                for tc in range(NTC):
                    bank = c.PS[tc]
                    for k in (3, 2, 1, 0):
                        off = KCONV - 1 - k
                        lo = tc * TC - off
                        if lo >= 0:
                            ins = eng.matmul(bank[:, 0:TC], dg[:, k * 128:(k + 1) * 128],
                                             xin[:, c.dcols(j, lo, lo + TC)],
                                             start=(k == 3), stop=(k == 0))
                        else:
                            ins = eng.matmul(bank[:, off:TC], dg[:, k * 128:(k + 1) * 128],
                                             xin[:, c.dcols(j, 0, TC - off)],
                                             start=(k == 3), stop=(k == 0))
                ltok["convt"] = c.inc(ins, "T")

            @blk.scalar
            def _(eng, j=j, XC=XC):
                c.wait_at(eng, "T", ltok["convt"])
                ins = eng.activation(XC[:, c.dcols(j)], c.PSA[:, 0:2048],
                                     AF.Identity, bias=c.mcol(M_CB, j))
                ltok["convs"] = c.inc(ins, "S")

        wtok = {}

        def load_wblock(kind, l_, jj, slot):
            src = dram[kind][l_][:, jj * 128:(jj + 1) * 128]
            src = src.rearrange("(kt p) o -> p kt o", p=128)

            @blk.gpsimd
            def _(eng):
                c.wait(eng, "T")
                ins = eng.dma_start(out=c.WB[:, slot * 1024:(slot + 1) * 1024], in_=src)
                wtok[(kind, jj)] = c.inc(ins, "G", 16)

        load_wblock("wrT", l, 0, 0)
        load_wblock("wiT", l, 0, 2)

        for j in range(NDT):
            if j + 1 < NDT:
                load_wblock("wrT", l, j + 1, (j + 1) % 2)
                load_wblock("wiT", l, j + 1, 2 + (j + 1) % 2)
            for kind, slot_base, bank_base, bias_c, dst, tk in (
                    ("wrT", 0, 4, M_WRB, c.t1, "r"), ("wiT", 2, 0, M_WIB, c.t4, "i")):
                slot = slot_base + j % 2

                @blk.tensor
                def _(eng, j=j, slot=slot, kind=kind, bank_base=bank_base, XC=XC, tk=tk):
                    c.wait_at(eng, "G", wtok[(kind, j)])
                    c.wait_at(eng, "S", ltok.get(tk + "s", 0))
                    if j == 0:
                        c.wait(eng, "S", "V")
                    for tc in range(NTC):
                        bank = c.PS[bank_base + tc]
                        for kt in range(NDT):
                            ins = eng.matmul(
                                bank[:, 0:TC],
                                c.WB[:, slot * 1024 + kt * 128:slot * 1024 + (kt + 1) * 128],
                                XC[:, c.dcols(kt, tc * TC, (tc + 1) * TC)],
                                start=(kt == 0), stop=(kt == NDT - 1))
                    ltok[tk + "t"] = c.inc(ins, "T")

                @blk.scalar
                def _(eng, j=j, bias_c=bias_c, dst=dst, bank_base=bank_base, tk=tk):
                    c.wait_at(eng, "T", ltok[tk + "t"])
                    c.wait_at(eng, "V", ltok.get("chainv", 0))
                    psrc = c.PSB if bank_base == 4 else c.PSA
                    ins = eng.activation(dst(), psrc[:, 0:2048],
                                         AF.Sigmoid, bias=c.mcol(bias_c, j))
                    ltok[tk + "s"] = c.inc(ins, "S")

            @blk.vector
            def _(eng, j=j):
                c.wait(eng, "S")
                eng.tensor_scalar(c.t2(), c.t1(), c.mcol(M_LSA, j), 8.0,
                                  ALU.mult, ALU.mult)
                eng.tensor_tensor_scan(c.t1(), c.ONES[:], c.t2(), 0.0, ALU.mult, ALU.add)
                ins = eng.tensor_scalar(c.t1(), c.t1(), c.MISC[:, 120:121], None, ALU.max)
                c.inc(ins, "V")

            @blk.scalar
            def _(eng, j=j):
                c.wait(eng, "V")
                eng.activation(c.t3(), c.t1(), AF.Exp)
                eng.activation(c.t1(), c.t1(), AF.Exp, scale=-1.0)
                eng.activation(c.t2(), c.t2(), AF.Exp)
                ins = eng.activation(c.t2(), c.t2(), AF.Square)
                c.inc(ins, "S")

            @blk.vector
            def _(eng, j=j):
                c.wait(eng, "S")
                eng.tensor_scalar(c.t2(), c.t2(), -1.0, 1.0, ALU.mult, ALU.add)
                ins = eng.tensor_scalar(c.t2(), c.t2(), c.MISC[:, 121:122], None, ALU.max)
                c.inc(ins, "V")

            @blk.scalar
            def _(eng, j=j):
                c.wait(eng, "V")
                eng.activation(c.t2(), c.t2(), AF.Ln)
                ins = eng.activation(c.t2(), c.t2(), AF.Exp, scale=0.5)
                c.inc(ins, "S")

            @blk.vector
            def _(eng, j=j, XC=XC, H=H):
                c.wait(eng, "S")
                eng.tensor_tensor(c.t4(), c.t4(), XC[:, c.dcols(j)], ALU.mult)
                eng.tensor_tensor(c.t4(), c.t4(), c.t2(), ALU.mult)
                eng.tensor_tensor(c.t4(), c.t4(), c.t1(), ALU.mult)
                eng.tensor_tensor_scan(c.t2(), c.ONES[:], c.t4(), 0.0, ALU.mult, ALU.add)
                ins = eng.tensor_tensor(H[:, c.dcols(j)], c.t3(), c.t2(), ALU.mult)
                ltok["chainv"] = c.inc(ins, "V")

        load_wblock("woT", l, 0, 4)
        for j in range(NDT):
            if j + 1 < NDT:
                load_wblock("woT", l, j + 1, 4 + (j + 1) % 2)
            slot = 4 + j % 2

            @blk.tensor
            def _(eng, j=j, slot=slot, H=H):
                c.wait_at(eng, "G", wtok[("woT", j)])
                c.wait_at(eng, "S", ltok.get("wos", 0))
                if j == 0:
                    c.wait(eng, "V", "S")
                for tc in range(NTC):
                    bank = c.PS[tc]
                    for kt in range(NDT):
                        ins = eng.matmul(
                            bank[:, 0:TC],
                            c.WB[:, slot * 1024 + kt * 128:slot * 1024 + (kt + 1) * 128],
                            H[:, c.dcols(kt, tc * TC, (tc + 1) * TC)],
                            start=(kt == 0), stop=(kt == NDT - 1))
                ltok["wot"] = c.inc(ins, "T")

            @blk.scalar
            def _(eng, j=j, XC=XC):
                c.wait_at(eng, "T", ltok["wot"])
                ins = eng.activation(XC[:, c.dcols(j)], c.PSA[:, 0:2048], AF.Identity)
                ltok["wos"] = c.inc(ins, "S")

        _emit_norm_fm(c, XC, XC, 1e-6, subtract_mean=False,
                      scale_col_of=lambda j: c.mcol(M_NW, j), src_bf16=True)

    load_cols(M_NOW, M_NOW + 8, dram["now"])
    yb = xc_buf[NL - 1]
    _emit_norm_fm(c, yb, yb, 1e-6, subtract_mean=False,
                  scale_col_of=lambda j: c.mcol(M_NOW, j), src_bf16=True)
    for tc in range(NTC):
        @blk.vector
        def _(eng, tc=tc):
            c.wait(eng, "V", "S", "G")
            for j in range(NDT):
                ins = eng.tensor_tensor(c.XR[:, c.dcols(j, tc * TC, (tc + 1) * TC)],
                                        c.XR[:, c.dcols(j, tc * TC, (tc + 1) * TC)],
                                        yb[:, c.dcols(j, tc * TC, (tc + 1) * TC)], ALU.add)
            c.inc(ins, "V")
    _emit_norm_fm(c, c.XR, c.XR, 1e-5, subtract_mean=True)


def _emit_l0_ffn(c, dram):
    nc, blk = c.nc, c.block
    BBF = c.BB.bitcast(F32)
    PR = lambda ch: c.SC[:, ch * 8:(ch + 1) * 8]
    PALL = c.SC[0:64, 128:2176]
    IND = c.SC[0:64, 2176:4224]
    TH = c.SC[0:64, 4224:4225]
    PLOC = c.SC[0:8, 5120:7168]

    @blk.gpsimd
    def _(eng):
        c.wait(eng, "T", "V", "S")
        ins = eng.dma_start(out=c.MR[:, 1472:1536],
                            in_=dram["Wg"].rearrange("(kt p) e -> p kt e", p=128))
        c.inc(ins, "G", 16)

        ins = eng.dma_start(out=c.BB[0:8, 2560:3584], in_=dram["OH8"])
        c.inc(ins, "G", 16)

    for ch in range(16):
        @blk.tensor
        def _(eng, ch=ch):
            c.wait(eng, "G", "V", "S")
            for kt in range(NDT):
                ins = eng.matmul(c.PS[2][:, 0:8],
                                 c.XR[:, c.dcols(kt, ch * 128, (ch + 1) * 128)],
                                 c.MR[:, 1472 + kt * 8:1472 + (kt + 1) * 8],
                                 start=(kt == 0), stop=(kt == NDT - 1))
            c.inc(ins, "T")

        @blk.vector
        def _(eng, ch=ch):
            c.wait(eng, "T", "S")
            mx = c.SC[:, 4288:4289]
            c.pb(eng, eng.tensor_reduce(mx, c.PS[2][:, 0:8], AX.X, ALU.max))
            ins = eng.tensor_scalar(c.SC[:, 4272:4280], c.PS[2][:, 0:8], mx, None,
                                    ALU.subtract)
            c.inc(ins, "V")

        @blk.scalar
        def _(eng, ch=ch):
            c.wait(eng, "V")
            ins = eng.activation(c.SC[:, 4272:4280], c.SC[:, 4272:4280], AF.Exp)
            c.inc(ins, "S")

        @blk.vector
        def _(eng, ch=ch):
            c.wait(eng, "S")
            sm = c.SC[:, 4289:4290]
            c.pb(eng, eng.tensor_reduce(sm, c.SC[:, 4272:4280], AX.X, ALU.add))
            c.pb(eng, eng.reciprocal(sm, sm))
            ins = eng.tensor_scalar(PR(ch), c.SC[:, 4272:4280], sm, None, ALU.mult)
            c.inc(ins, "V")

        @blk.tensor
        def _(eng, ch=ch):
            c.wait(eng, "V")
            ins = eng.transpose(c.PS[3][0:8, 0:128], PR(ch), c.IDT[:])
            c.inc(ins, "T")

        @blk.vector
        def _(eng, ch=ch):
            c.wait(eng, "T")
            ins = eng.tensor_copy(PLOC[:, ch * 128:(ch + 1) * 128], c.PS[3][0:8, 0:128])
            c.inc(ins, "V")

    @blk.sync
    def _(eng):
        c.wait(eng, "V")
        ins = eng.dma_start(out=dram["cc_in"], in_=PLOC)
        c.inc(ins, "Y", 16)
        ins = eng.dma_start(out=c.SC[0:64, 4320:4384], in_=dram["G64"])
        c.inc(ins, "Y", 16)

    @blk.gpsimd
    def _(eng):
        c.wait(eng, "Y")
        ins = eng.collective_compute(
            "AllGather", ALU.bypass, replica_groups=[list(range(B))],
            ins=[dram["cc_in"].opt()], outs=[dram["cc_out"].opt()])
        c.inc(ins, "C")
        eng.wait_ge(c.sem["C"], c.cnt["C"])
        ins = eng.dma_start(out=PALL, in_=dram["cc_out"])
        c.inc(ins, "G", 16)

    @blk.vector
    def _(eng):
        c.wait(eng, "G")
        ins = eng.memset(TH, 0.5)
        c.inc(ins, "V")
    NIT = 26
    for it in range(NIT):
        step = 0.5 ** (it + 2)

        @blk.vector
        def _(eng, it=it):
            c.wait(eng, "T", "V")
            c.pb(eng, eng.tensor_scalar(IND, PALL, TH, None, ALU.is_gt))
            ins = eng.tensor_reduce(c.SC[0:64, 4384:4385], IND, AX.X, ALU.add)
            c.inc(ins, "V")

        @blk.tensor
        def _(eng, it=it):
            c.wait(eng, "V")
            if it == 0:
                c.wait(eng, "Y")
            ins = eng.matmul(c.PS[2][0:64, 0:1], c.SC[0:64, 4320:4384],
                             c.SC[0:64, 4384:4385], start=True, stop=True)
            c.inc(ins, "T")

        @blk.vector
        def _(eng, it=it, step=step):
            c.wait(eng, "T")
            d = c.SC[0:64, 4226:4227]
            c.pb(eng, eng.tensor_scalar(d, c.PS[2][0:64, 0:1], 2048.0, None, ALU.is_gt))
            c.pb(eng, eng.tensor_scalar(d, d, 2.0, -1.0, ALU.mult, ALU.add))
            ins = eng.scalar_tensor_tensor(TH, d, float(step), TH, ALU.mult, ALU.add)
            c.inc(ins, "V")

    @blk.vector
    def _(eng):
        ins = eng.tensor_scalar(TH, TH, 4.0 * (0.5 ** (NIT + 1)), None, ALU.add)
        c.inc(ins, "V")

    @blk.tensor
    def _(eng):
        c.wait(eng, "V")
        ins = eng.transpose(c.PS[2][0:1, 0:8], TH[0:8, 0:1], c.IDT[0:8, 0:8])
        c.inc(ins, "T")

    @blk.vector
    def _(eng):
        c.wait(eng, "T")
        ins = eng.tensor_copy(c.SC[0:1, 4400:4408], c.PS[2][0:1, 0:8])
        c.inc(ins, "V")

    @blk.tensor
    def _(eng):
        c.wait(eng, "V")
        ins = eng.matmul(c.PS[2][:, 0:8], c.ONES[0:1, 0:128], c.SC[0:1, 4400:4408],
                         start=True, stop=True)
        c.inc(ins, "T")

    @blk.scalar
    def _(eng):
        c.wait(eng, "T")
        ins = eng.activation(c.SC[:, 4240:4248], c.PS[2][:, 0:8], AF.Identity)
        c.inc(ins, "S")

    for ch in range(16):
        @blk.vector
        def _(eng, ch=ch):
            c.wait(eng, "S", "T")
            thb = c.SC[:, 4240:4248]
            sel = c.SC[:, 4256:4264]
            tB = c.SC[:, 4272:4280]
            nselc = c.SC[:, 4288:4289]
            asg = c.SC[:, 4289:4290]
            mxs = c.SC[:, 4290:4291]
            mxa = c.SC[:, 4291:4292]
            wv = c.SC[:, 4292:4293]
            inda = c.SC[:, 4304:4312]
            indb = c.SC[:, 4312:4320]
            c.pb(eng, eng.tensor_tensor(sel, PR(ch), thb, ALU.is_gt))
            c.pb(eng, eng.tensor_reduce(nselc, sel, AX.X, ALU.add))
            c.pb(eng, eng.tensor_scalar(asg, nselc, 0.5, None, ALU.is_gt))
            c.pb(eng, eng.tensor_scalar(tB, PR(ch), 10000.0, None, ALU.add))
            c.pb(eng, eng.tensor_tensor(tB, tB, sel, ALU.mult))
            c.pb(eng, eng.tensor_scalar(tB, tB, 10000.0, None, ALU.subtract))
            c.pb(eng, eng.tensor_reduce(mxs, tB, AX.X, ALU.max))
            c.pb(eng, eng.tensor_reduce(mxa, PR(ch), AX.X, ALU.max))
            c.pb(eng, eng.tensor_tensor(wv, mxs, mxa, ALU.subtract))
            c.pb(eng, eng.scalar_tensor_tensor(wv, wv, asg, mxa, ALU.mult, ALU.add))
            c.pb(eng, eng.tensor_scalar(inda, tB, mxs, None, ALU.is_equal))
            c.pb(eng, eng.tensor_scalar(indb, PR(ch), mxa, None, ALU.is_equal))
            c.pb(eng, eng.tensor_tensor(inda, inda, indb, ALU.subtract))
            c.pb(eng, eng.scalar_tensor_tensor(inda, inda, asg, indb, ALU.mult, ALU.add))
            ins = eng.tensor_scalar(inda, inda, wv, None, ALU.mult)
            c.inc(ins, "V")

        @blk.tensor
        def _(eng, ch=ch):
            c.wait(eng, "V")
            ins = eng.transpose(c.PS[3][0:8, 0:128], c.SC[:, 4304:4312], c.IDT[:])
            c.inc(ins, "T")

        @blk.vector
        def _(eng, ch=ch):
            c.wait(eng, "T")
            ins = eng.tensor_copy(c.BB[0:8, ch * 128:(ch + 1) * 128],
                                  c.PS[3][0:8, 0:128])
            c.inc(ins, "V")

    if _DBG["on"]:
        @blk.sync
        def _(eng):
            c.wait(eng, "V", "S", "T")
            eng.dma_start(out=dram["dbg_pr"], in_=c.SC[:, 0:128]).then_inc(c.sem["O"], 16)
            eng.dma_start(out=dram["dbg_th"], in_=TH).then_inc(c.sem["O"], 16)
            c.cnt["O"] += 32

        @blk.gpsimd
        def _(eng):
            c.wait(eng, "V")
            ins = eng.dma_start(out=dram["dbg_gate"], in_=c.BB[0:8, 0:2048])
            c.inc(ins, "G", 16)

    wtok = {}
    tok = {"gbt": 0, "gbc": 0, "comb": 0}

    ctr = {"w1": 0, "w2": 0}
    for s_ in range(3):
        tok[("w1u", s_)] = c.cnt["T"]
        tok[("w2u", s_)] = c.cnt["T"]

    def load_w1(e, fc, _slot_unused=None):
        slot = ctr["w1"] % 3
        ctr["w1"] += 1
        wtok[("w1s", e, fc)] = slot
        src = dram["W1_0"][e][:, fc * 128:(fc + 1) * 128]
        src = src.rearrange("(kt p) f -> p kt f", p=128)

        @blk.gpsimd
        def _(eng, slot=slot):
            c.wait_at(eng, "T", tok.get(("w1u", slot), 0))
            ins = eng.dma_start(out=c.WR[:, slot * 1024:(slot + 1) * 1024], in_=src)
            wtok[("w1", e, fc)] = c.inc(ins, "G", 16)

    def load_w2(e, oc, _slot_unused=None):
        slot = ctr["w2"] % 3
        ctr["w2"] += 1
        wtok[("w2s", e, oc)] = slot
        src = dram["W2_0"][e][:, oc * 128:(oc + 1) * 128]
        src = src.rearrange("(fc p) o -> p fc o", p=128)

        @blk.gpsimd
        def _(eng, slot=slot):
            c.wait_at(eng, "T", tok.get(("w2u", slot), 0))
            ins = eng.dma_start(out=c.WB[:, slot * 2048:slot * 2048 + 2048], in_=src)
            wtok[("w2", e, oc)] = c.inc(ins, "G", 16)

    pf_s = {0: c.cnt["S"], 1: c.cnt["S"]}
    pf_v = {0: c.cnt["V"], 1: c.cnt["V"]}
    tok["gbc"] = c.cnt["S"]
    tok["comb"] = c.cnt["V"]
    for half in range(2):
        t0 = half * 1024
        load_w1(0, 0, 0)
        load_w1(0, 1, 1)
        for e in range(NE0):
            for fc in range(16):
                la, lb = (e, fc + 2) if fc + 2 < 16 else (e + 1, fc + 2 - 16)
                if la < NE0:
                    load_w1(la, lb, (fc + 2) % 3)
                pr = fc % 2
                bk0, bk1 = c.PS[2 * pr], c.PS[2 * pr + 1]

                @blk.tensor
                def _(eng, e=e, fc=fc, pr=pr, bk0=bk0, bk1=bk1, t0=t0):
                    c.wait_at(eng, "G", wtok[("w1", e, fc)])
                    c.wait_at(eng, "S", pf_s[pr])
                    c.wait_at(eng, "V", pf_v[pr])
                    slot = wtok[("w1s", e, fc)]
                    for tq, bank in ((0, bk0), (1, bk1)):
                        for kt in range(NDT):
                            ins = eng.matmul(
                                bank[:, 0:TC],
                                c.WR[:, slot * 1024 + kt * 128:slot * 1024 + (kt + 1) * 128],
                                c.XR[:, c.dcols(kt, t0 + tq * TC, t0 + (tq + 1) * TC)],
                                start=(kt == 0), stop=(kt == NDT - 1))
                    tok[("h1t", pr)] = tok[("w1u", slot)] = c.inc(ins, "T")

                @blk.scalar
                def _(eng, fc=fc, pr=pr, bk0=bk0):
                    c.wait_at(eng, "T", tok[("h1t", pr)])
                    ins = eng.activation(c.BA[:, fc * 1024:(fc + 1) * 1024],
                                         c.PSA[:, pr * 1024:(pr + 1) * 1024], AF.Gelu)
                    pf_s[pr] = c.inc(ins, "S")

            load_w2(e, 0, 0)
            load_w2(e, 1, 1)
            for oc in range(NDT):
                la, lb = (e, oc + 2) if oc + 2 < NDT else (e + 1, oc + 2 - NDT)
                if la < NE0:
                    load_w2(la, lb, (oc + 2) % 3)
                pr = oc % 2
                bk0, bk1 = c.PS[2 * pr], c.PS[2 * pr + 1]

                @blk.tensor
                def _(eng, e=e, oc=oc, pr=pr, bk0=bk0, bk1=bk1, half=half):
                    c.wait_at(eng, "G", wtok[("w2", e, oc)])
                    c.wait_at(eng, "S", max(pf_s[0], pf_s[1]))
                    c.wait_at(eng, "V", pf_v[pr])
                    slot = wtok[("w2s", e, oc)]
                    for tq, bank in ((0, bk0), (1, bk1)):
                        for fc in range(16):
                            ins = eng.matmul(
                                bank[:, 0:TC],
                                c.WB[:, slot * 2048 + fc * 128:slot * 2048 + (fc + 1) * 128],
                                c.BA[:, fc * 1024 + tq * TC:fc * 1024 + (tq + 1) * TC],
                                start=(fc == 0), stop=(fc == 15))
                    tok[("h2t", pr)] = tok[("w2u", slot)] = c.inc(ins, "T")
                    c.wait_at(eng, "S", tok["gbc"])
                    eng.matmul(c.PS[6][:, 0:TC],
                               c.BB[0:8, 2560 + e * 128:2560 + (e + 1) * 128],
                               c.BB[0:8, half * 1024:half * 1024 + TC],
                               start=True, stop=True)
                    ins = eng.matmul(c.PS[7][:, 0:TC],
                                     c.BB[0:8, 2560 + e * 128:2560 + (e + 1) * 128],
                                     c.BB[0:8, half * 1024 + TC:half * 1024 + 1024],
                                     start=True, stop=True)
                    tok["gbt"] = c.inc(ins, "T")

                @blk.scalar
                def _(eng):
                    c.wait_at(eng, "T", tok["gbt"])
                    c.wait_at(eng, "V", tok["comb"])
                    eng.activation(c.BB[:, 2048:2560], c.PS[6][:, 0:TC], AF.Identity)
                    ins = eng.activation(c.BB[:, 3584:4096], c.PS[7][:, 0:TC], AF.Identity)
                    tok["gbc"] = c.inc(ins, "S")

                @blk.vector
                def _(eng, e=e, oc=oc, pr=pr, bk0=bk0, bk1=bk1):
                    c.wait_at(eng, "T", tok[("h2t", pr)])
                    c.wait_at(eng, "S", tok["gbc"])
                    acc = c.SC[:, oc * 1024:(oc + 1) * 1024]
                    for tq, bank, gc in ((0, bk0, 2048), (1, bk1, 3584)):
                        a_ = acc[:, tq * TC:(tq + 1) * TC]
                        gb = c.BB[:, gc:gc + 512]
                        if e == 0:
                            ins = eng.tensor_tensor(a_, bank[:, 0:TC], gb, ALU.mult)
                        else:
                            tmp = c.BB[:, 4608:5120]
                            eng.tensor_tensor(tmp, bank[:, 0:TC], gb, ALU.mult)
                            ins = eng.tensor_tensor(a_, a_, tmp, ALU.add)
                    pf_v[pr] = tok["comb"] = c.inc(ins, "V")

        for j in range(NDT):
            @blk.vector
            def _(eng, j=j, t0=t0):
                c.wait(eng, "V", "S")
                ins = eng.tensor_tensor(
                    c.XR[:, c.dcols(j, t0, t0 + 1024)],
                    c.XR[:, c.dcols(j, t0, t0 + 1024)],
                    c.SC[:, j * 1024:(j + 1) * 1024], ALU.add)
                c.inc(ins, "V")

    _emit_norm_fm(c, c.XR, c.XR, 1e-5, subtract_mean=True)


def _emit_l1_attn(c, dram):
    nc, blk = c.nc, c.block
    QP = lambda c0, c1: c.BB[:, c0:c1]
    VW0 = 4096
    EXS = 6176
    BTR = 8224
    @blk.sync
    def _(eng):
        c.wait(eng, "V", "S", "T")
        ins = eng.dma_start(out=c.MISC[:, M_GAM:M_GAM + 8], in_=dram["gamma"])
        c.inc(ins, "Y", 16)
        ins = eng.dma_start(out=c.MISC[:, M_BET:M_BET + 8], in_=dram["beta"])
        c.inc(ins, "Y", 16)

    wtok = {}

    def load_wpair(kind, gp, slot):
        src = dram[kind][:, gp * 128:(gp + 1) * 128]
        src = src.rearrange("(kt p) o -> p kt o", p=128)

        @blk.gpsimd
        def _(eng):
            c.wait(eng, "T")
            ins = eng.dma_start(out=c.WR[:, slot * 1024:(slot + 1) * 1024], in_=src)
            wtok[(kind, gp)] = c.inc(ins, "G", 16)

    btok = {}

    def load_bias(g, kc, qc):
        src = dram["biasT"][g][kc * 128:(kc + 1) * 128, qc * TC:(qc + 1) * TC]

        @blk.gpsimd
        def _(eng, kc=kc):
            c.wait_at(eng, "T", bslot_tok.get(kc, 0))
            ins = eng.dma_start(out=c.WB[:, kc * TC:(kc + 1) * TC], in_=src)
            btok[(g, kc, qc)] = c.inc(ins, "G", 16)

    tok = {"qkc": c.cnt["S"], "vv": 0, "pv": 0, "cmb": c.cnt["V"],
           "rcp": c.cnt["V"], "bcc": c.cnt["S"]}
    exp_tok = {0: c.cnt["S"], 1: c.cnt["S"]}
    pvu_tok = {0: c.cnt["T"], 1: c.cnt["T"]}
    bslot_tok = {}
    for kc_ in range(16):
        bslot_tok[kc_] = c.cnt["T"]
    for gp in range(NH // 2):
        for idx, kind in enumerate(("Wq", "Wk", "Wv")):
            load_wpair(kind, gp, idx)
        for idx, kind in enumerate(("Wq", "Wk")):
            @blk.tensor
            def _(eng, kind=kind, idx=idx, gp=gp):
                c.wait_at(eng, "G", wtok[(kind, gp)])
                c.wait_at(eng, "S", tok["qkc"])
                c.wait_at(eng, "S", exp_tok[0])
                c.wait_at(eng, "S", exp_tok[1])
                c.wait_at(eng, "V", tok["cmb"])
                for tc in range(NTC):
                    for kt in range(NDT):
                        ins = eng.matmul(
                            c.PS[tc][:, 0:TC],
                            c.WR[:, idx * 1024 + kt * 128:idx * 1024 + (kt + 1) * 128],
                            c.XR[:, c.dcols(kt, tc * TC, (tc + 1) * TC)],
                            start=(kt == 0), stop=(kt == NDT - 1))
                tok["qkt"] = c.inc(ins, "T")

            @blk.scalar
            def _(eng, idx=idx):
                c.wait_at(eng, "T", tok["qkt"])
                ins = eng.activation(c.BB[:, idx * 2048:(idx + 1) * 2048], c.PSA[:, 0:2048], AF.Identity)
                tok["qkc"] = c.inc(ins, "S")

        @blk.vector
        def _(eng, gp=gp):
            c.wait_at(eng, "S", tok["qkc"])
            for tt in range(16):
                for h in range(2):
                    ins = eng.memset(c.BB[:, VW0 + (tt * 2 + h) * 65 + 64:VW0 + (tt * 2 + h) * 65 + 65], 1.0)
            tok["vz"] = c.inc(ins, "V")

        for tt in range(16):
            bank = c.PS[tt % 2]

            @blk.tensor
            def _(eng, tt=tt, bank=bank, gp=gp):
                c.wait_at(eng, "G", wtok[("Wv", gp)])
                if tt == 0:
                    c.wait_at(eng, "S", tok["qkc"])
                    c.wait_at(eng, "V", tok["vz"])
                else:
                    c.wait_at(eng, "V", tok.get(("vc", tt % 2), 0))
                for kt in range(NDT):
                    ins = eng.matmul(
                        bank[:, 0:128],
                        c.XR[:, c.dcols(kt, tt * 128, (tt + 1) * 128)],
                        c.WR[:, 2 * 1024 + kt * 128:2 * 1024 + (kt + 1) * 128],
                        start=(kt == 0), stop=(kt == NDT - 1))
                tok[("vt", tt % 2)] = c.inc(ins, "T")

            @blk.vector
            def _(eng, tt=tt, bank=bank):
                c.wait_at(eng, "T", tok[("vt", tt % 2)])
                eng.tensor_copy(c.BB[:, VW0 + (tt * 2) * 65:VW0 + (tt * 2) * 65 + 64],
                                bank[:, 0:64])
                ins = eng.tensor_copy(c.BB[:, VW0 + (tt * 2 + 1) * 65:VW0 + (tt * 2 + 1) * 65 + 64],
                                      bank[:, 64:128])
                tok[("vc", tt % 2)] = c.inc(ins, "V")

        for hh in range(2):
            g = gp * 2 + hh
            rb = hh * 64

            @blk.tensor
            def _(eng):
                eng.drain()

            for qc in range(NTC):
                PV = c.PS[4]
                BV = c.PS[5]
                def emit_scores(g2):
                    load_bias(g, g2 * 2, qc)
                    load_bias(g, g2 * 2 + 1, qc)
                    pr = g2 % 2
                    bk0, bk1 = c.PS[2 * pr], c.PS[2 * pr + 1]

                    @blk.tensor
                    def _(eng, g=g, rb=rb, qc=qc, g2=g2, pr=pr, bk0=bk0, bk1=bk1):
                        c.wait_at(eng, "S", exp_tok[pr])
                        if g2 < 2:
                            c.wait_at(eng, "V", tok.get(("vc", 0), 0))
                            c.wait_at(eng, "V", tok.get(("vc", 1), 0))
                        for i, bank in ((0, bk0), (1, bk1)):
                            kc = g2 * 2 + i
                            ins = eng.matmul(
                                bank[:, 0:TC],
                                c.BB[rb:rb + 64, 2048 + kc * 128:2048 + (kc + 1) * 128],
                                c.BB[rb:rb + 64, qc * TC:(qc + 1) * TC],
                                start=True, stop=True)
                        tok[("sct", pr)] = c.inc(ins, "T")

                def emit_exp(g2):
                    pr = g2 % 2

                    @blk.scalar
                    def _(eng, g2=g2, pr=pr):
                        c.wait_at(eng, "T", tok[("sct", pr)])
                        c.wait_at(eng, "T", pvu_tok[pr])
                        ins = eng.activation(
                            c.BB[:, EXS + pr * 1024:EXS + (pr + 1) * 1024],
                            c.PSA[:, pr * 1024:(pr + 1) * 1024], AF.Exp,
                            scale=float(D) ** -0.5)
                        exp_tok[pr] = c.inc(ins, "S")

                def emit_pv(g2):
                    pr = g2 % 2

                    @blk.tensor
                    def _(eng, g=g, qc=qc, g2=g2, pr=pr, PV=PV):
                        c.wait_at(eng, "S", exp_tok[pr])
                        if g2 == 0:
                            c.wait_at(eng, "V", tok["cmb"])
                        for i in range(2):
                            kc = g2 * 2 + i
                            vw = c.BB[:, VW0 + (kc * 2 + g % 2) * 65:VW0 + (kc * 2 + g % 2) * 65 + 65]
                            ins = eng.matmul(PV[0:65, 0:TC], vw,
                                             c.BB[:, EXS + pr * 1024 + i * TC:EXS + pr * 1024 + (i + 1) * TC],
                                             start=(kc == 0), stop=(kc == 15))
                        pvu_tok[pr] = c.inc(ins, "T")

                emit_scores(0)
                for g2 in range(8):
                    emit_exp(g2)
                    if g2 + 1 < 8:
                        emit_scores(g2 + 1)
                    emit_pv(g2)

                @blk.vector
                def _(eng, PV=PV):
                    c.wait_at(eng, "T", c.cnt["T"])
                    with nc.allow_low_precision(reason="f32r recip row for bcast matmul"):
                        ins = eng.reciprocal(c.MR[0:1, 256:768][0:1, 0:TC], PV[64:65, 0:TC])
                    tok["rcp"] = c.inc(ins, "V")

                @blk.tensor
                def _(eng, g=g, qc=qc, BV=BV):
                    for kc in range(16):
                        c.wait_at(eng, "G", btok[(g, kc, qc)])
                        vw = c.BB[:, VW0 + (kc * 2 + g % 2) * 65:VW0 + (kc * 2 + g % 2) * 65 + 65]
                        ins = eng.matmul(BV[0:65, 0:TC], vw,
                                         c.WB[:, kc * TC:(kc + 1) * TC],
                                         start=(kc == 0), stop=(kc == 15))
                        bslot_tok[kc] = c.inc(ins, "T")

                @blk.tensor
                def _(eng):
                    c.wait_at(eng, "V", tok["rcp"])
                    ins = eng.matmul(c.PS[6][0:64, 0:TC], c.MR[0:1, 65:129],
                                     c.MR[0:1, 256:768][0:1, 0:TC], start=True, stop=True)
                    tok["bct"] = c.inc(ins, "T")

                @blk.scalar
                def _(eng):
                    c.wait_at(eng, "T", tok["bct"])
                    c.wait_at(eng, "V", tok["cmb"])
                    ins = eng.activation(c.SC[0:64, 0:TC], c.PS[6][0:64, 0:TC], AF.Identity)
                    tok["bcc"] = c.inc(ins, "S")

                @blk.vector
                def _(eng, g=g, qc=qc, PV=PV, BV=BV):
                    c.wait_at(eng, "S", tok["bcc"])
                    c.wait_at(eng, "T", c.cnt["T"])
                    tmp = c.SC[0:64, 512:1024]
                    eng.tensor_tensor(tmp, PV[0:64, 0:TC], c.SC[0:64, 0:TC], ALU.mult)
                    dst = c.BA[(g % 2) * 64:(g % 2) * 64 + 64,
                               c.dcols(g // 2, qc * TC, (qc + 1) * TC)]
                    ins = eng.tensor_tensor(dst, tmp, BV[0:64, 0:TC], ALU.add)
                    tok["cmb"] = c.inc(ins, "V")

    _emit_norm_fm(c, c.BA, c.BA, 1e-3, subtract_mean=True, gb=True, src_bf16=True)
    for tc in range(NTC):
        @blk.vector
        def _(eng, tc=tc):
            c.wait(eng, "V", "S")
            for j in range(NDT):
                ins = eng.tensor_tensor(c.XR[:, c.dcols(j, tc * TC, (tc + 1) * TC)],
                                        c.XR[:, c.dcols(j, tc * TC, (tc + 1) * TC)],
                                        c.BA[:, c.dcols(j, tc * TC, (tc + 1) * TC)], ALU.add)
            c.inc(ins, "V")
    _emit_norm_fm(c, c.XR, c.XR, 1e-5, subtract_mean=True)


def _emit_l1_ffn(c, dram):
    nc, blk = c.nc, c.block
    P0 = lambda ch: c.SC[:, ch * 16:(ch + 1) * 16]

    @blk.gpsimd
    def _(eng):
        c.wait(eng, "T", "V", "S")
        ins = eng.dma_start(out=c.MR[:, 1280:1408],
                            in_=dram["rw"].rearrange("(kt p) e -> p kt e", p=128))
        c.inc(ins, "G", 16)
        ins = eng.dma_start(out=c.MR[0:1, 1552:1568], in_=dram["rb"].unsqueeze(0))
        c.inc(ins, "G", 16)
        ins = eng.dma_start(out=c.BB[0:16, 2560:4608], in_=dram["OH16"])
        c.inc(ins, "G", 16)
        ins = eng.dma_start(out=c.WB[0:16, 6144:7168], in_=dram["b2"])
        c.inc(ins, "G", 16)

    @blk.tensor
    def _(eng):
        c.wait(eng, "G", "V", "S")
        ins = eng.matmul(c.PS[3][:, 0:16], c.MR[0:1, 1:129], c.MR[0:1, 1552:1568],
                         start=True, stop=True)
        c.inc(ins, "T")

    @blk.scalar
    def _(eng):
        c.wait(eng, "T")
        ins = eng.activation(c.SC[:, 4352:4368], c.PS[3][:, 0:16], AF.Identity)
        c.inc(ins, "S")

    for ch in range(16):
        @blk.tensor
        def _(eng, ch=ch):
            c.wait(eng, "G", "V", "S")
            for kt in range(NDT):
                ins = eng.matmul(c.PS[2][:, 0:16],
                                 c.XR[:, c.dcols(kt, ch * 128, (ch + 1) * 128)],
                                 c.MR[:, 1280 + kt * 16:1280 + (kt + 1) * 16],
                                 start=(kt == 0), stop=(kt == NDT - 1))
            c.inc(ins, "T")

        @blk.vector
        def _(eng, ch=ch):
            c.wait(eng, "T", "S")
            pm = c.SC[:, 4368:4384]
            mx = c.SC[:, 4401:4402]
            eng.tensor_tensor(pm, c.PS[2][:, 0:16], c.SC[:, 4352:4368], ALU.add)
            c.pb(eng, eng.tensor_reduce(mx, pm, AX.X, ALU.max))
            ins = eng.tensor_scalar(pm, pm, mx, None, ALU.subtract)
            c.inc(ins, "V")

        @blk.scalar
        def _(eng, ch=ch):
            c.wait(eng, "V")
            ins = eng.activation(c.SC[:, 4368:4384], c.SC[:, 4368:4384], AF.Exp)
            c.inc(ins, "S")

        @blk.vector
        def _(eng, ch=ch):
            c.wait(eng, "S")
            pm = c.SC[:, 4368:4384]
            sm = c.SC[:, 4401:4402]
            c.pb(eng, eng.tensor_reduce(sm, pm, AX.X, ALU.add))
            c.pb(eng, eng.reciprocal(sm, sm))
            c.pb(eng, eng.tensor_scalar(P0(ch), pm, sm, None, ALU.mult))
            indsum = c.SC[:, 4384:4400]
            wsum = c.SC[:, 4400:4401]
            tmu = c.SC[:, 4402:4418]
            mm_ = c.SC[:, 4418:4419]
            eng.tensor_copy(pm, P0(ch))
            eng.memset(indsum, 0.0)
            c.pb(eng, eng.memset(wsum, 0.0))
            for k4 in range(4):
                c.pb(eng, eng.tensor_reduce(mm_, pm, AX.X, ALU.max))
                c.pb(eng, eng.tensor_scalar(tmu, pm, mm_, None, ALU.is_equal))
                c.pb(eng, eng.tensor_tensor(indsum, indsum, tmu, ALU.add))
                c.pb(eng, eng.tensor_tensor(wsum, wsum, mm_, ALU.add))
                c.pb(eng, eng.tensor_tensor(tmu, tmu, pm, ALU.mult))
                c.pb(eng, eng.tensor_tensor(pm, pm, tmu, ALU.subtract))
            c.pb(eng, eng.reciprocal(wsum, wsum))
            c.pb(eng, eng.tensor_tensor(tmu, P0(ch), indsum, ALU.mult))
            ins = eng.tensor_scalar(tmu, tmu, wsum, None, ALU.mult)
            c.inc(ins, "V")

        @blk.tensor
        def _(eng, ch=ch):
            c.wait(eng, "V")
            ins = eng.transpose(c.PS[3][0:16, 0:128], c.SC[:, 4402:4418], c.IDT[:])
            c.inc(ins, "T")

        @blk.vector
        def _(eng, ch=ch):
            c.wait(eng, "T")
            ins = eng.tensor_copy(c.BB[0:16, ch * 128:(ch + 1) * 128],
                                  c.PS[3][0:16, 0:128])
            c.inc(ins, "V")

    wtok = {}

    tok = {"gbt": 0, "gbc": 0, "comb": 0}

    ctr = {"w1": 0, "w2": 0}
    for s_ in range(3):
        tok[("w1u", s_)] = c.cnt["T"]
        tok[("w2u", s_)] = c.cnt["T"]

    def load_w1(e, fc, _slot_unused=None):
        slot = ctr["w1"] % 3
        ctr["w1"] += 1
        wtok[("w1s", e, fc)] = slot
        src = dram["W1_1"][e][:, fc * 128:(fc + 1) * 128]
        src = src.rearrange("(kt p) f -> p kt f", p=128)

        @blk.gpsimd
        def _(eng, slot=slot):
            c.wait_at(eng, "T", tok.get(("w1u", slot), 0))
            ins = eng.dma_start(out=c.WR[:, slot * 1024:(slot + 1) * 1024], in_=src)
            wtok[("w1", e, fc)] = c.inc(ins, "G", 16)

    def load_w2(e, oc, _slot_unused=None):
        slot = ctr["w2"] % 3
        ctr["w2"] += 1
        wtok[("w2s", e, oc)] = slot
        src = dram["W2_1"][e][:, oc * 128:(oc + 1) * 128]
        src = src.rearrange("(fc p) o -> p fc o", p=128)

        @blk.gpsimd
        def _(eng, slot=slot):
            c.wait_at(eng, "T", tok.get(("w2u", slot), 0))
            ins = eng.dma_start(out=c.WB[:, slot * 2048:slot * 2048 + 2048], in_=src)
            wtok[("w2", e, oc)] = c.inc(ins, "G", 16)

    def load_b1(e):
        @blk.sync
        def _(eng):
            c.wait(eng, "S")
            ins = eng.dma_start(out=c.MISC[:, 104:120], in_=dram["b1c"][e])
            wtok[("b1", e)] = c.inc(ins, "Y", 16)

    pf_s = {0: c.cnt["S"], 1: c.cnt["S"]}
    pf_v = {0: c.cnt["V"], 1: c.cnt["V"]}
    tok["gbc"] = c.cnt["S"]
    tok["comb"] = c.cnt["V"]
    for half in range(2):
        t0 = half * 1024
        load_w1(0, 0, 0)
        load_w1(0, 1, 1)
        for e in range(NE1):
            load_b1(e)
            for fc in range(16):
                la, lb = (e, fc + 2) if fc + 2 < 16 else (e + 1, fc + 2 - 16)
                if la < NE1:
                    load_w1(la, lb, (fc + 2) % 3)
                pr = fc % 2
                bk0, bk1 = c.PS[2 * pr], c.PS[2 * pr + 1]

                @blk.tensor
                def _(eng, e=e, fc=fc, pr=pr, bk0=bk0, bk1=bk1, t0=t0):
                    c.wait_at(eng, "G", wtok[("w1", e, fc)])
                    c.wait_at(eng, "S", pf_s[pr])
                    c.wait_at(eng, "V", pf_v[pr])
                    slot = wtok[("w1s", e, fc)]
                    for tq, bank in ((0, bk0), (1, bk1)):
                        for kt in range(NDT):
                            ins = eng.matmul(
                                bank[:, 0:TC],
                                c.WR[:, slot * 1024 + kt * 128:slot * 1024 + (kt + 1) * 128],
                                c.XR[:, c.dcols(kt, t0 + tq * TC, t0 + (tq + 1) * TC)],
                                start=(kt == 0), stop=(kt == NDT - 1))
                    tok[("h1t", pr)] = tok[("w1u", slot)] = c.inc(ins, "T")

                @blk.scalar
                def _(eng, e=e, fc=fc, pr=pr, bk0=bk0):
                    c.wait_at(eng, "T", tok[("h1t", pr)])
                    c.wait_at(eng, "Y", wtok[("b1", e)])
                    ins = eng.activation(c.BA[:, fc * 1024:(fc + 1) * 1024],
                                         c.PSA[:, pr * 1024:(pr + 1) * 1024], AF.Gelu,
                                         bias=c.mcol(104, fc))
                    pf_s[pr] = c.inc(ins, "S")

            load_w2(e, 0, 0)
            load_w2(e, 1, 1)
            for oc in range(NDT):
                la, lb = (e, oc + 2) if oc + 2 < NDT else (e + 1, oc + 2 - NDT)
                if la < NE1:
                    load_w2(la, lb, (oc + 2) % 3)
                pr = oc % 2
                bk0, bk1 = c.PS[2 * pr], c.PS[2 * pr + 1]

                @blk.tensor
                def _(eng, e=e, oc=oc, pr=pr, bk0=bk0, bk1=bk1, half=half):
                    c.wait_at(eng, "G", wtok[("w2", e, oc)])
                    c.wait_at(eng, "S", max(pf_s[0], pf_s[1]))
                    c.wait_at(eng, "V", pf_v[pr])
                    slot = wtok[("w2s", e, oc)]
                    for tq, bank in ((0, bk0), (1, bk1)):
                        for fc in range(16):
                            ins = eng.matmul(
                                bank[:, 0:TC],
                                c.WB[:, slot * 2048 + fc * 128:slot * 2048 + (fc + 1) * 128],
                                c.BA[:, fc * 1024 + tq * TC:fc * 1024 + (tq + 1) * TC],
                                start=(fc == 0), stop=(fc == 15))
                    tok[("h2t", pr)] = tok[("w2u", slot)] = c.inc(ins, "T")
                    c.wait_at(eng, "S", tok["gbc"])
                    eng.matmul(c.PS[6][:, 0:TC],
                               c.BB[0:16, 2560 + e * 128:2560 + (e + 1) * 128],
                               c.BB[0:16, half * 1024:half * 1024 + TC],
                               start=True, stop=True)
                    ins = eng.matmul(c.PS[7][:, 0:TC],
                                     c.BB[0:16, 2560 + e * 128:2560 + (e + 1) * 128],
                                     c.BB[0:16, half * 1024 + TC:half * 1024 + 1024],
                                     start=True, stop=True)
                    tok["gbt"] = c.inc(ins, "T")

                @blk.scalar
                def _(eng):
                    c.wait_at(eng, "T", tok["gbt"])
                    c.wait_at(eng, "V", tok["comb"])
                    eng.activation(c.BB[:, 2048:2560], c.PS[6][:, 0:TC], AF.Identity)
                    ins = eng.activation(c.BB[:, 4608:5120], c.PS[7][:, 0:TC], AF.Identity)
                    tok["gbc"] = c.inc(ins, "S")

                @blk.vector
                def _(eng, e=e, oc=oc, pr=pr, bk0=bk0, bk1=bk1):
                    c.wait_at(eng, "T", tok[("h2t", pr)])
                    c.wait_at(eng, "S", tok["gbc"])
                    acc = c.SC[:, oc * 1024:(oc + 1) * 1024]
                    for tq, bank, gc in ((0, bk0, 2048), (1, bk1, 4608)):
                        a_ = acc[:, tq * TC:(tq + 1) * TC]
                        gb = c.BB[:, gc:gc + 512]
                        if e == 0:
                            ins = eng.tensor_tensor(a_, bank[:, 0:TC], gb, ALU.mult)
                        else:
                            tmp = c.BB[:, 5120:5632]
                            eng.tensor_tensor(tmp, bank[:, 0:TC], gb, ALU.mult)
                            ins = eng.tensor_tensor(a_, a_, tmp, ALU.add)
                    pf_v[pr] = tok["comb"] = c.inc(ins, "V")

        for oc in range(NDT):
            for tq in range(2):
                @blk.tensor
                def _(eng, oc=oc, tq=tq, half=half):
                    c.wait(eng, "V", "S")
                    ins = eng.matmul(c.PS[5][:, 0:TC],
                                     c.WB[0:16, 6144 + oc * 128:6144 + (oc + 1) * 128],
                                     c.BB[0:16, half * 1024 + tq * TC:half * 1024 + (tq + 1) * TC],
                                     start=True, stop=True)
                    c.inc(ins, "T")

                @blk.vector
                def _(eng, oc=oc, tq=tq):
                    c.wait(eng, "T")
                    acc = c.SC[:, oc * 1024 + tq * TC:oc * 1024 + (tq + 1) * TC]
                    ins = eng.tensor_tensor(acc, acc, c.PS[5][:, 0:TC], ALU.add)
                    c.inc(ins, "V")

        for j in range(NDT):
            @blk.vector
            def _(eng, j=j, t0=t0):
                c.wait(eng, "V", "S")
                ins = eng.tensor_tensor(
                    c.XR[:, c.dcols(j, t0, t0 + 1024)],
                    c.XR[:, c.dcols(j, t0, t0 + 1024)],
                    c.SC[:, j * 1024:(j + 1) * 1024], ALU.add)
                c.inc(ins, "V")

    _emit_norm_fm(c, c.XR, c.XR, 1e-5, subtract_mean=True)






def _emit_barrier(c):
    blk = c.block

    @blk.tensor
    def _(eng):
        c.wait(eng, "V", "S", "G", "Y")
        ins = eng.nop()
        c.inc(ins, "T")

    @blk.vector
    def _(eng):
        c.wait(eng, "T", "S", "G", "Y")
        ins = eng.engine_nop()
        c.inc(ins, "V")

    @blk.scalar
    def _(eng):
        c.wait(eng, "T", "V", "G", "Y")
        ins = eng.nop()
        c.inc(ins, "S")

    @blk.gpsimd
    def _(eng):
        c.wait(eng, "T", "V", "S")

    @blk.sync
    def _(eng):
        c.wait(eng, "T", "V", "S")


def _emit_prologue(c, dram):
    blk = c.block

    @blk.sync
    def _(eng):
        ins = eng.dma_start(out=c.IDT[:], in_=dram["ident"])
        c.inc(ins, "Y", 16)

    @blk.gpsimd
    def _(eng):
        ins = eng.dma_start(out=c.MR[:, 0:1], in_=dram["onecol"])
        c.inc(ins, "G", 16)
        ins = eng.dma_start(out=c.MR[0:1, 1:129], in_=dram["onerow"])
        c.inc(ins, "G", 16)

    @blk.vector
    def _(eng):
        eng.memset(c.ONES[:], 1.0)
        eng.memset(c.OB[:], 1.0)
        eng.memset(c.MISC[:, 120:121], -80.0)
        eng.memset(c.MISC[:, 121:122], 1e-6)
        eng.memset(c.MISC[:, 122:123], 1e-5)
        ins = eng.memset(c.MISC[:, 123:124], 1e-3)
        c.inc(ins, "V")

    if c.phases[0] == "l0a":
        for tt in range(T // 128):
            @blk.sync
            def _(eng, tt=tt):
                c.wait(eng, "T")
                ins = eng.dma_start(out=c.SC[:, 0:1024],
                                    in_=dram["x0"][tt * 128:(tt + 1) * 128, :])
                c.inc(ins, "Y", 16)

            vbase = c.cnt["V"]

            @blk.tensor
            def _(eng, tt=tt, vbase=vbase):
                c.wait(eng, "Y")
                for j in range(NDT):
                    if j == 0:
                        c.wait_at(eng, "V", vbase)
                    if j >= 2:
                        c.wait_at(eng, "V", vbase + j - 1)
                    ins = eng.transpose(c.PS[4 + j % 2][:, 0:128],
                                        c.SC[:, j * 128:(j + 1) * 128], c.IDT[:])
                    c.inc(ins, "T")

            @blk.vector
            def _(eng, tt=tt):
                tbase = c.cnt["T"] - NDT
                for j in range(NDT):
                    c.wait_at(eng, "T", tbase + j + 1)
                    ins = eng.tensor_copy(c.XR[:, c.dcols(j, tt * 128, tt * 128 + 128)],
                                          c.PS[4 + j % 2][:, 0:128])
                    c.inc(ins, "V")
    else:
        @blk.gpsimd
        def _(eng):
            ins = eng.dma_start(out=c.XR[:], in_=dram["xin"])
            c.inc(ins, "G", 16)


def _emit_epilogue(c, dram):
    nc, blk = c.nc, c.block
    if c.phases[-1] == "l1f":
        OTM = c.SC[:, 0:1024]
        for tt in range(T // 128):
            vbase = c.cnt["V"]
            otok = c.cnt["O"]

            @blk.tensor
            def _(eng, tt=tt, vbase=vbase):
                c.wait(eng, "G", "S")
                for j in range(NDT):
                    if j == 0:
                        c.wait_at(eng, "V", vbase)
                    if j >= 2:
                        c.wait_at(eng, "V", vbase + j - 1)
                    ins = eng.transpose(c.PS[4 + j % 2][:, 0:128],
                                        c.XR[:, c.dcols(j, tt * 128, tt * 128 + 128)].bitcast(F32),
                                        c.IDT[:])
                    c.inc(ins, "T")

            @blk.vector
            def _(eng, tt=tt, otok=otok):
                tbase = c.cnt["T"] - NDT
                c.wait_at(eng, "O", otok)
                for j in range(NDT):
                    c.wait_at(eng, "T", tbase + j + 1)
                    ins = eng.tensor_copy(OTM[:, j * 128:(j + 1) * 128],
                                          c.PS[4 + j % 2][:, 0:128])
                    c.inc(ins, "V")

            @blk.sync
            def _(eng, tt=tt):
                c.wait(eng, "V")
                ins = eng.dma_start(out=dram["y"][tt * 128:(tt + 1) * 128, :], in_=OTM)
                c.inc(ins, "O", 16)
    else:
        @blk.sync
        def _(eng):
            c.wait(eng, "V", "S", "T", "G")
            ins = eng.dma_start(out=dram["xout"], in_=c.XR[:].bitcast(F32))
            c.inc(ins, "O", 16)

    @blk.sync
    def _(eng):
        c.wait_at(eng, "O", c.cnt["O"])



def build_program(phases):
    nc = bass.Bass("TRN2", target_bir_lowering=False, debug=False)
    dram = {}

    def din(name, shape, dt=F32):
        dram[name] = nc.dram_tensor(name, list(shape), dt, kind="ExternalInput").ap()

    def dout(name, shape, dt=F32):
        dram[name] = nc.dram_tensor(name, list(shape), dt, kind="ExternalOutput").ap()

    if phases[0] == "l0a":
        din("x0", (T, D))
    else:
        din("xin", (128, NDT * T))
    if phases[-1] == "l1f":
        dout("y", (T, D))
    else:
        dout("xout", (128, NDT * T))
    if "l0a" in phases:
        din("wrT", (NL, D, D)); din("wiT", (NL, D, D)); din("woT", (NL, D, D))
        din("cwc", (NL, 128, 32)); din("cb", (NL, 128, 8)); din("wrb", (NL, 128, 8))
        din("wib", (NL, 128, 8)); din("la", (NL, 128, 8)); din("nw", (NL, 128, 8))
        din("now", (128, 8))
    if "l0f" in phases:
        din("Wg", (D, NE0)); din("W1_0", (NE0, D, F0)); din("W2_0", (NE0, F0, D))
        din("G64", (64, 64)); din("OH8", (NE0, NE0 * 128))
        if _DBG["on"]:
            dout("dbg_pr", (128, 128)); dout("dbg_th", (64, 1))
            dout("dbg_gate", (8, 2048)); dout("dbg_acc", (128, 8192))
    if "l0f" in phases:
        dram["cc_in"] = nc.dram_tensor("cc_in", [NE0, T], F32, kind="Internal").ap()
        dram["cc_out"] = nc.dram_tensor("cc_out", [NE0 * B, T], F32, kind="Internal").ap()
    if "l1a" in phases and _DBG["on"]:
        dout("dbg_attn", (128, NDT * T))
    if "l1a" in phases:
        din("Wq", (D, D)); din("Wk", (D, D)); din("Wv", (D, D))
        din("gamma", (128, 8)); din("beta", (128, 8)); din("biasT", (NH, T, T))
    if "l1f" in phases:
        din("rw", (D, NE1)); din("rb", (NE1,))
        din("W1_1", (NE1, D, F1)); din("b1c", (NE1, 128, 16))
        din("W2_1", (NE1, F1, D)); din("b2", (NE1, D))
        din("OH16", (NE1, NE1 * 128))
    din("ident", (128, 128)); din("onecol", (128, 1)); din("onerow", (1, 128))

    es = contextlib.ExitStack()
    with es:
        c = Ctx(nc, es, phases)
        _emit_prologue(c, dram)
        for ph in phases:
            _emit_barrier(c)
            {"l0a": _emit_l0_attn, "l0f": _emit_l0_ffn,
             "l1a": _emit_l1_attn, "l1f": _emit_l1_ffn}[ph](c, dram)
        _emit_barrier(c)
        _emit_epilogue(c, dram)
    return nc


_INPUT_KEYS = {
    "l0a": ["wrT", "wiT", "woT", "cwc", "cb", "wrb", "wib", "la", "nw", "now"],
    "l0f": ["Wg", "W1_0", "W2_0", "G64", "OH8"],
    "l1a": ["Wq", "Wk", "Wv", "gamma", "beta", "biasT"],
    "l1f": ["rw", "rb", "W1_1", "b1c", "W2_1", "b2", "OH16"],
}


def kernel(**inputs):
    host = _prep_host(inputs)
    x0 = np.ascontiguousarray(np.asarray(inputs["L0_attn_x"], dtype=np.float32))
    cur = [np.ascontiguousarray(x0[b]) for b in range(B)]
    for phases in PHASE_GROUPS:
        nc = build_program(phases)
        shared = {"ident": host["ident"], "onecol": host["onecol"],
                  "onerow": host["onerow"]}
        for ph in phases:
            for k in _INPUT_KEYS[ph]:
                shared[k] = host[k]
        in_maps = []
        for b in range(B):
            m = dict(shared)
            if phases[0] == "l0a":
                m["x0"] = cur[b]
            else:
                m["xin"] = cur[b]
            in_maps.append(m)
        res = run_bass_kernel_spmd(nc, in_maps, core_ids=list(range(B)))
        if phases[-1] == "l1f":
            return np.stack([res.results[b]["y"] for b in range(B)], axis=0)
        cur = [np.ascontiguousarray(res.results[b]["xout"]) for b in range(B)]
    raise RuntimeError("phase groups must end with l1f")


# version 09 (final)
# speedup vs baseline: 2.4444x; 1.0018x over previous
import contextlib
import numpy as np
import concourse.bass as bass
import concourse.mybir as mybir
from concourse.bass_utils import run_bass_kernel_spmd

F32 = mybir.dt.float32
F32R = mybir.dt.float32r
BF16 = mybir.dt.bfloat16
AF = mybir.ActivationFunctionType
ALU = mybir.AluOpType
AX = mybir.AxisListType

B, T, D = 8, 2048, 1024
NDT = D // 128
TC = 512
NTC = T // TC
NL = 3
KCONV = 4
NE0, F0 = 8, 2048
NE1, F1 = 16, 2048
NH, DH = 16, 64

PHASE_GROUPS = [["l0a", "l0f", "l1a", "l1f"]]
_DBG = {"on": False}

M_CW = 0
M_CB = 32
M_WRB = 40
M_WIB = 48
M_LSA = 56
M_LA = 64
M_NW = 72
M_NOW = 80
M_GAM = 88
M_BET = 96
M_B2 = 104


class Ctx:
    def __init__(self, nc, es, phases):
        self.nc = nc
        self.es = es
        self.phases = phases
        sb = lambda name, shape, dt: es.enter_context(nc.sbuf_tensor(name, shape, dt))
        self.XR = sb("XR", [128, NDT * T], F32R)
        self.BA = sb("BA", [128, NDT * T], BF16)
        self.BB = sb("BB", [128, NDT * T], BF16)
        self.SC = sb("SC", [128, 4 * T], F32)
        self.WB = sb("WB", [128, 8192], BF16)
        self.WR = sb("WR", [128, 3072], F32R)
        self.ONES = sb("ONES", [128, T], F32)
        self.MISC = sb("MISC", [128, 1024], F32)
        self.MR = sb("MR", [128, 1664], F32R)
        self.OB = sb("OB", [128, 2], BF16)
        self.IDT = sb("IDT", [128, 128], F32)
        self.PSA = es.enter_context(nc.psum_tensor("psA", [128, 2048], F32))
        self.PSB = es.enter_context(nc.psum_tensor("psB", [128, 2048], F32))
        self.PS = [self.PSA[:, i * 512:(i + 1) * 512] for i in range(4)] + \
                  [self.PSB[:, i * 512:(i + 1) * 512] for i in range(4)]
        self.sem = {}
        self.cnt = {}
        for name in ["T", "V", "S", "G", "Y", "O", "C"]:
            self.sem[name] = es.enter_context(nc.semaphore("sem" + name))
            self.cnt[name] = 0
        self.block = es.enter_context(nc.Block())

    def inc(self, ins, name, n=1):
        self.cnt[name] += n
        ins.then_inc(self.sem[name], n)
        return self.cnt[name]

    def wait(self, eng, *names):
        for n in names:
            if self.cnt[n] > 0:
                eng.wait_ge(self.sem[n], self.cnt[n])

    def wait_at(self, eng, name, val):
        if val > 0:
            eng.wait_ge(self.sem[name], val)

    def pb(self, eng, ins):
        self.inc(ins, "V")
        eng.wait_ge(self.sem["V"], self.cnt["V"])

    @staticmethod
    def dcols(j, c0=0, c1=T):
        return slice(j * T + c0, j * T + c1)

    def t1(self, c0=0, c1=T): return self.SC[:, 0 * T + c0:0 * T + c1]
    def t2(self, c0=0, c1=T): return self.SC[:, 1 * T + c0:1 * T + c1]
    def t3(self, c0=0, c1=T): return self.SC[:, 2 * T + c0:2 * T + c1]
    def t4(self, c0=0, c1=T): return self.SC[:, 3 * T + c0:3 * T + c1]

    def mcol(self, base, j):
        return self.MISC[:, base + j:base + j + 1]



def _col8(v):
    v = np.asarray(v)
    return np.ascontiguousarray(v.reshape(-1, 128).T)


def _prep_host(inp):
    d = {}
    d["wrT"] = np.ascontiguousarray(np.asarray(inp["L0_attn_wr_w"]).transpose(0, 2, 1))
    d["wiT"] = np.ascontiguousarray(np.asarray(inp["L0_attn_wi_w"]).transpose(0, 2, 1))
    d["woT"] = np.ascontiguousarray(np.asarray(inp["L0_attn_wo_w"]).transpose(0, 2, 1))
    cw = np.asarray(inp["L0_attn_conv_w"])
    d["cwc"] = np.ascontiguousarray(
        np.stack([_col8(cw[l].T.reshape(-1)) for l in range(NL)]))
    for k_src, k_dst in [("L0_attn_conv_b", "cb"), ("L0_attn_wr_b", "wrb"),
                         ("L0_attn_wi_b", "wib"), ("L0_attn_log_a", "la"),
                         ("L0_attn_norm_w", "nw")]:
        v = np.asarray(inp[k_src])
        d[k_dst] = np.ascontiguousarray(np.stack([_col8(v[l]) for l in range(NL)]))
    d["now"] = _col8(inp["L0_attn_norm_out_w"])
    d["Wg"] = np.ascontiguousarray(np.asarray(inp["L0_ffn_Wg"]))
    d["W1_0"] = np.ascontiguousarray(np.asarray(inp["L0_ffn_W1"]))
    d["W2_0"] = np.ascontiguousarray(np.asarray(inp["L0_ffn_W2"]))
    d["Wq"] = np.ascontiguousarray(np.asarray(inp["L1_attn_Wq"]))
    d["Wk"] = np.ascontiguousarray(np.asarray(inp["L1_attn_Wk"]))
    d["Wv"] = np.ascontiguousarray(np.asarray(inp["L1_attn_Wv"]))
    d["gamma"] = _col8(inp["L1_attn_gamma"])
    d["beta"] = _col8(inp["L1_attn_beta"])
    rt = np.asarray(inp["L1_attn_rel_table"])
    r = np.arange(T)
    rel_index = np.clip(r[None, :] - r[:, None] + (T - 1), 0, 2 * T - 2)
    rel_index = np.minimum(rel_index, rt.shape[0] - 1)
    bias0 = rt[rel_index]
    biasg = bias0.reshape(-1).reshape(NH, T, T)
    d["biasT"] = np.ascontiguousarray(biasg.transpose(0, 2, 1))
    d["rw"] = np.ascontiguousarray(np.asarray(inp["L1_ffn_router_w"]))
    d["rb"] = np.ascontiguousarray(np.asarray(inp["L1_ffn_router_b"]))
    d["W1_1"] = np.ascontiguousarray(np.asarray(inp["L1_ffn_w1"]))
    b1 = np.asarray(inp["L1_ffn_b1"])
    d["b1c"] = np.ascontiguousarray(np.stack([_col8(b1[e]) for e in range(NE1)]))
    d["W2_1"] = np.ascontiguousarray(np.asarray(inp["L1_ffn_w2"]))
    d["b2"] = np.ascontiguousarray(np.asarray(inp["L1_ffn_b2"]))
    d["ident"] = np.eye(128, dtype=np.float32)
    d["onecol"] = np.ones((128, 1), np.float32)
    d["onerow"] = np.ones((1, 128), np.float32)
    i8 = np.arange(64) % NE0
    d["G64"] = (i8[:, None] == i8[None, :]).astype(np.float32)
    oh8 = np.zeros((NE0, NE0 * 128), np.float32)
    for e in range(NE0):
        oh8[e, e * 128:(e + 1) * 128] = 1.0
    d["OH8"] = oh8
    oh16 = np.zeros((NE1, NE1 * 128), np.float32)
    for e in range(NE1):
        oh16[e, e * 128:(e + 1) * 128] = 1.0
    d["OH16"] = oh16
    return d



def _emit_norm_fm(c, buf, out_buf, eps, subtract_mean=True, scale_col_of=None,
                  gb=False, src_bf16=False):
    eps_col = c.MISC[:, {1e-6: 121, 1e-5: 122, 1e-3: 123}[eps]:
                        {1e-6: 121, 1e-5: 122, 1e-3: 123}[eps] + 1]
    """out = (buf - mu)/sqrt(var+eps) over the 1024-feature (partition) dim.
    scale_col_of(j): per-feature multiplier fused into the apply (rms nw).
    gb: apply gamma/beta columns from MISC. Processes per 512-token chunk.
    Row scratch: SC[0:1, 4096+q*512] for q in (sx, sq, mu, var).
    """
    nc, blk = c.nc, c.block
    onec = c.OB[:, 0:1] if src_bf16 else c.MR[:, 0:1]
    oner = c.MR[0:1, 1:129]
    inv_n = 1.0 / D
    PS_X, PS_Q = c.PS[6], c.PS[7]
    R_SX = c.SC[0:1, 4096:4608]
    R_MU = c.SC[0:1, 5120:5632]
    R_VR = c.SC[0:1, 5632:6144]
    R_RSTD = c.MR[0:1, 256:768]
    R_ROWB = c.MR[0:1, 768:1280]
    SQV = c.SC.bitcast(BF16)
    for tc in range(NTC):
        @blk.scalar
        def _(eng, tc=tc):
            c.wait(eng, "V", "T")
            for j in range(NDT):
                ins = eng.activation(
                    SQV[:, j * TC:(j + 1) * TC],
                    buf[:, c.dcols(j, tc * TC, (tc + 1) * TC)], AF.Square)
            c.inc(ins, "S")

        @blk.tensor
        def _(eng, tc=tc):
            c.wait(eng, "S", "V")
            if subtract_mean:
                for j in range(NDT):
                    eng.matmul(PS_X[0:1, 0:TC], onec,
                               buf[:, c.dcols(j, tc * TC, (tc + 1) * TC)],
                               start=(j == 0), stop=(j == NDT - 1))
            for j in range(NDT):
                ins = eng.matmul(PS_Q[0:1, 0:TC], c.OB[:, 0:1],
                                 SQV[:, j * TC:(j + 1) * TC],
                                 start=(j == 0), stop=(j == NDT - 1))
            c.inc(ins, "T")

        @blk.vector
        def _(eng, tc=tc):
            c.wait(eng, "T", "S")
            if subtract_mean:
                eng.tensor_scalar(R_MU, PS_X[0:1, 0:TC], inv_n, None, ALU.mult)
                eng.tensor_scalar(R_VR, PS_Q[0:1, 0:TC], inv_n, None, ALU.mult)
                eng.tensor_tensor(R_SX, R_MU, R_MU, ALU.mult)
                ins = eng.tensor_tensor(R_VR, R_VR, R_SX, ALU.subtract)
            else:
                ins = eng.tensor_scalar(R_VR, PS_Q[0:1, 0:TC], inv_n, None, ALU.mult)
            c.inc(ins, "V")

        @blk.scalar
        def _(eng, tc=tc):
            c.wait(eng, "V")
            ins = eng.activation(R_VR, R_VR, AF.Ln, bias=eps_col[0:1, 0:1])
            c.inc(ins, "S")
            eng.wait_ge(c.sem["S"], c.cnt["S"])
            ins = eng.activation(R_RSTD, R_VR, AF.Exp, scale=-0.5)
            c.inc(ins, "S")

        if subtract_mean:
            @blk.vector
            def _(eng, tc=tc):
                c.wait(eng, "S")
                eng.tensor_tensor(R_MU, R_MU, R_RSTD, ALU.mult)
                ins = eng.tensor_scalar(R_ROWB, R_MU, -1.0, None, ALU.mult)
                c.inc(ins, "V")

        @blk.tensor
        def _(eng, tc=tc):
            c.wait(eng, "V", "S")
            ins = eng.matmul(PS_X[:, 0:TC], oner, R_RSTD,
                             start=True, stop=True)
            if subtract_mean:
                ins = eng.matmul(PS_Q[:, 0:TC], oner, R_ROWB,
                                 start=True, stop=True)
            c.inc(ins, "T")

        @blk.scalar
        def _(eng, tc=tc):
            c.wait(eng, "T", "V")
            ins = eng.activation(c.SC[:, 3584:4096], PS_X[:, 0:TC], AF.Identity)
            c.inc(ins, "S")

        @blk.vector
        def _(eng, tc=tc):
            c.wait(eng, "T", "S")
            rbc = c.SC[:, 3584:4096]
            tmp = c.SC[:, 3072:3584]
            for j in range(NDT):
                src = buf[:, c.dcols(j, tc * TC, (tc + 1) * TC)]
                dst = out_buf[:, c.dcols(j, tc * TC, (tc + 1) * TC)]
                if scale_col_of is not None:
                    ins = eng.scalar_tensor_tensor(dst, src, scale_col_of(j), rbc,
                                                   ALU.mult, ALU.mult)
                elif subtract_mean:
                    eng.tensor_tensor(tmp, src, rbc, ALU.mult)
                    if gb:
                        eng.tensor_tensor(tmp, tmp, PS_Q[:, 0:TC], ALU.add)
                        ins = eng.tensor_scalar(dst, tmp, c.mcol(M_GAM, j),
                                                c.mcol(M_BET, j), ALU.mult, ALU.add)
                    else:
                        ins = eng.tensor_tensor(dst, tmp, PS_Q[:, 0:TC], ALU.add)
                else:
                    ins = eng.tensor_tensor(dst, src, rbc, ALU.mult)
            c.inc(ins, "V")



def _emit_l0_attn(c, dram):
    nc, blk = c.nc, c.block
    xc_buf = [c.BA, c.BB, c.BA]
    h_buf = [c.BB, c.BA, c.BB]
    DGB = c.WB[:, 7168:7680]

    def load_cols(dst_c0, dst_c1, src):
        @blk.sync
        def _(eng):
            c.wait(eng, "V", "S", "T")
            ins = eng.dma_start(out=c.MISC[:, dst_c0:dst_c1], in_=src)
            c.inc(ins, "Y", 16)

    ltok = {}
    for l in range(NL):
        XC = xc_buf[l]
        H = h_buf[l]
        if l == 0:
            xin = h_buf[0]
            for tc in range(NTC):
                @blk.vector
                def _(eng, tc=tc):
                    c.wait(eng, "G", "S", "T")
                    for j in range(NDT):
                        ins = eng.tensor_copy(
                            xin[:, c.dcols(j, tc * TC, (tc + 1) * TC)],
                            c.XR[:, c.dcols(j, tc * TC, (tc + 1) * TC)])
                    c.inc(ins, "V")
        else:
            xin = xc_buf[l - 1]
        load_cols(M_CW, M_CW + 32, dram["cwc"][l])
        load_cols(M_CB, M_CB + 8, dram["cb"][l])
        load_cols(M_WRB, M_WRB + 8, dram["wrb"][l])
        load_cols(M_WIB, M_WIB + 8, dram["wib"][l])
        load_cols(M_LA, M_LA + 8, dram["la"][l])
        load_cols(M_NW, M_NW + 8, dram["nw"][l])

        @blk.scalar
        def _(eng, l=l):
            c.wait(eng, "Y", "V")
            ins = eng.activation(c.MISC[:, M_LSA:M_LSA + 8], c.MISC[:, M_LA:M_LA + 8],
                                 AF.Sigmoid)
            c.inc(ins, "S")
            eng.wait_ge(c.sem["S"], c.cnt["S"])
            ins = eng.activation(c.MISC[:, M_LSA:M_LSA + 8], c.MISC[:, M_LSA:M_LSA + 8],
                                 AF.Ln)
            c.inc(ins, "S")

        for j in range(NDT):
            @blk.vector
            def _(eng, l=l, j=j):
                c.wait(eng, "Y")
                c.wait_at(eng, "T", ltok.get("convt", 0))
                dg = DGB
                for k in range(KCONV):
                    ins = eng.tensor_scalar(dg[:, k * 128:(k + 1) * 128], c.IDT[:],
                                            c.mcol(M_CW, k * NDT + j), None, ALU.mult)
                ltok["diag"] = c.inc(ins, "V")

            @blk.tensor
            def _(eng, l=l, j=j, xin=xin):
                c.wait_at(eng, "V", ltok["diag"])
                c.wait_at(eng, "S", ltok.get("convs", 0))
                c.wait_at(eng, "S", ltok.get("wos", 0))
                if l == 0 and j == 0:
                    c.wait(eng, "V", "G")
                dg = DGB
                for tc in range(NTC):
                    bank = c.PS[tc]
                    for k in (3, 2, 1, 0):
                        off = KCONV - 1 - k
                        lo = tc * TC - off
                        if lo >= 0:
                            ins = eng.matmul(bank[:, 0:TC], dg[:, k * 128:(k + 1) * 128],
                                             xin[:, c.dcols(j, lo, lo + TC)],
                                             start=(k == 3), stop=(k == 0))
                        else:
                            ins = eng.matmul(bank[:, off:TC], dg[:, k * 128:(k + 1) * 128],
                                             xin[:, c.dcols(j, 0, TC - off)],
                                             start=(k == 3), stop=(k == 0))
                ltok["convt"] = c.inc(ins, "T")

            @blk.scalar
            def _(eng, j=j, XC=XC):
                c.wait_at(eng, "T", ltok["convt"])
                ins = eng.activation(XC[:, c.dcols(j)], c.PSA[:, 0:2048],
                                     AF.Identity, bias=c.mcol(M_CB, j))
                ltok["convs"] = c.inc(ins, "S")

        wtok = {}

        def load_wblock(kind, l_, jj, slot):
            src = dram[kind][l_][:, jj * 128:(jj + 1) * 128]
            src = src.rearrange("(kt p) o -> p kt o", p=128)

            @blk.gpsimd
            def _(eng):
                c.wait(eng, "T")
                ins = eng.dma_start(out=c.WB[:, slot * 1024:(slot + 1) * 1024], in_=src)
                wtok[(kind, jj)] = c.inc(ins, "G", 16)

        load_wblock("wrT", l, 0, 0)
        load_wblock("wiT", l, 0, 2)

        for j in range(NDT):
            if j + 1 < NDT:
                load_wblock("wrT", l, j + 1, (j + 1) % 2)
                load_wblock("wiT", l, j + 1, 2 + (j + 1) % 2)
            for kind, slot_base, bank_base, bias_c, dst, tk in (
                    ("wrT", 0, 4, M_WRB, c.t1, "r"), ("wiT", 2, 0, M_WIB, c.t4, "i")):
                slot = slot_base + j % 2

                @blk.tensor
                def _(eng, j=j, slot=slot, kind=kind, bank_base=bank_base, XC=XC, tk=tk):
                    c.wait_at(eng, "G", wtok[(kind, j)])
                    c.wait_at(eng, "S", ltok.get(tk + "s", 0))
                    if j == 0:
                        c.wait(eng, "S", "V")
                    for tc in range(NTC):
                        bank = c.PS[bank_base + tc]
                        for kt in range(NDT):
                            ins = eng.matmul(
                                bank[:, 0:TC],
                                c.WB[:, slot * 1024 + kt * 128:slot * 1024 + (kt + 1) * 128],
                                XC[:, c.dcols(kt, tc * TC, (tc + 1) * TC)],
                                start=(kt == 0), stop=(kt == NDT - 1))
                    ltok[tk + "t"] = c.inc(ins, "T")

                @blk.scalar
                def _(eng, j=j, bias_c=bias_c, dst=dst, bank_base=bank_base, tk=tk):
                    c.wait_at(eng, "T", ltok[tk + "t"])
                    c.wait_at(eng, "V", ltok.get("chainv", 0))
                    psrc = c.PSB if bank_base == 4 else c.PSA
                    ins = eng.activation(dst(), psrc[:, 0:2048],
                                         AF.Sigmoid, bias=c.mcol(bias_c, j))
                    ltok[tk + "s"] = c.inc(ins, "S")

            @blk.vector
            def _(eng, j=j):
                c.wait(eng, "S")
                eng.tensor_scalar(c.t2(), c.t1(), c.mcol(M_LSA, j), 8.0,
                                  ALU.mult, ALU.mult)
                eng.tensor_tensor_scan(c.t1(), c.ONES[:], c.t2(), 0.0, ALU.mult, ALU.add)
                ins = eng.tensor_scalar(c.t1(), c.t1(), c.MISC[:, 120:121], None, ALU.max)
                c.inc(ins, "V")

            @blk.scalar
            def _(eng, j=j):
                c.wait(eng, "V")
                eng.activation(c.t3(), c.t1(), AF.Exp)
                eng.activation(c.t1(), c.t1(), AF.Exp, scale=-1.0)
                eng.activation(c.t2(), c.t2(), AF.Exp)
                ins = eng.activation(c.t2(), c.t2(), AF.Square)
                c.inc(ins, "S")

            @blk.vector
            def _(eng, j=j):
                c.wait(eng, "S")
                eng.tensor_scalar(c.t2(), c.t2(), -1.0, 1.0, ALU.mult, ALU.add)
                ins = eng.tensor_scalar(c.t2(), c.t2(), c.MISC[:, 121:122], None, ALU.max)
                c.inc(ins, "V")

            @blk.scalar
            def _(eng, j=j):
                c.wait(eng, "V")
                eng.activation(c.t2(), c.t2(), AF.Ln)
                ins = eng.activation(c.t2(), c.t2(), AF.Exp, scale=0.5)
                c.inc(ins, "S")

            @blk.vector
            def _(eng, j=j, XC=XC, H=H):
                c.wait(eng, "S")
                eng.tensor_tensor(c.t4(), c.t4(), XC[:, c.dcols(j)], ALU.mult)
                eng.tensor_tensor(c.t4(), c.t4(), c.t2(), ALU.mult)
                eng.tensor_tensor(c.t4(), c.t4(), c.t1(), ALU.mult)
                eng.tensor_tensor_scan(c.t2(), c.ONES[:], c.t4(), 0.0, ALU.mult, ALU.add)
                ins = eng.tensor_tensor(H[:, c.dcols(j)], c.t3(), c.t2(), ALU.mult)
                ltok["chainv"] = c.inc(ins, "V")

        load_wblock("woT", l, 0, 4)
        for j in range(NDT):
            if j + 1 < NDT:
                load_wblock("woT", l, j + 1, 4 + (j + 1) % 2)
            slot = 4 + j % 2

            @blk.tensor
            def _(eng, j=j, slot=slot, H=H):
                c.wait_at(eng, "G", wtok[("woT", j)])
                c.wait_at(eng, "S", ltok.get("wos", 0))
                if j == 0:
                    c.wait(eng, "V", "S")
                for tc in range(NTC):
                    bank = c.PS[tc]
                    for kt in range(NDT):
                        ins = eng.matmul(
                            bank[:, 0:TC],
                            c.WB[:, slot * 1024 + kt * 128:slot * 1024 + (kt + 1) * 128],
                            H[:, c.dcols(kt, tc * TC, (tc + 1) * TC)],
                            start=(kt == 0), stop=(kt == NDT - 1))
                ltok["wot"] = c.inc(ins, "T")

            @blk.scalar
            def _(eng, j=j, XC=XC):
                c.wait_at(eng, "T", ltok["wot"])
                ins = eng.activation(XC[:, c.dcols(j)], c.PSA[:, 0:2048], AF.Identity)
                ltok["wos"] = c.inc(ins, "S")

        _emit_norm_fm(c, XC, XC, 1e-6, subtract_mean=False,
                      scale_col_of=lambda j: c.mcol(M_NW, j), src_bf16=True)

    load_cols(M_NOW, M_NOW + 8, dram["now"])
    yb = xc_buf[NL - 1]
    _emit_norm_fm(c, yb, yb, 1e-6, subtract_mean=False,
                  scale_col_of=lambda j: c.mcol(M_NOW, j), src_bf16=True)
    for tc in range(NTC):
        @blk.vector
        def _(eng, tc=tc):
            c.wait(eng, "V", "S", "G")
            for j in range(NDT):
                ins = eng.tensor_tensor(c.XR[:, c.dcols(j, tc * TC, (tc + 1) * TC)],
                                        c.XR[:, c.dcols(j, tc * TC, (tc + 1) * TC)],
                                        yb[:, c.dcols(j, tc * TC, (tc + 1) * TC)], ALU.add)
            c.inc(ins, "V")
    _emit_norm_fm(c, c.XR, c.XR, 1e-5, subtract_mean=True)


def _emit_l0_ffn(c, dram):
    nc, blk = c.nc, c.block
    BBF = c.BB.bitcast(F32)
    PR = lambda ch: c.SC[:, ch * 8:(ch + 1) * 8]
    PALL = c.SC[0:64, 128:2176]
    IND = c.SC[0:64, 2176:4224]
    TH = c.SC[0:64, 4224:4225]
    PLOC = c.SC[0:8, 5120:7168]

    @blk.gpsimd
    def _(eng):
        c.wait(eng, "T", "V", "S")
        ins = eng.dma_start(out=c.MR[:, 1472:1536],
                            in_=dram["Wg"].rearrange("(kt p) e -> p kt e", p=128))
        c.inc(ins, "G", 16)

        ins = eng.dma_start(out=c.BB[0:8, 2560:3584], in_=dram["OH8"])
        c.inc(ins, "G", 16)

    for ch in range(16):
        @blk.tensor
        def _(eng, ch=ch):
            c.wait(eng, "G", "V", "S")
            for kt in range(NDT):
                ins = eng.matmul(c.PS[2][:, 0:8],
                                 c.XR[:, c.dcols(kt, ch * 128, (ch + 1) * 128)],
                                 c.MR[:, 1472 + kt * 8:1472 + (kt + 1) * 8],
                                 start=(kt == 0), stop=(kt == NDT - 1))
            c.inc(ins, "T")

        @blk.vector
        def _(eng, ch=ch):
            c.wait(eng, "T", "S")
            mx = c.SC[:, 4288:4289]
            c.pb(eng, eng.tensor_reduce(mx, c.PS[2][:, 0:8], AX.X, ALU.max))
            ins = eng.tensor_scalar(c.SC[:, 4272:4280], c.PS[2][:, 0:8], mx, None,
                                    ALU.subtract)
            c.inc(ins, "V")

        @blk.scalar
        def _(eng, ch=ch):
            c.wait(eng, "V")
            ins = eng.activation(c.SC[:, 4272:4280], c.SC[:, 4272:4280], AF.Exp)
            c.inc(ins, "S")

        @blk.vector
        def _(eng, ch=ch):
            c.wait(eng, "S")
            sm = c.SC[:, 4289:4290]
            c.pb(eng, eng.tensor_reduce(sm, c.SC[:, 4272:4280], AX.X, ALU.add))
            c.pb(eng, eng.reciprocal(sm, sm))
            ins = eng.tensor_scalar(PR(ch), c.SC[:, 4272:4280], sm, None, ALU.mult)
            c.inc(ins, "V")

        @blk.tensor
        def _(eng, ch=ch):
            c.wait(eng, "V")
            ins = eng.transpose(c.PS[3][0:8, 0:128], PR(ch), c.IDT[:])
            c.inc(ins, "T")

        @blk.vector
        def _(eng, ch=ch):
            c.wait(eng, "T")
            ins = eng.tensor_copy(PLOC[:, ch * 128:(ch + 1) * 128], c.PS[3][0:8, 0:128])
            c.inc(ins, "V")

    @blk.sync
    def _(eng):
        c.wait(eng, "V")
        ins = eng.dma_start(out=dram["cc_in"], in_=PLOC)
        c.inc(ins, "Y", 16)
        ins = eng.dma_start(out=c.SC[0:64, 4320:4384], in_=dram["G64"])
        c.inc(ins, "Y", 16)

    @blk.gpsimd
    def _(eng):
        c.wait(eng, "Y")
        ins = eng.collective_compute(
            "AllGather", ALU.bypass, replica_groups=[list(range(B))],
            ins=[dram["cc_in"].opt()], outs=[dram["cc_out"].opt()])
        c.inc(ins, "C")
        eng.wait_ge(c.sem["C"], c.cnt["C"])
        ins = eng.dma_start(out=PALL, in_=dram["cc_out"])
        c.inc(ins, "G", 16)

    @blk.vector
    def _(eng):
        c.wait(eng, "G")
        ins = eng.memset(TH, 0.5)
        c.inc(ins, "V")
    NIT = 26
    for it in range(NIT):
        step = 0.5 ** (it + 2)

        @blk.vector
        def _(eng, it=it):
            c.wait(eng, "T", "V")
            c.pb(eng, eng.tensor_scalar(IND, PALL, TH, None, ALU.is_gt))
            ins = eng.tensor_reduce(c.SC[0:64, 4384:4385], IND, AX.X, ALU.add)
            c.inc(ins, "V")

        @blk.tensor
        def _(eng, it=it):
            c.wait(eng, "V")
            if it == 0:
                c.wait(eng, "Y")
            ins = eng.matmul(c.PS[2][0:64, 0:1], c.SC[0:64, 4320:4384],
                             c.SC[0:64, 4384:4385], start=True, stop=True)
            c.inc(ins, "T")

        @blk.vector
        def _(eng, it=it, step=step):
            c.wait(eng, "T")
            d = c.SC[0:64, 4226:4227]
            c.pb(eng, eng.tensor_scalar(d, c.PS[2][0:64, 0:1], 2048.0, None, ALU.is_gt))
            c.pb(eng, eng.tensor_scalar(d, d, 2.0, -1.0, ALU.mult, ALU.add))
            ins = eng.scalar_tensor_tensor(TH, d, float(step), TH, ALU.mult, ALU.add)
            c.inc(ins, "V")

    @blk.vector
    def _(eng):
        ins = eng.tensor_scalar(TH, TH, 4.0 * (0.5 ** (NIT + 1)), None, ALU.add)
        c.inc(ins, "V")

    @blk.tensor
    def _(eng):
        c.wait(eng, "V")
        ins = eng.transpose(c.PS[2][0:1, 0:8], TH[0:8, 0:1], c.IDT[0:8, 0:8])
        c.inc(ins, "T")

    @blk.vector
    def _(eng):
        c.wait(eng, "T")
        ins = eng.tensor_copy(c.SC[0:1, 4400:4408], c.PS[2][0:1, 0:8])
        c.inc(ins, "V")

    @blk.tensor
    def _(eng):
        c.wait(eng, "V")
        ins = eng.matmul(c.PS[2][:, 0:8], c.ONES[0:1, 0:128], c.SC[0:1, 4400:4408],
                         start=True, stop=True)
        c.inc(ins, "T")

    @blk.scalar
    def _(eng):
        c.wait(eng, "T")
        ins = eng.activation(c.SC[:, 4240:4248], c.PS[2][:, 0:8], AF.Identity)
        c.inc(ins, "S")

    for ch in range(16):
        @blk.vector
        def _(eng, ch=ch):
            c.wait(eng, "S", "T")
            thb = c.SC[:, 4240:4248]
            sel = c.SC[:, 4256:4264]
            tB = c.SC[:, 4272:4280]
            nselc = c.SC[:, 4288:4289]
            asg = c.SC[:, 4289:4290]
            mxs = c.SC[:, 4290:4291]
            mxa = c.SC[:, 4291:4292]
            wv = c.SC[:, 4292:4293]
            inda = c.SC[:, 4304:4312]
            indb = c.SC[:, 4312:4320]
            c.pb(eng, eng.tensor_tensor(sel, PR(ch), thb, ALU.is_gt))
            c.pb(eng, eng.tensor_reduce(nselc, sel, AX.X, ALU.add))
            c.pb(eng, eng.tensor_scalar(asg, nselc, 0.5, None, ALU.is_gt))
            c.pb(eng, eng.tensor_scalar(tB, PR(ch), 10000.0, None, ALU.add))
            c.pb(eng, eng.tensor_tensor(tB, tB, sel, ALU.mult))
            c.pb(eng, eng.tensor_scalar(tB, tB, 10000.0, None, ALU.subtract))
            c.pb(eng, eng.tensor_reduce(mxs, tB, AX.X, ALU.max))
            c.pb(eng, eng.tensor_reduce(mxa, PR(ch), AX.X, ALU.max))
            c.pb(eng, eng.tensor_tensor(wv, mxs, mxa, ALU.subtract))
            c.pb(eng, eng.scalar_tensor_tensor(wv, wv, asg, mxa, ALU.mult, ALU.add))
            c.pb(eng, eng.tensor_scalar(inda, tB, mxs, None, ALU.is_equal))
            c.pb(eng, eng.tensor_scalar(indb, PR(ch), mxa, None, ALU.is_equal))
            c.pb(eng, eng.tensor_tensor(inda, inda, indb, ALU.subtract))
            c.pb(eng, eng.scalar_tensor_tensor(inda, inda, asg, indb, ALU.mult, ALU.add))
            ins = eng.tensor_scalar(inda, inda, wv, None, ALU.mult)
            c.inc(ins, "V")

        @blk.tensor
        def _(eng, ch=ch):
            c.wait(eng, "V")
            ins = eng.transpose(c.PS[3][0:8, 0:128], c.SC[:, 4304:4312], c.IDT[:])
            c.inc(ins, "T")

        @blk.vector
        def _(eng, ch=ch):
            c.wait(eng, "T")
            ins = eng.tensor_copy(c.BB[0:8, ch * 128:(ch + 1) * 128],
                                  c.PS[3][0:8, 0:128])
            c.inc(ins, "V")

    if _DBG["on"]:
        @blk.sync
        def _(eng):
            c.wait(eng, "V", "S", "T")
            eng.dma_start(out=dram["dbg_pr"], in_=c.SC[:, 0:128]).then_inc(c.sem["O"], 16)
            eng.dma_start(out=dram["dbg_th"], in_=TH).then_inc(c.sem["O"], 16)
            c.cnt["O"] += 32

        @blk.gpsimd
        def _(eng):
            c.wait(eng, "V")
            ins = eng.dma_start(out=dram["dbg_gate"], in_=c.BB[0:8, 0:2048])
            c.inc(ins, "G", 16)

    wtok = {}
    tok = {"gbt": 0, "gbc": 0, "comb": 0}

    ctr = {"w1": 0, "w2": 0}
    for s_ in range(3):
        tok[("w1u", s_)] = c.cnt["T"]
        tok[("w2u", s_)] = c.cnt["T"]

    def load_w1(e, fc, _slot_unused=None):
        slot = ctr["w1"] % 3
        ctr["w1"] += 1
        wtok[("w1s", e, fc)] = slot
        src = dram["W1_0"][e][:, fc * 128:(fc + 1) * 128]
        src = src.rearrange("(kt p) f -> p kt f", p=128)

        @blk.gpsimd
        def _(eng, slot=slot):
            c.wait_at(eng, "T", tok.get(("w1u", slot), 0))
            ins = eng.dma_start(out=c.WR[:, slot * 1024:(slot + 1) * 1024], in_=src)
            wtok[("w1", e, fc)] = c.inc(ins, "G", 16)

    def load_w2(e, oc, _slot_unused=None):
        slot = ctr["w2"] % 3
        ctr["w2"] += 1
        wtok[("w2s", e, oc)] = slot
        src = dram["W2_0"][e][:, oc * 128:(oc + 1) * 128]
        src = src.rearrange("(fc p) o -> p fc o", p=128)

        @blk.gpsimd
        def _(eng, slot=slot):
            c.wait_at(eng, "T", tok.get(("w2u", slot), 0))
            ins = eng.dma_start(out=c.WB[:, slot * 2048:slot * 2048 + 2048], in_=src)
            wtok[("w2", e, oc)] = c.inc(ins, "G", 16)

    pf_s = {0: c.cnt["S"], 1: c.cnt["S"]}
    pf_v = {0: c.cnt["V"], 1: c.cnt["V"]}
    tok["gbc"] = c.cnt["S"]
    tok["comb"] = c.cnt["V"]
    for half in range(2):
        t0 = half * 1024
        load_w1(0, 0, 0)
        load_w1(0, 1, 1)
        for e in range(NE0):
            for fc in range(16):
                la, lb = (e, fc + 2) if fc + 2 < 16 else (e + 1, fc + 2 - 16)
                if la < NE0:
                    load_w1(la, lb, (fc + 2) % 3)
                pr = fc % 2
                bk0, bk1 = c.PS[2 * pr], c.PS[2 * pr + 1]

                @blk.tensor
                def _(eng, e=e, fc=fc, pr=pr, bk0=bk0, bk1=bk1, t0=t0):
                    c.wait_at(eng, "G", wtok[("w1", e, fc)])
                    c.wait_at(eng, "S", pf_s[pr])
                    c.wait_at(eng, "V", pf_v[pr])
                    slot = wtok[("w1s", e, fc)]
                    for tq, bank in ((0, bk0), (1, bk1)):
                        for kt in range(NDT):
                            ins = eng.matmul(
                                bank[:, 0:TC],
                                c.WR[:, slot * 1024 + kt * 128:slot * 1024 + (kt + 1) * 128],
                                c.XR[:, c.dcols(kt, t0 + tq * TC, t0 + (tq + 1) * TC)],
                                start=(kt == 0), stop=(kt == NDT - 1))
                    tok[("h1t", pr)] = tok[("w1u", slot)] = c.inc(ins, "T")

                @blk.scalar
                def _(eng, fc=fc, pr=pr, bk0=bk0):
                    c.wait_at(eng, "T", tok[("h1t", pr)])
                    ins = eng.activation(c.BA[:, fc * 1024:(fc + 1) * 1024],
                                         c.PSA[:, pr * 1024:(pr + 1) * 1024], AF.Gelu)
                    pf_s[pr] = c.inc(ins, "S")

            load_w2(e, 0, 0)
            load_w2(e, 1, 1)
            for oc in range(NDT):
                la, lb = (e, oc + 2) if oc + 2 < NDT else (e + 1, oc + 2 - NDT)
                if la < NE0:
                    load_w2(la, lb, (oc + 2) % 3)
                pr = oc % 2
                bk0, bk1 = c.PS[2 * pr], c.PS[2 * pr + 1]

                @blk.tensor
                def _(eng, e=e, oc=oc, pr=pr, bk0=bk0, bk1=bk1, half=half):
                    c.wait_at(eng, "G", wtok[("w2", e, oc)])
                    c.wait_at(eng, "S", max(pf_s[0], pf_s[1]))
                    c.wait_at(eng, "V", pf_v[pr])
                    slot = wtok[("w2s", e, oc)]
                    for tq, bank in ((0, bk0), (1, bk1)):
                        for fc in range(16):
                            ins = eng.matmul(
                                bank[:, 0:TC],
                                c.WB[:, slot * 2048 + fc * 128:slot * 2048 + (fc + 1) * 128],
                                c.BA[:, fc * 1024 + tq * TC:fc * 1024 + (tq + 1) * TC],
                                start=(fc == 0), stop=(fc == 15))
                    tok[("h2t", pr)] = tok[("w2u", slot)] = c.inc(ins, "T")
                    c.wait_at(eng, "S", tok["gbc"])
                    eng.matmul(c.PS[6][:, 0:TC],
                               c.BB[0:8, 2560 + e * 128:2560 + (e + 1) * 128],
                               c.BB[0:8, half * 1024:half * 1024 + TC],
                               start=True, stop=True)
                    ins = eng.matmul(c.PS[7][:, 0:TC],
                                     c.BB[0:8, 2560 + e * 128:2560 + (e + 1) * 128],
                                     c.BB[0:8, half * 1024 + TC:half * 1024 + 1024],
                                     start=True, stop=True)
                    tok["gbt"] = c.inc(ins, "T")

                @blk.scalar
                def _(eng):
                    c.wait_at(eng, "T", tok["gbt"])
                    c.wait_at(eng, "V", tok["comb"])
                    eng.activation(c.BB[:, 2048:2560], c.PS[6][:, 0:TC], AF.Identity)
                    ins = eng.activation(c.BB[:, 3584:4096], c.PS[7][:, 0:TC], AF.Identity)
                    tok["gbc"] = c.inc(ins, "S")

                @blk.vector
                def _(eng, e=e, oc=oc, pr=pr, bk0=bk0, bk1=bk1):
                    c.wait_at(eng, "T", tok[("h2t", pr)])
                    c.wait_at(eng, "S", tok["gbc"])
                    acc = c.SC[:, oc * 1024:(oc + 1) * 1024]
                    for tq, bank, gc in ((0, bk0, 2048), (1, bk1, 3584)):
                        a_ = acc[:, tq * TC:(tq + 1) * TC]
                        gb = c.BB[:, gc:gc + 512]
                        if e == 0:
                            ins = eng.tensor_tensor(a_, bank[:, 0:TC], gb, ALU.mult)
                        else:
                            tmp = c.BB[:, 4608:5120]
                            eng.tensor_tensor(tmp, bank[:, 0:TC], gb, ALU.mult)
                            ins = eng.tensor_tensor(a_, a_, tmp, ALU.add)
                    pf_v[pr] = tok["comb"] = c.inc(ins, "V")

        for j in range(NDT):
            @blk.vector
            def _(eng, j=j, t0=t0):
                c.wait(eng, "V", "S")
                ins = eng.tensor_tensor(
                    c.XR[:, c.dcols(j, t0, t0 + 1024)],
                    c.XR[:, c.dcols(j, t0, t0 + 1024)],
                    c.SC[:, j * 1024:(j + 1) * 1024], ALU.add)
                c.inc(ins, "V")

    _emit_norm_fm(c, c.XR, c.XR, 1e-5, subtract_mean=True)


def _emit_l1_attn(c, dram):
    nc, blk = c.nc, c.block
    QP = lambda c0, c1: c.BB[:, c0:c1]
    VW0 = 4096
    EXS = 6176
    BTR = 8224
    @blk.sync
    def _(eng):
        c.wait(eng, "V", "S", "T")
        ins = eng.dma_start(out=c.MISC[:, M_GAM:M_GAM + 8], in_=dram["gamma"])
        c.inc(ins, "Y", 16)
        ins = eng.dma_start(out=c.MISC[:, M_BET:M_BET + 8], in_=dram["beta"])
        c.inc(ins, "Y", 16)

    wtok = {}

    def load_wpair(kind, gp, slot):
        src = dram[kind][:, gp * 128:(gp + 1) * 128]
        src = src.rearrange("(kt p) o -> p kt o", p=128)

        @blk.gpsimd
        def _(eng):
            c.wait(eng, "T")
            ins = eng.dma_start(out=c.WR[:, slot * 1024:(slot + 1) * 1024], in_=src)
            wtok[(kind, gp)] = c.inc(ins, "G", 16)

    btok = {}

    def load_bias(g, kc, qc):
        src = dram["biasT"][g][kc * 128:(kc + 1) * 128, qc * TC:(qc + 1) * TC]

        @blk.gpsimd
        def _(eng, kc=kc):
            c.wait_at(eng, "T", bslot_tok.get(kc, 0))
            ins = eng.dma_start(out=c.WB[:, kc * TC:(kc + 1) * TC], in_=src)
            btok[(g, kc, qc)] = c.inc(ins, "G", 16)

    tok = {"qkc": c.cnt["S"], "vv": 0, "pv": 0, "cmb": c.cnt["V"],
           "rcp": c.cnt["V"], "bcc": c.cnt["S"]}
    exp_tok = {0: c.cnt["S"], 1: c.cnt["S"]}
    pvu_tok = {0: c.cnt["T"], 1: c.cnt["T"]}
    bslot_tok = {}
    for kc_ in range(16):
        bslot_tok[kc_] = c.cnt["T"]
    for gp in range(NH // 2):
        for idx, kind in enumerate(("Wq", "Wk", "Wv")):
            load_wpair(kind, gp, idx)
        for idx, kind in enumerate(("Wq", "Wk")):
            @blk.tensor
            def _(eng, kind=kind, idx=idx, gp=gp):
                c.wait_at(eng, "G", wtok[(kind, gp)])
                c.wait_at(eng, "S", tok["qkc"])
                c.wait_at(eng, "S", exp_tok[0])
                c.wait_at(eng, "S", exp_tok[1])
                c.wait_at(eng, "V", tok["cmb"])
                for tc in range(NTC):
                    for kt in range(NDT):
                        ins = eng.matmul(
                            c.PS[tc][:, 0:TC],
                            c.WR[:, idx * 1024 + kt * 128:idx * 1024 + (kt + 1) * 128],
                            c.XR[:, c.dcols(kt, tc * TC, (tc + 1) * TC)],
                            start=(kt == 0), stop=(kt == NDT - 1))
                tok["qkt"] = c.inc(ins, "T")

            @blk.scalar
            def _(eng, idx=idx):
                c.wait_at(eng, "T", tok["qkt"])
                ins = eng.activation(c.BB[:, idx * 2048:(idx + 1) * 2048], c.PSA[:, 0:2048], AF.Identity)
                tok["qkc"] = c.inc(ins, "S")

        @blk.vector
        def _(eng, gp=gp):
            c.wait_at(eng, "S", tok["qkc"])
            for tt in range(16):
                for h in range(2):
                    ins = eng.memset(c.BB[:, VW0 + (tt * 2 + h) * 65 + 64:VW0 + (tt * 2 + h) * 65 + 65], 1.0)
            tok["vz"] = c.inc(ins, "V")

        for tt in range(16):
            bank = c.PS[tt % 2]

            @blk.tensor
            def _(eng, tt=tt, bank=bank, gp=gp):
                c.wait_at(eng, "G", wtok[("Wv", gp)])
                if tt == 0:
                    c.wait_at(eng, "S", tok["qkc"])
                    c.wait_at(eng, "V", tok["vz"])
                else:
                    c.wait_at(eng, "V", tok.get(("vc", tt % 2), 0))
                for kt in range(NDT):
                    ins = eng.matmul(
                        bank[:, 0:128],
                        c.XR[:, c.dcols(kt, tt * 128, (tt + 1) * 128)],
                        c.WR[:, 2 * 1024 + kt * 128:2 * 1024 + (kt + 1) * 128],
                        start=(kt == 0), stop=(kt == NDT - 1))
                tok[("vt", tt % 2)] = c.inc(ins, "T")

            @blk.vector
            def _(eng, tt=tt, bank=bank):
                c.wait_at(eng, "T", tok[("vt", tt % 2)])
                eng.tensor_copy(c.BB[:, VW0 + (tt * 2) * 65:VW0 + (tt * 2) * 65 + 64],
                                bank[:, 0:64])
                ins = eng.tensor_copy(c.BB[:, VW0 + (tt * 2 + 1) * 65:VW0 + (tt * 2 + 1) * 65 + 64],
                                      bank[:, 64:128])
                tok[("vc", tt % 2)] = c.inc(ins, "V")

        for hh in range(2):
            g = gp * 2 + hh
            rb = hh * 64

            @blk.tensor
            def _(eng):
                eng.drain()

            for qc in range(NTC):
                PV = c.PS[4]
                BV = c.PS[5]
                def emit_scores(g2):
                    load_bias(g, g2 * 2, qc)
                    load_bias(g, g2 * 2 + 1, qc)
                    pr = g2 % 2
                    bk0, bk1 = c.PS[2 * pr], c.PS[2 * pr + 1]

                    @blk.tensor
                    def _(eng, g=g, rb=rb, qc=qc, g2=g2, pr=pr, bk0=bk0, bk1=bk1):
                        c.wait_at(eng, "S", exp_tok[pr])
                        if g2 < 2:
                            c.wait_at(eng, "V", tok.get(("vc", 0), 0))
                            c.wait_at(eng, "V", tok.get(("vc", 1), 0))
                        for i, bank in ((0, bk0), (1, bk1)):
                            kc = g2 * 2 + i
                            ins = eng.matmul(
                                bank[:, 0:TC],
                                c.BB[rb:rb + 64, 2048 + kc * 128:2048 + (kc + 1) * 128],
                                c.BB[rb:rb + 64, qc * TC:(qc + 1) * TC],
                                start=True, stop=True)
                        tok[("sct", pr)] = c.inc(ins, "T")

                def emit_exp(g2):
                    pr = g2 % 2

                    @blk.scalar
                    def _(eng, g2=g2, pr=pr):
                        c.wait_at(eng, "T", tok[("sct", pr)])
                        c.wait_at(eng, "T", pvu_tok[pr])
                        ins = eng.activation(
                            c.BB[:, EXS + pr * 1024:EXS + (pr + 1) * 1024],
                            c.PSA[:, pr * 1024:(pr + 1) * 1024], AF.Exp,
                            scale=float(D) ** -0.5)
                        exp_tok[pr] = c.inc(ins, "S")

                def emit_pv(g2):
                    pr = g2 % 2

                    @blk.tensor
                    def _(eng, g=g, qc=qc, g2=g2, pr=pr, PV=PV):
                        c.wait_at(eng, "S", exp_tok[pr])
                        if g2 == 0:
                            c.wait_at(eng, "V", tok["cmb"])
                        for i in range(2):
                            kc = g2 * 2 + i
                            vw = c.BB[:, VW0 + (kc * 2 + g % 2) * 65:VW0 + (kc * 2 + g % 2) * 65 + 65]
                            ins = eng.matmul(PV[0:65, 0:TC], vw,
                                             c.BB[:, EXS + pr * 1024 + i * TC:EXS + pr * 1024 + (i + 1) * TC],
                                             start=(kc == 0), stop=(kc == 15))
                        pvu_tok[pr] = c.inc(ins, "T")

                emit_scores(0)
                for g2 in range(8):
                    emit_exp(g2)
                    if g2 + 1 < 8:
                        emit_scores(g2 + 1)
                    emit_pv(g2)

                @blk.vector
                def _(eng, PV=PV):
                    c.wait_at(eng, "T", c.cnt["T"])
                    with nc.allow_low_precision(reason="f32r recip row for bcast matmul"):
                        ins = eng.reciprocal(c.MR[0:1, 256:768][0:1, 0:TC], PV[64:65, 0:TC])
                    tok["rcp"] = c.inc(ins, "V")

                @blk.tensor
                def _(eng, g=g, qc=qc, BV=BV):
                    for kc in range(16):
                        c.wait_at(eng, "G", btok[(g, kc, qc)])
                        vw = c.BB[:, VW0 + (kc * 2 + g % 2) * 65:VW0 + (kc * 2 + g % 2) * 65 + 65]
                        ins = eng.matmul(BV[0:65, 0:TC], vw,
                                         c.WB[:, kc * TC:(kc + 1) * TC],
                                         start=(kc == 0), stop=(kc == 15))
                        bslot_tok[kc] = c.inc(ins, "T")

                @blk.tensor
                def _(eng):
                    c.wait_at(eng, "V", tok["rcp"])
                    ins = eng.matmul(c.PS[6][0:64, 0:TC], c.MR[0:1, 65:129],
                                     c.MR[0:1, 256:768][0:1, 0:TC], start=True, stop=True)
                    tok["bct"] = c.inc(ins, "T")

                @blk.scalar
                def _(eng):
                    c.wait_at(eng, "T", tok["bct"])
                    c.wait_at(eng, "V", tok["cmb"])
                    ins = eng.activation(c.SC[0:64, 0:TC], c.PS[6][0:64, 0:TC], AF.Identity)
                    tok["bcc"] = c.inc(ins, "S")

                @blk.vector
                def _(eng, g=g, qc=qc, PV=PV, BV=BV):
                    c.wait_at(eng, "S", tok["bcc"])
                    c.wait_at(eng, "T", c.cnt["T"])
                    tmp = c.SC[0:64, 512:1024]
                    eng.tensor_tensor(tmp, PV[0:64, 0:TC], c.SC[0:64, 0:TC], ALU.mult)
                    dst = c.BA[(g % 2) * 64:(g % 2) * 64 + 64,
                               c.dcols(g // 2, qc * TC, (qc + 1) * TC)]
                    ins = eng.tensor_tensor(dst, tmp, BV[0:64, 0:TC], ALU.add)
                    tok["cmb"] = c.inc(ins, "V")

    _emit_norm_fm(c, c.BA, c.BA, 1e-3, subtract_mean=True, gb=True, src_bf16=True)
    for tc in range(NTC):
        @blk.vector
        def _(eng, tc=tc):
            c.wait(eng, "V", "S")
            for j in range(NDT):
                ins = eng.tensor_tensor(c.XR[:, c.dcols(j, tc * TC, (tc + 1) * TC)],
                                        c.XR[:, c.dcols(j, tc * TC, (tc + 1) * TC)],
                                        c.BA[:, c.dcols(j, tc * TC, (tc + 1) * TC)], ALU.add)
            c.inc(ins, "V")
    _emit_norm_fm(c, c.XR, c.XR, 1e-5, subtract_mean=True)


def _emit_l1_ffn(c, dram):
    nc, blk = c.nc, c.block
    P0 = lambda ch: c.SC[:, ch * 16:(ch + 1) * 16]

    @blk.gpsimd
    def _(eng):
        c.wait(eng, "T", "V", "S")
        ins = eng.dma_start(out=c.MR[:, 1280:1408],
                            in_=dram["rw"].rearrange("(kt p) e -> p kt e", p=128))
        c.inc(ins, "G", 16)
        ins = eng.dma_start(out=c.MR[0:1, 1552:1568], in_=dram["rb"].unsqueeze(0))
        c.inc(ins, "G", 16)
        ins = eng.dma_start(out=c.BB[0:16, 2560:4608], in_=dram["OH16"])
        c.inc(ins, "G", 16)
        ins = eng.dma_start(out=c.WB[0:16, 6144:7168], in_=dram["b2"])
        c.inc(ins, "G", 16)

    @blk.tensor
    def _(eng):
        c.wait(eng, "G", "V", "S")
        ins = eng.matmul(c.PS[3][:, 0:16], c.MR[0:1, 1:129], c.MR[0:1, 1552:1568],
                         start=True, stop=True)
        c.inc(ins, "T")

    @blk.scalar
    def _(eng):
        c.wait(eng, "T")
        ins = eng.activation(c.SC[:, 4352:4368], c.PS[3][:, 0:16], AF.Identity)
        c.inc(ins, "S")

    for ch in range(16):
        @blk.tensor
        def _(eng, ch=ch):
            c.wait(eng, "G", "V", "S")
            for kt in range(NDT):
                ins = eng.matmul(c.PS[2][:, 0:16],
                                 c.XR[:, c.dcols(kt, ch * 128, (ch + 1) * 128)],
                                 c.MR[:, 1280 + kt * 16:1280 + (kt + 1) * 16],
                                 start=(kt == 0), stop=(kt == NDT - 1))
            c.inc(ins, "T")

        @blk.vector
        def _(eng, ch=ch):
            c.wait(eng, "T", "S")
            pm = c.SC[:, 4368:4384]
            mx = c.SC[:, 4401:4402]
            eng.tensor_tensor(pm, c.PS[2][:, 0:16], c.SC[:, 4352:4368], ALU.add)
            c.pb(eng, eng.tensor_reduce(mx, pm, AX.X, ALU.max))
            ins = eng.tensor_scalar(pm, pm, mx, None, ALU.subtract)
            c.inc(ins, "V")

        @blk.scalar
        def _(eng, ch=ch):
            c.wait(eng, "V")
            ins = eng.activation(c.SC[:, 4368:4384], c.SC[:, 4368:4384], AF.Exp)
            c.inc(ins, "S")

        @blk.vector
        def _(eng, ch=ch):
            c.wait(eng, "S")
            pm = c.SC[:, 4368:4384]
            sm = c.SC[:, 4401:4402]
            c.pb(eng, eng.tensor_reduce(sm, pm, AX.X, ALU.add))
            c.pb(eng, eng.reciprocal(sm, sm))
            c.pb(eng, eng.tensor_scalar(P0(ch), pm, sm, None, ALU.mult))
            indsum = c.SC[:, 4384:4400]
            wsum = c.SC[:, 4400:4401]
            tmu = c.SC[:, 4402:4418]
            mm_ = c.SC[:, 4418:4419]
            eng.tensor_copy(pm, P0(ch))
            eng.memset(indsum, 0.0)
            c.pb(eng, eng.memset(wsum, 0.0))
            for k4 in range(4):
                c.pb(eng, eng.tensor_reduce(mm_, pm, AX.X, ALU.max))
                c.pb(eng, eng.tensor_scalar(tmu, pm, mm_, None, ALU.is_equal))
                c.pb(eng, eng.tensor_tensor(indsum, indsum, tmu, ALU.add))
                c.pb(eng, eng.tensor_tensor(wsum, wsum, mm_, ALU.add))
                c.pb(eng, eng.tensor_tensor(tmu, tmu, pm, ALU.mult))
                c.pb(eng, eng.tensor_tensor(pm, pm, tmu, ALU.subtract))
            c.pb(eng, eng.reciprocal(wsum, wsum))
            c.pb(eng, eng.tensor_tensor(tmu, P0(ch), indsum, ALU.mult))
            ins = eng.tensor_scalar(tmu, tmu, wsum, None, ALU.mult)
            c.inc(ins, "V")

        @blk.tensor
        def _(eng, ch=ch):
            c.wait(eng, "V")
            ins = eng.transpose(c.PS[3][0:16, 0:128], c.SC[:, 4402:4418], c.IDT[:])
            c.inc(ins, "T")

        @blk.vector
        def _(eng, ch=ch):
            c.wait(eng, "T")
            ins = eng.tensor_copy(c.BB[0:16, ch * 128:(ch + 1) * 128],
                                  c.PS[3][0:16, 0:128])
            c.inc(ins, "V")

    wtok = {}

    tok = {"gbt": 0, "gbc": 0, "comb": 0}

    ctr = {"w1": 0, "w2": 0}
    for s_ in range(3):
        tok[("w1u", s_)] = c.cnt["T"]
        tok[("w2u", s_)] = c.cnt["T"]

    def load_w1(e, fc, _slot_unused=None):
        slot = ctr["w1"] % 3
        ctr["w1"] += 1
        wtok[("w1s", e, fc)] = slot
        src = dram["W1_1"][e][:, fc * 128:(fc + 1) * 128]
        src = src.rearrange("(kt p) f -> p kt f", p=128)

        @blk.gpsimd
        def _(eng, slot=slot):
            c.wait_at(eng, "T", tok.get(("w1u", slot), 0))
            ins = eng.dma_start(out=c.WR[:, slot * 1024:(slot + 1) * 1024], in_=src)
            wtok[("w1", e, fc)] = c.inc(ins, "G", 16)

    def load_w2(e, oc, _slot_unused=None):
        slot = ctr["w2"] % 3
        ctr["w2"] += 1
        wtok[("w2s", e, oc)] = slot
        src = dram["W2_1"][e][:, oc * 128:(oc + 1) * 128]
        src = src.rearrange("(fc p) o -> p fc o", p=128)

        @blk.gpsimd
        def _(eng, slot=slot):
            c.wait_at(eng, "T", tok.get(("w2u", slot), 0))
            ins = eng.dma_start(out=c.WB[:, slot * 2048:slot * 2048 + 2048], in_=src)
            wtok[("w2", e, oc)] = c.inc(ins, "G", 16)

    def load_b1(e):
        @blk.sync
        def _(eng):
            c.wait(eng, "S")
            ins = eng.dma_start(out=c.MISC[:, 104:120], in_=dram["b1c"][e])
            wtok[("b1", e)] = c.inc(ins, "Y", 16)

    pf_s = {0: c.cnt["S"], 1: c.cnt["S"]}
    pf_v = {0: c.cnt["V"], 1: c.cnt["V"]}
    tok["gbc"] = c.cnt["S"]
    tok["comb"] = c.cnt["V"]
    for half in range(2):
        t0 = half * 1024
        load_w1(0, 0, 0)
        load_w1(0, 1, 1)
        for e in range(NE1):
            load_b1(e)
            for fc in range(16):
                la, lb = (e, fc + 2) if fc + 2 < 16 else (e + 1, fc + 2 - 16)
                if la < NE1:
                    load_w1(la, lb, (fc + 2) % 3)
                pr = fc % 2
                bk0, bk1 = c.PS[2 * pr], c.PS[2 * pr + 1]

                @blk.tensor
                def _(eng, e=e, fc=fc, pr=pr, bk0=bk0, bk1=bk1, t0=t0):
                    c.wait_at(eng, "G", wtok[("w1", e, fc)])
                    c.wait_at(eng, "S", pf_s[pr])
                    c.wait_at(eng, "V", pf_v[pr])
                    slot = wtok[("w1s", e, fc)]
                    for tq, bank in ((0, bk0), (1, bk1)):
                        for kt in range(NDT):
                            ins = eng.matmul(
                                bank[:, 0:TC],
                                c.WR[:, slot * 1024 + kt * 128:slot * 1024 + (kt + 1) * 128],
                                c.XR[:, c.dcols(kt, t0 + tq * TC, t0 + (tq + 1) * TC)],
                                start=(kt == 0), stop=(kt == NDT - 1))
                    tok[("h1t", pr)] = tok[("w1u", slot)] = c.inc(ins, "T")

                @blk.scalar
                def _(eng, e=e, fc=fc, pr=pr, bk0=bk0):
                    c.wait_at(eng, "T", tok[("h1t", pr)])
                    c.wait_at(eng, "Y", wtok[("b1", e)])
                    ins = eng.activation(c.BA[:, fc * 1024:(fc + 1) * 1024],
                                         c.PSA[:, pr * 1024:(pr + 1) * 1024], AF.Gelu,
                                         bias=c.mcol(104, fc))
                    pf_s[pr] = c.inc(ins, "S")

            load_w2(e, 0, 0)
            load_w2(e, 1, 1)
            for oc in range(NDT):
                la, lb = (e, oc + 2) if oc + 2 < NDT else (e + 1, oc + 2 - NDT)
                if la < NE1:
                    load_w2(la, lb, (oc + 2) % 3)
                pr = oc % 2
                bk0, bk1 = c.PS[2 * pr], c.PS[2 * pr + 1]

                @blk.tensor
                def _(eng, e=e, oc=oc, pr=pr, bk0=bk0, bk1=bk1, half=half):
                    c.wait_at(eng, "G", wtok[("w2", e, oc)])
                    c.wait_at(eng, "S", max(pf_s[0], pf_s[1]))
                    c.wait_at(eng, "V", pf_v[pr])
                    slot = wtok[("w2s", e, oc)]
                    for tq, bank in ((0, bk0), (1, bk1)):
                        for fc in range(16):
                            ins = eng.matmul(
                                bank[:, 0:TC],
                                c.WB[:, slot * 2048 + fc * 128:slot * 2048 + (fc + 1) * 128],
                                c.BA[:, fc * 1024 + tq * TC:fc * 1024 + (tq + 1) * TC],
                                start=(fc == 0), stop=(fc == 15))
                    tok[("h2t", pr)] = tok[("w2u", slot)] = c.inc(ins, "T")
                    c.wait_at(eng, "S", tok["gbc"])
                    eng.matmul(c.PS[6][:, 0:TC],
                               c.BB[0:16, 2560 + e * 128:2560 + (e + 1) * 128],
                               c.BB[0:16, half * 1024:half * 1024 + TC],
                               start=True, stop=True)
                    ins = eng.matmul(c.PS[7][:, 0:TC],
                                     c.BB[0:16, 2560 + e * 128:2560 + (e + 1) * 128],
                                     c.BB[0:16, half * 1024 + TC:half * 1024 + 1024],
                                     start=True, stop=True)
                    tok["gbt"] = c.inc(ins, "T")

                @blk.scalar
                def _(eng):
                    c.wait_at(eng, "T", tok["gbt"])
                    c.wait_at(eng, "V", tok["comb"])
                    eng.activation(c.BB[:, 2048:2560], c.PS[6][:, 0:TC], AF.Identity)
                    ins = eng.activation(c.BB[:, 4608:5120], c.PS[7][:, 0:TC], AF.Identity)
                    tok["gbc"] = c.inc(ins, "S")

                @blk.vector
                def _(eng, e=e, oc=oc, pr=pr, bk0=bk0, bk1=bk1):
                    c.wait_at(eng, "T", tok[("h2t", pr)])
                    c.wait_at(eng, "S", tok["gbc"])
                    acc = c.SC[:, oc * 1024:(oc + 1) * 1024]
                    for tq, bank, gc in ((0, bk0, 2048), (1, bk1, 4608)):
                        a_ = acc[:, tq * TC:(tq + 1) * TC]
                        gb = c.BB[:, gc:gc + 512]
                        if e == 0:
                            ins = eng.tensor_tensor(a_, bank[:, 0:TC], gb, ALU.mult)
                        else:
                            tmp = c.BB[:, 5120:5632]
                            eng.tensor_tensor(tmp, bank[:, 0:TC], gb, ALU.mult)
                            ins = eng.tensor_tensor(a_, a_, tmp, ALU.add)
                    pf_v[pr] = tok["comb"] = c.inc(ins, "V")

        for oc in range(NDT):
            for tq in range(2):
                @blk.tensor
                def _(eng, oc=oc, tq=tq, half=half):
                    c.wait(eng, "V", "S")
                    ins = eng.matmul(c.PS[5][:, 0:TC],
                                     c.WB[0:16, 6144 + oc * 128:6144 + (oc + 1) * 128],
                                     c.BB[0:16, half * 1024 + tq * TC:half * 1024 + (tq + 1) * TC],
                                     start=True, stop=True)
                    c.inc(ins, "T")

                @blk.vector
                def _(eng, oc=oc, tq=tq):
                    c.wait(eng, "T")
                    acc = c.SC[:, oc * 1024 + tq * TC:oc * 1024 + (tq + 1) * TC]
                    ins = eng.tensor_tensor(acc, acc, c.PS[5][:, 0:TC], ALU.add)
                    c.inc(ins, "V")

        for j in range(NDT):
            @blk.vector
            def _(eng, j=j, t0=t0):
                c.wait(eng, "V", "S")
                ins = eng.tensor_tensor(
                    c.XR[:, c.dcols(j, t0, t0 + 1024)],
                    c.XR[:, c.dcols(j, t0, t0 + 1024)],
                    c.SC[:, j * 1024:(j + 1) * 1024], ALU.add)
                c.inc(ins, "V")

    _emit_norm_fm(c, c.XR, c.XR, 1e-5, subtract_mean=True)






def _emit_barrier(c):
    blk = c.block

    @blk.tensor
    def _(eng):
        c.wait(eng, "V", "S", "G", "Y")
        ins = eng.nop()
        c.inc(ins, "T")

    @blk.vector
    def _(eng):
        c.wait(eng, "T", "S", "G", "Y")
        ins = eng.engine_nop()
        c.inc(ins, "V")

    @blk.scalar
    def _(eng):
        c.wait(eng, "T", "V", "G", "Y")
        ins = eng.nop()
        c.inc(ins, "S")

    @blk.gpsimd
    def _(eng):
        c.wait(eng, "T", "V", "S")

    @blk.sync
    def _(eng):
        c.wait(eng, "T", "V", "S")


def _emit_prologue(c, dram):
    blk = c.block

    @blk.sync
    def _(eng):
        ins = eng.dma_start(out=c.IDT[:], in_=dram["ident"])
        c.inc(ins, "Y", 16)

    @blk.gpsimd
    def _(eng):
        ins = eng.dma_start(out=c.MR[:, 0:1], in_=dram["onecol"])
        c.inc(ins, "G", 16)
        ins = eng.dma_start(out=c.MR[0:1, 1:129], in_=dram["onerow"])
        c.inc(ins, "G", 16)

    @blk.vector
    def _(eng):
        eng.memset(c.ONES[:], 1.0)
        eng.memset(c.OB[:], 1.0)
        eng.memset(c.MISC[:, 120:121], -80.0)
        eng.memset(c.MISC[:, 121:122], 1e-6)
        eng.memset(c.MISC[:, 122:123], 1e-5)
        ins = eng.memset(c.MISC[:, 123:124], 1e-3)
        c.inc(ins, "V")

    if c.phases[0] == "l0a":
        for tt in range(T // 128):
            @blk.sync
            def _(eng, tt=tt):
                c.wait(eng, "T")
                ins = eng.dma_start(out=c.SC[:, 0:1024],
                                    in_=dram["x0"][tt * 128:(tt + 1) * 128, :])
                c.inc(ins, "Y", 16)

            vbase = c.cnt["V"]

            @blk.tensor
            def _(eng, tt=tt, vbase=vbase):
                c.wait(eng, "Y")
                for j in range(NDT):
                    if j == 0:
                        c.wait_at(eng, "V", vbase)
                    if j >= 2:
                        c.wait_at(eng, "V", vbase + j - 1)
                    ins = eng.transpose(c.PS[4 + j % 2][:, 0:128],
                                        c.SC[:, j * 128:(j + 1) * 128], c.IDT[:])
                    c.inc(ins, "T")

            @blk.vector
            def _(eng, tt=tt):
                tbase = c.cnt["T"] - NDT
                for j in range(NDT):
                    c.wait_at(eng, "T", tbase + j + 1)
                    ins = eng.tensor_copy(c.XR[:, c.dcols(j, tt * 128, tt * 128 + 128)],
                                          c.PS[4 + j % 2][:, 0:128])
                    c.inc(ins, "V")
    else:
        @blk.gpsimd
        def _(eng):
            ins = eng.dma_start(out=c.XR[:], in_=dram["xin"])
            c.inc(ins, "G", 16)


def _emit_epilogue(c, dram):
    nc, blk = c.nc, c.block
    if c.phases[-1] == "l1f":
        OTM = c.SC[:, 0:1024]
        for tt in range(T // 128):
            vbase = c.cnt["V"]
            otok = c.cnt["O"]

            @blk.tensor
            def _(eng, tt=tt, vbase=vbase):
                c.wait(eng, "G", "S")
                for j in range(NDT):
                    if j == 0:
                        c.wait_at(eng, "V", vbase)
                    if j >= 2:
                        c.wait_at(eng, "V", vbase + j - 1)
                    ins = eng.transpose(c.PS[4 + j % 2][:, 0:128],
                                        c.XR[:, c.dcols(j, tt * 128, tt * 128 + 128)].bitcast(F32),
                                        c.IDT[:])
                    c.inc(ins, "T")

            @blk.vector
            def _(eng, tt=tt, otok=otok):
                tbase = c.cnt["T"] - NDT
                c.wait_at(eng, "O", otok)
                for j in range(NDT):
                    c.wait_at(eng, "T", tbase + j + 1)
                    ins = eng.tensor_copy(OTM[:, j * 128:(j + 1) * 128],
                                          c.PS[4 + j % 2][:, 0:128])
                    c.inc(ins, "V")

            @blk.sync
            def _(eng, tt=tt):
                c.wait(eng, "V")
                ins = eng.dma_start(out=dram["y"][tt * 128:(tt + 1) * 128, :], in_=OTM)
                c.inc(ins, "O", 16)
    else:
        @blk.sync
        def _(eng):
            c.wait(eng, "V", "S", "T", "G")
            ins = eng.dma_start(out=dram["xout"], in_=c.XR[:].bitcast(F32))
            c.inc(ins, "O", 16)

    @blk.sync
    def _(eng):
        c.wait_at(eng, "O", c.cnt["O"])



def build_program(phases):
    nc = bass.Bass("TRN2", target_bir_lowering=False, debug=False)
    dram = {}

    def din(name, shape, dt=F32):
        dram[name] = nc.dram_tensor(name, list(shape), dt, kind="ExternalInput").ap()

    def dout(name, shape, dt=F32):
        dram[name] = nc.dram_tensor(name, list(shape), dt, kind="ExternalOutput").ap()

    if phases[0] == "l0a":
        din("x0", (T, D))
    else:
        din("xin", (128, NDT * T))
    if phases[-1] == "l1f":
        dout("y", (T, D))
    else:
        dout("xout", (128, NDT * T))
    if "l0a" in phases:
        din("wrT", (NL, D, D)); din("wiT", (NL, D, D)); din("woT", (NL, D, D))
        din("cwc", (NL, 128, 32)); din("cb", (NL, 128, 8)); din("wrb", (NL, 128, 8))
        din("wib", (NL, 128, 8)); din("la", (NL, 128, 8)); din("nw", (NL, 128, 8))
        din("now", (128, 8))
    if "l0f" in phases:
        din("Wg", (D, NE0)); din("W1_0", (NE0, D, F0)); din("W2_0", (NE0, F0, D))
        din("G64", (64, 64)); din("OH8", (NE0, NE0 * 128))
        if _DBG["on"]:
            dout("dbg_pr", (128, 128)); dout("dbg_th", (64, 1))
            dout("dbg_gate", (8, 2048)); dout("dbg_acc", (128, 8192))
    if "l0f" in phases:
        dram["cc_in"] = nc.dram_tensor("cc_in", [NE0, T], F32, kind="Internal").ap()
        dram["cc_out"] = nc.dram_tensor("cc_out", [NE0 * B, T], F32, kind="Internal").ap()
    if "l1a" in phases and _DBG["on"]:
        dout("dbg_attn", (128, NDT * T))
    if "l1a" in phases:
        din("Wq", (D, D)); din("Wk", (D, D)); din("Wv", (D, D))
        din("gamma", (128, 8)); din("beta", (128, 8)); din("biasT", (NH, T, T))
    if "l1f" in phases:
        din("rw", (D, NE1)); din("rb", (NE1,))
        din("W1_1", (NE1, D, F1)); din("b1c", (NE1, 128, 16))
        din("W2_1", (NE1, F1, D)); din("b2", (NE1, D))
        din("OH16", (NE1, NE1 * 128))
    din("ident", (128, 128)); din("onecol", (128, 1)); din("onerow", (1, 128))

    es = contextlib.ExitStack()
    with es:
        c = Ctx(nc, es, phases)
        _emit_prologue(c, dram)
        for ph in phases:
            _emit_barrier(c)
            {"l0a": _emit_l0_attn, "l0f": _emit_l0_ffn,
             "l1a": _emit_l1_attn, "l1f": _emit_l1_ffn}[ph](c, dram)
        _emit_barrier(c)
        _emit_epilogue(c, dram)
    return nc


_INPUT_KEYS = {
    "l0a": ["wrT", "wiT", "woT", "cwc", "cb", "wrb", "wib", "la", "nw", "now"],
    "l0f": ["Wg", "W1_0", "W2_0", "G64", "OH8"],
    "l1a": ["Wq", "Wk", "Wv", "gamma", "beta", "biasT"],
    "l1f": ["rw", "rb", "W1_1", "b1c", "W2_1", "b2", "OH16"],
}


def kernel(**inputs):
    host = _prep_host(inputs)
    x0 = np.ascontiguousarray(np.asarray(inputs["L0_attn_x"], dtype=np.float32))
    cur = [np.ascontiguousarray(x0[b]) for b in range(B)]
    for phases in PHASE_GROUPS:
        nc = build_program(phases)
        shared = {"ident": host["ident"], "onecol": host["onecol"],
                  "onerow": host["onerow"]}
        for ph in phases:
            for k in _INPUT_KEYS[ph]:
                shared[k] = host[k]
        in_maps = []
        for b in range(B):
            m = dict(shared)
            if phases[0] == "l0a":
                m["x0"] = cur[b]
            else:
                m["xin"] = cur[b]
            in_maps.append(m)
        res = run_bass_kernel_spmd(nc, in_maps, core_ids=list(range(B)))
        if phases[-1] == "l1f":
            return np.stack([res.results[b]["y"] for b in range(B)], axis=0)
        cur = [np.ascontiguousarray(res.results[b]["xout"]) for b in range(B)]
    raise RuntimeError("phase groups must end with l1f")


# version 10
# speedup vs baseline: 2.5074x; 1.0258x over previous
import contextlib
import numpy as np
import concourse.bass as bass
import concourse.mybir as mybir
from concourse.bass_utils import run_bass_kernel_spmd

F32 = mybir.dt.float32
F32R = mybir.dt.float32r
BF16 = mybir.dt.bfloat16
AF = mybir.ActivationFunctionType
ALU = mybir.AluOpType
AX = mybir.AxisListType

B, T, D = 8, 2048, 1024
NDT = D // 128
TC = 512
NTC = T // TC
NL = 3
KCONV = 4
NE0, F0 = 8, 2048
NE1, F1 = 16, 2048
NH, DH = 16, 64

PHASE_GROUPS = [["l0a", "l0f", "l1a", "l1f"]]
_DBG = {"on": False}

M_CW = 0
M_CB = 32
M_WRB = 40
M_WIB = 48
M_LSA = 56
M_LA = 64
M_NW = 72
M_NOW = 80
M_GAM = 88
M_BET = 96
M_B2 = 104


class Ctx:
    def __init__(self, nc, es, phases):
        self.nc = nc
        self.es = es
        self.phases = phases
        sb = lambda name, shape, dt: es.enter_context(nc.sbuf_tensor(name, shape, dt))
        self.XR = sb("XR", [128, NDT * T], F32R)
        self.BA = sb("BA", [128, NDT * T], BF16)
        self.BB = sb("BB", [128, NDT * T], BF16)
        self.SC = sb("SC", [128, 4 * T], F32)
        self.WB = sb("WB", [128, 8192], BF16)
        self.WR = sb("WR", [128, 3072], F32R)
        self.ONES = sb("ONES", [128, T], F32)
        self.MISC = sb("MISC", [128, 1024], F32)
        self.MR = sb("MR", [128, 1664], F32R)
        self.OB = sb("OB", [128, 2], BF16)
        self.IDT = sb("IDT", [128, 128], F32)
        self.PSA = es.enter_context(nc.psum_tensor("psA", [128, 2048], F32))
        self.PSB = es.enter_context(nc.psum_tensor("psB", [128, 2048], F32))
        self.PS = [self.PSA[:, i * 512:(i + 1) * 512] for i in range(4)] + \
                  [self.PSB[:, i * 512:(i + 1) * 512] for i in range(4)]
        self.sem = {}
        self.cnt = {}
        for name in ["T", "V", "S", "G", "Y", "O", "C"]:
            self.sem[name] = es.enter_context(nc.semaphore("sem" + name))
            self.cnt[name] = 0
        self.block = es.enter_context(nc.Block())

    def inc(self, ins, name, n=1):
        self.cnt[name] += n
        ins.then_inc(self.sem[name], n)
        return self.cnt[name]

    def wait(self, eng, *names):
        for n in names:
            if self.cnt[n] > 0:
                eng.wait_ge(self.sem[n], self.cnt[n])

    def wait_at(self, eng, name, val):
        if val > 0:
            eng.wait_ge(self.sem[name], val)

    def pb(self, eng, ins):
        self.inc(ins, "V")
        eng.wait_ge(self.sem["V"], self.cnt["V"])

    @staticmethod
    def dcols(j, c0=0, c1=T):
        return slice(j * T + c0, j * T + c1)

    def t1(self, c0=0, c1=T): return self.SC[:, 0 * T + c0:0 * T + c1]
    def t2(self, c0=0, c1=T): return self.SC[:, 1 * T + c0:1 * T + c1]
    def t3(self, c0=0, c1=T): return self.SC[:, 2 * T + c0:2 * T + c1]
    def t4(self, c0=0, c1=T): return self.SC[:, 3 * T + c0:3 * T + c1]

    def mcol(self, base, j):
        return self.MISC[:, base + j:base + j + 1]



def _col8(v):
    v = np.asarray(v)
    return np.ascontiguousarray(v.reshape(-1, 128).T)


def _prep_host(inp):
    d = {}
    d["wrT"] = np.ascontiguousarray(np.asarray(inp["L0_attn_wr_w"]).transpose(0, 2, 1))
    d["wiT"] = np.ascontiguousarray(np.asarray(inp["L0_attn_wi_w"]).transpose(0, 2, 1))
    d["woT"] = np.ascontiguousarray(np.asarray(inp["L0_attn_wo_w"]).transpose(0, 2, 1))
    cw = np.asarray(inp["L0_attn_conv_w"])
    d["cwc"] = np.ascontiguousarray(
        np.stack([_col8(cw[l].T.reshape(-1)) for l in range(NL)]))
    for k_src, k_dst in [("L0_attn_conv_b", "cb"), ("L0_attn_wr_b", "wrb"),
                         ("L0_attn_wi_b", "wib"), ("L0_attn_log_a", "la"),
                         ("L0_attn_norm_w", "nw")]:
        v = np.asarray(inp[k_src])
        d[k_dst] = np.ascontiguousarray(np.stack([_col8(v[l]) for l in range(NL)]))
    d["now"] = _col8(inp["L0_attn_norm_out_w"])
    d["Wg"] = np.ascontiguousarray(np.asarray(inp["L0_ffn_Wg"]))
    d["W1_0"] = np.ascontiguousarray(np.asarray(inp["L0_ffn_W1"]))
    d["W2_0"] = np.ascontiguousarray(np.asarray(inp["L0_ffn_W2"]))
    d["Wq"] = np.ascontiguousarray(np.asarray(inp["L1_attn_Wq"]))
    d["Wk"] = np.ascontiguousarray(np.asarray(inp["L1_attn_Wk"]))
    d["Wv"] = np.ascontiguousarray(np.asarray(inp["L1_attn_Wv"]))
    d["gamma"] = _col8(inp["L1_attn_gamma"])
    d["beta"] = _col8(inp["L1_attn_beta"])
    rt = np.asarray(inp["L1_attn_rel_table"])
    r = np.arange(T)
    rel_index = np.clip(r[None, :] - r[:, None] + (T - 1), 0, 2 * T - 2)
    rel_index = np.minimum(rel_index, rt.shape[0] - 1)
    bias0 = rt[rel_index]
    biasg = bias0.reshape(-1).reshape(NH, T, T)
    d["biasT"] = np.ascontiguousarray(biasg.transpose(0, 2, 1))
    d["rw"] = np.ascontiguousarray(np.asarray(inp["L1_ffn_router_w"]))
    d["rb"] = np.ascontiguousarray(np.asarray(inp["L1_ffn_router_b"]))
    d["W1_1"] = np.ascontiguousarray(np.asarray(inp["L1_ffn_w1"]))
    b1 = np.asarray(inp["L1_ffn_b1"])
    d["b1c"] = np.ascontiguousarray(np.stack([_col8(b1[e]) for e in range(NE1)]))
    d["W2_1"] = np.ascontiguousarray(np.asarray(inp["L1_ffn_w2"]))
    d["b2"] = np.ascontiguousarray(np.asarray(inp["L1_ffn_b2"]))
    d["ident"] = np.eye(128, dtype=np.float32)
    d["onecol"] = np.ones((128, 1), np.float32)
    d["onerow"] = np.ones((1, 128), np.float32)
    i8 = np.arange(64) % NE0
    d["G64"] = (i8[:, None] == i8[None, :]).astype(np.float32)
    oh8 = np.zeros((NE0, NE0 * 128), np.float32)
    for e in range(NE0):
        oh8[e, e * 128:(e + 1) * 128] = 1.0
    d["OH8"] = oh8
    oh16 = np.zeros((NE1, NE1 * 128), np.float32)
    for e in range(NE1):
        oh16[e, e * 128:(e + 1) * 128] = 1.0
    d["OH16"] = oh16
    return d



def _emit_norm_fm(c, buf, out_buf, eps, subtract_mean=True, scale_col_of=None,
                  gb=False, src_bf16=False):
    eps_col = c.MISC[:, {1e-6: 121, 1e-5: 122, 1e-3: 123}[eps]:
                        {1e-6: 121, 1e-5: 122, 1e-3: 123}[eps] + 1]
    """out = (buf - mu)/sqrt(var+eps) over the 1024-feature (partition) dim.
    scale_col_of(j): per-feature multiplier fused into the apply (rms nw).
    gb: apply gamma/beta columns from MISC. Processes per 512-token chunk.
    Row scratch: SC[0:1, 4096+q*512] for q in (sx, sq, mu, var).
    """
    nc, blk = c.nc, c.block
    onec = c.OB[:, 0:1] if src_bf16 else c.MR[:, 0:1]
    oner = c.MR[0:1, 1:129]
    inv_n = 1.0 / D
    PS_X, PS_Q = c.PS[6], c.PS[7]
    R_SX = c.SC[0:1, 4096:4608]
    R_MU = c.SC[0:1, 5120:5632]
    R_VR = c.SC[0:1, 5632:6144]
    R_RSTD = c.MR[0:1, 256:768]
    R_ROWB = c.MR[0:1, 768:1280]
    SQV = c.SC.bitcast(BF16)
    for tc in range(NTC):
        @blk.scalar
        def _(eng, tc=tc):
            c.wait(eng, "V", "T")
            for j in range(NDT):
                ins = eng.activation(
                    SQV[:, j * TC:(j + 1) * TC],
                    buf[:, c.dcols(j, tc * TC, (tc + 1) * TC)], AF.Square)
            c.inc(ins, "S")

        @blk.tensor
        def _(eng, tc=tc):
            c.wait(eng, "S", "V")
            if subtract_mean:
                for j in range(NDT):
                    eng.matmul(PS_X[0:1, 0:TC], onec,
                               buf[:, c.dcols(j, tc * TC, (tc + 1) * TC)],
                               start=(j == 0), stop=(j == NDT - 1))
            for j in range(NDT):
                ins = eng.matmul(PS_Q[0:1, 0:TC], c.OB[:, 0:1],
                                 SQV[:, j * TC:(j + 1) * TC],
                                 start=(j == 0), stop=(j == NDT - 1))
            c.inc(ins, "T")

        @blk.vector
        def _(eng, tc=tc):
            c.wait(eng, "T", "S")
            if subtract_mean:
                eng.tensor_scalar(R_MU, PS_X[0:1, 0:TC], inv_n, None, ALU.mult)
                eng.tensor_scalar(R_VR, PS_Q[0:1, 0:TC], inv_n, None, ALU.mult)
                eng.tensor_tensor(R_SX, R_MU, R_MU, ALU.mult)
                ins = eng.tensor_tensor(R_VR, R_VR, R_SX, ALU.subtract)
            else:
                ins = eng.tensor_scalar(R_VR, PS_Q[0:1, 0:TC], inv_n, None, ALU.mult)
            c.inc(ins, "V")

        @blk.scalar
        def _(eng, tc=tc):
            c.wait(eng, "V")
            ins = eng.activation(R_VR, R_VR, AF.Ln, bias=eps_col[0:1, 0:1])
            c.inc(ins, "S")
            eng.wait_ge(c.sem["S"], c.cnt["S"])
            ins = eng.activation(R_RSTD, R_VR, AF.Exp, scale=-0.5)
            c.inc(ins, "S")

        if subtract_mean:
            @blk.vector
            def _(eng, tc=tc):
                c.wait(eng, "S")
                eng.tensor_tensor(R_MU, R_MU, R_RSTD, ALU.mult)
                ins = eng.tensor_scalar(R_ROWB, R_MU, -1.0, None, ALU.mult)
                c.inc(ins, "V")

        @blk.tensor
        def _(eng, tc=tc):
            c.wait(eng, "V", "S")
            ins = eng.matmul(PS_X[:, 0:TC], oner, R_RSTD,
                             start=True, stop=True)
            if subtract_mean:
                ins = eng.matmul(PS_Q[:, 0:TC], oner, R_ROWB,
                                 start=True, stop=True)
            c.inc(ins, "T")

        @blk.scalar
        def _(eng, tc=tc):
            c.wait(eng, "T", "V")
            ins = eng.activation(c.SC[:, 3584:4096], PS_X[:, 0:TC], AF.Identity)
            c.inc(ins, "S")

        @blk.vector
        def _(eng, tc=tc):
            c.wait(eng, "T", "S")
            rbc = c.SC[:, 3584:4096]
            tmp = c.SC[:, 3072:3584]
            for j in range(NDT):
                src = buf[:, c.dcols(j, tc * TC, (tc + 1) * TC)]
                dst = out_buf[:, c.dcols(j, tc * TC, (tc + 1) * TC)]
                if scale_col_of is not None:
                    ins = eng.scalar_tensor_tensor(dst, src, scale_col_of(j), rbc,
                                                   ALU.mult, ALU.mult)
                elif subtract_mean:
                    eng.tensor_tensor(tmp, src, rbc, ALU.mult)
                    if gb:
                        eng.tensor_tensor(tmp, tmp, PS_Q[:, 0:TC], ALU.add)
                        ins = eng.tensor_scalar(dst, tmp, c.mcol(M_GAM, j),
                                                c.mcol(M_BET, j), ALU.mult, ALU.add)
                    else:
                        ins = eng.tensor_tensor(dst, tmp, PS_Q[:, 0:TC], ALU.add)
                else:
                    ins = eng.tensor_tensor(dst, src, rbc, ALU.mult)
            c.inc(ins, "V")



def _emit_l0_attn(c, dram):
    nc, blk = c.nc, c.block
    xc_buf = [c.BA, c.BB, c.BA]
    h_buf = [c.BB, c.BA, c.BB]
    DGB = c.WB[:, 7168:7680]

    def load_cols(dst_c0, dst_c1, src):
        @blk.sync
        def _(eng):
            c.wait(eng, "V", "S", "T")
            ins = eng.dma_start(out=c.MISC[:, dst_c0:dst_c1], in_=src)
            c.inc(ins, "Y", 16)

    ltok = {}
    for l in range(NL):
        XC = xc_buf[l]
        H = h_buf[l]
        if l == 0:
            xin = h_buf[0]
            for tc in range(NTC):
                @blk.vector
                def _(eng, tc=tc):
                    c.wait(eng, "G", "S", "T")
                    for j in range(NDT):
                        ins = eng.tensor_copy(
                            xin[:, c.dcols(j, tc * TC, (tc + 1) * TC)],
                            c.XR[:, c.dcols(j, tc * TC, (tc + 1) * TC)])
                    c.inc(ins, "V")
        else:
            xin = xc_buf[l - 1]
        load_cols(M_CW, M_CW + 32, dram["cwc"][l])
        load_cols(M_CB, M_CB + 8, dram["cb"][l])
        load_cols(M_WRB, M_WRB + 8, dram["wrb"][l])
        load_cols(M_WIB, M_WIB + 8, dram["wib"][l])
        load_cols(M_LA, M_LA + 8, dram["la"][l])
        load_cols(M_NW, M_NW + 8, dram["nw"][l])

        @blk.scalar
        def _(eng, l=l):
            c.wait(eng, "Y", "V")
            ins = eng.activation(c.MISC[:, M_LSA:M_LSA + 8], c.MISC[:, M_LA:M_LA + 8],
                                 AF.Sigmoid)
            c.inc(ins, "S")
            eng.wait_ge(c.sem["S"], c.cnt["S"])
            ins = eng.activation(c.MISC[:, M_LSA:M_LSA + 8], c.MISC[:, M_LSA:M_LSA + 8],
                                 AF.Ln)
            c.inc(ins, "S")

        for j in range(NDT):
            @blk.vector
            def _(eng, l=l, j=j):
                c.wait(eng, "Y")
                c.wait_at(eng, "T", ltok.get("convt", 0))
                dg = DGB
                for k in range(KCONV):
                    ins = eng.tensor_scalar(dg[:, k * 128:(k + 1) * 128], c.IDT[:],
                                            c.mcol(M_CW, k * NDT + j), None, ALU.mult)
                ltok["diag"] = c.inc(ins, "V")

            @blk.tensor
            def _(eng, l=l, j=j, xin=xin):
                c.wait_at(eng, "V", ltok["diag"])
                c.wait_at(eng, "S", ltok.get("convs", 0))
                c.wait_at(eng, "S", ltok.get("wos", 0))
                if l == 0 and j == 0:
                    c.wait(eng, "V", "G")
                dg = DGB
                for tc in range(NTC):
                    bank = c.PS[tc]
                    for k in (3, 2, 1, 0):
                        off = KCONV - 1 - k
                        lo = tc * TC - off
                        if lo >= 0:
                            ins = eng.matmul(bank[:, 0:TC], dg[:, k * 128:(k + 1) * 128],
                                             xin[:, c.dcols(j, lo, lo + TC)],
                                             start=(k == 3), stop=(k == 0))
                        else:
                            ins = eng.matmul(bank[:, off:TC], dg[:, k * 128:(k + 1) * 128],
                                             xin[:, c.dcols(j, 0, TC - off)],
                                             start=(k == 3), stop=(k == 0))
                ltok["convt"] = c.inc(ins, "T")

            @blk.scalar
            def _(eng, j=j, XC=XC):
                c.wait_at(eng, "T", ltok["convt"])
                ins = eng.activation(XC[:, c.dcols(j)], c.PSA[:, 0:2048],
                                     AF.Identity, bias=c.mcol(M_CB, j))
                ltok["convs"] = c.inc(ins, "S")

        wtok = {}

        def load_wblock(kind, l_, jj, slot):
            src = dram[kind][l_][:, jj * 128:(jj + 1) * 128]
            src = src.rearrange("(kt p) o -> p kt o", p=128)

            @blk.gpsimd
            def _(eng):
                c.wait(eng, "T")
                ins = eng.dma_start(out=c.WB[:, slot * 1024:(slot + 1) * 1024], in_=src)
                wtok[(kind, jj)] = c.inc(ins, "G", 16)

        load_wblock("wrT", l, 0, 0)
        load_wblock("wiT", l, 0, 2)

        for j in range(NDT):
            if j + 1 < NDT:
                load_wblock("wrT", l, j + 1, (j + 1) % 2)
                load_wblock("wiT", l, j + 1, 2 + (j + 1) % 2)
            for kind, slot_base, bank_base, bias_c, dst, tk in (
                    ("wrT", 0, 4, M_WRB, c.t1, "r"), ("wiT", 2, 0, M_WIB, c.t4, "i")):
                slot = slot_base + j % 2

                @blk.tensor
                def _(eng, j=j, slot=slot, kind=kind, bank_base=bank_base, XC=XC, tk=tk):
                    c.wait_at(eng, "G", wtok[(kind, j)])
                    c.wait_at(eng, "S", ltok.get(tk + "s", 0))
                    if j == 0:
                        c.wait(eng, "S", "V")
                    for tc in range(NTC):
                        bank = c.PS[bank_base + tc]
                        for kt in range(NDT):
                            ins = eng.matmul(
                                bank[:, 0:TC],
                                c.WB[:, slot * 1024 + kt * 128:slot * 1024 + (kt + 1) * 128],
                                XC[:, c.dcols(kt, tc * TC, (tc + 1) * TC)],
                                start=(kt == 0), stop=(kt == NDT - 1))
                    ltok[tk + "t"] = c.inc(ins, "T")

                @blk.scalar
                def _(eng, j=j, bias_c=bias_c, dst=dst, bank_base=bank_base, tk=tk):
                    c.wait_at(eng, "T", ltok[tk + "t"])
                    c.wait_at(eng, "V", ltok.get("chainv", 0))
                    psrc = c.PSB if bank_base == 4 else c.PSA
                    ins = eng.activation(dst(), psrc[:, 0:2048],
                                         AF.Sigmoid, bias=c.mcol(bias_c, j))
                    ltok[tk + "s"] = c.inc(ins, "S")

            @blk.vector
            def _(eng, j=j):
                c.wait(eng, "S")
                eng.tensor_scalar(c.t2(), c.t1(), c.mcol(M_LSA, j), 8.0,
                                  ALU.mult, ALU.mult)
                eng.tensor_tensor_scan(c.t1(), c.ONES[:], c.t2(), 0.0, ALU.mult, ALU.add)
                ins = eng.tensor_scalar(c.t1(), c.t1(), c.MISC[:, 120:121], None, ALU.max)
                c.inc(ins, "V")

            @blk.scalar
            def _(eng, j=j):
                c.wait(eng, "V")
                eng.activation(c.t3(), c.t1(), AF.Exp)
                eng.activation(c.t1(), c.t1(), AF.Exp, scale=-1.0)
                eng.activation(c.t2(), c.t2(), AF.Exp)
                ins = eng.activation(c.t2(), c.t2(), AF.Square)
                c.inc(ins, "S")

            @blk.vector
            def _(eng, j=j):
                c.wait(eng, "S")
                eng.tensor_scalar(c.t2(), c.t2(), -1.0, 1.0, ALU.mult, ALU.add)
                ins = eng.tensor_scalar(c.t2(), c.t2(), c.MISC[:, 121:122], None, ALU.max)
                c.inc(ins, "V")

            @blk.scalar
            def _(eng, j=j):
                c.wait(eng, "V")
                eng.activation(c.t2(), c.t2(), AF.Ln)
                ins = eng.activation(c.t2(), c.t2(), AF.Exp, scale=0.5)
                c.inc(ins, "S")

            @blk.vector
            def _(eng, j=j, XC=XC, H=H):
                c.wait(eng, "S")
                eng.tensor_tensor(c.t4(), c.t4(), XC[:, c.dcols(j)], ALU.mult)
                eng.tensor_tensor(c.t4(), c.t4(), c.t2(), ALU.mult)
                eng.tensor_tensor(c.t4(), c.t4(), c.t1(), ALU.mult)
                eng.tensor_tensor_scan(c.t2(), c.ONES[:], c.t4(), 0.0, ALU.mult, ALU.add)
                ins = eng.tensor_tensor(H[:, c.dcols(j)], c.t3(), c.t2(), ALU.mult)
                ltok["chainv"] = c.inc(ins, "V")

        load_wblock("woT", l, 0, 4)
        for j in range(NDT):
            if j + 1 < NDT:
                load_wblock("woT", l, j + 1, 4 + (j + 1) % 2)
            slot = 4 + j % 2

            @blk.tensor
            def _(eng, j=j, slot=slot, H=H):
                c.wait_at(eng, "G", wtok[("woT", j)])
                c.wait_at(eng, "S", ltok.get("wos", 0))
                if j == 0:
                    c.wait(eng, "V", "S")
                for tc in range(NTC):
                    bank = c.PS[tc]
                    for kt in range(NDT):
                        ins = eng.matmul(
                            bank[:, 0:TC],
                            c.WB[:, slot * 1024 + kt * 128:slot * 1024 + (kt + 1) * 128],
                            H[:, c.dcols(kt, tc * TC, (tc + 1) * TC)],
                            start=(kt == 0), stop=(kt == NDT - 1))
                ltok["wot"] = c.inc(ins, "T")

            @blk.scalar
            def _(eng, j=j, XC=XC):
                c.wait_at(eng, "T", ltok["wot"])
                ins = eng.activation(XC[:, c.dcols(j)], c.PSA[:, 0:2048], AF.Identity)
                ltok["wos"] = c.inc(ins, "S")

        _emit_norm_fm(c, XC, XC, 1e-6, subtract_mean=False,
                      scale_col_of=lambda j: c.mcol(M_NW, j), src_bf16=True)

    load_cols(M_NOW, M_NOW + 8, dram["now"])
    yb = xc_buf[NL - 1]
    _emit_norm_fm(c, yb, yb, 1e-6, subtract_mean=False,
                  scale_col_of=lambda j: c.mcol(M_NOW, j), src_bf16=True)
    for tc in range(NTC):
        @blk.vector
        def _(eng, tc=tc):
            c.wait(eng, "V", "S", "G")
            for j in range(NDT):
                ins = eng.tensor_tensor(c.XR[:, c.dcols(j, tc * TC, (tc + 1) * TC)],
                                        c.XR[:, c.dcols(j, tc * TC, (tc + 1) * TC)],
                                        yb[:, c.dcols(j, tc * TC, (tc + 1) * TC)], ALU.add)
            c.inc(ins, "V")
    _emit_norm_fm(c, c.XR, c.XR, 1e-5, subtract_mean=True)


def _emit_l0_ffn(c, dram):
    nc, blk = c.nc, c.block
    BBF = c.BB.bitcast(F32)
    PR = lambda ch: c.SC[:, ch * 8:(ch + 1) * 8]
    PALL = c.SC[0:64, 128:2176]
    IND = c.SC[0:64, 2176:4224]
    TH = c.SC[0:64, 4224:4225]
    PLOC = c.SC[0:8, 5120:7168]

    @blk.gpsimd
    def _(eng):
        c.wait(eng, "T", "V", "S")
        ins = eng.dma_start(out=c.MR[:, 1472:1536],
                            in_=dram["Wg"].rearrange("(kt p) e -> p kt e", p=128))
        c.inc(ins, "G", 16)

        ins = eng.dma_start(out=c.BB[0:8, 2560:3584], in_=dram["OH8"])
        c.inc(ins, "G", 16)

    for ch in range(16):
        @blk.tensor
        def _(eng, ch=ch):
            c.wait(eng, "G", "V", "S")
            for kt in range(NDT):
                ins = eng.matmul(c.PS[2][:, 0:8],
                                 c.XR[:, c.dcols(kt, ch * 128, (ch + 1) * 128)],
                                 c.MR[:, 1472 + kt * 8:1472 + (kt + 1) * 8],
                                 start=(kt == 0), stop=(kt == NDT - 1))
            c.inc(ins, "T")

        @blk.vector
        def _(eng, ch=ch):
            c.wait(eng, "T", "S")
            mx = c.SC[:, 4288:4289]
            c.pb(eng, eng.tensor_reduce(mx, c.PS[2][:, 0:8], AX.X, ALU.max))
            ins = eng.tensor_scalar(c.SC[:, 4272:4280], c.PS[2][:, 0:8], mx, None,
                                    ALU.subtract)
            c.inc(ins, "V")

        @blk.scalar
        def _(eng, ch=ch):
            c.wait(eng, "V")
            ins = eng.activation(c.SC[:, 4272:4280], c.SC[:, 4272:4280], AF.Exp)
            c.inc(ins, "S")

        @blk.vector
        def _(eng, ch=ch):
            c.wait(eng, "S")
            sm = c.SC[:, 4289:4290]
            c.pb(eng, eng.tensor_reduce(sm, c.SC[:, 4272:4280], AX.X, ALU.add))
            c.pb(eng, eng.reciprocal(sm, sm))
            ins = eng.tensor_scalar(PR(ch), c.SC[:, 4272:4280], sm, None, ALU.mult)
            c.inc(ins, "V")

        @blk.tensor
        def _(eng, ch=ch):
            c.wait(eng, "V")
            ins = eng.transpose(c.PS[3][0:8, 0:128], PR(ch), c.IDT[:])
            c.inc(ins, "T")

        @blk.vector
        def _(eng, ch=ch):
            c.wait(eng, "T")
            ins = eng.tensor_copy(PLOC[:, ch * 128:(ch + 1) * 128], c.PS[3][0:8, 0:128])
            c.inc(ins, "V")

    @blk.sync
    def _(eng):
        c.wait(eng, "V")
        ins = eng.dma_start(out=dram["cc_in"], in_=PLOC)
        c.inc(ins, "Y", 16)
        ins = eng.dma_start(out=c.SC[0:64, 4320:4384], in_=dram["G64"])
        c.inc(ins, "Y", 16)

    @blk.gpsimd
    def _(eng):
        c.wait(eng, "Y")
        ins = eng.collective_compute(
            "AllGather", ALU.bypass, replica_groups=[list(range(B))],
            ins=[dram["cc_in"].opt()], outs=[dram["cc_out"].opt()])
        c.inc(ins, "C")
        eng.wait_ge(c.sem["C"], c.cnt["C"])
        ins = eng.dma_start(out=PALL, in_=dram["cc_out"])
        c.inc(ins, "G", 16)

    @blk.vector
    def _(eng):
        c.wait(eng, "G")
        ins = eng.memset(TH, 0.5)
        c.inc(ins, "V")
    NIT = 26
    for it in range(NIT):
        step = 0.5 ** (it + 2)

        @blk.vector
        def _(eng, it=it):
            c.wait(eng, "T", "V")
            c.pb(eng, eng.tensor_scalar(IND, PALL, TH, None, ALU.is_gt))
            ins = eng.tensor_reduce(c.SC[0:64, 4384:4385], IND, AX.X, ALU.add)
            c.inc(ins, "V")

        @blk.tensor
        def _(eng, it=it):
            c.wait(eng, "V")
            if it == 0:
                c.wait(eng, "Y")
            ins = eng.matmul(c.PS[2][0:64, 0:1], c.SC[0:64, 4320:4384],
                             c.SC[0:64, 4384:4385], start=True, stop=True)
            c.inc(ins, "T")

        @blk.vector
        def _(eng, it=it, step=step):
            c.wait(eng, "T")
            d = c.SC[0:64, 4226:4227]
            c.pb(eng, eng.tensor_scalar(d, c.PS[2][0:64, 0:1], 2048.0, None, ALU.is_gt))
            c.pb(eng, eng.tensor_scalar(d, d, 2.0, -1.0, ALU.mult, ALU.add))
            ins = eng.scalar_tensor_tensor(TH, d, float(step), TH, ALU.mult, ALU.add)
            c.inc(ins, "V")

    @blk.vector
    def _(eng):
        ins = eng.tensor_scalar(TH, TH, 4.0 * (0.5 ** (NIT + 1)), None, ALU.add)
        c.inc(ins, "V")

    @blk.tensor
    def _(eng):
        c.wait(eng, "V")
        ins = eng.transpose(c.PS[2][0:1, 0:8], TH[0:8, 0:1], c.IDT[0:8, 0:8])
        c.inc(ins, "T")

    @blk.vector
    def _(eng):
        c.wait(eng, "T")
        ins = eng.tensor_copy(c.SC[0:1, 4400:4408], c.PS[2][0:1, 0:8])
        c.inc(ins, "V")

    @blk.tensor
    def _(eng):
        c.wait(eng, "V")
        ins = eng.matmul(c.PS[2][:, 0:8], c.ONES[0:1, 0:128], c.SC[0:1, 4400:4408],
                         start=True, stop=True)
        c.inc(ins, "T")

    @blk.scalar
    def _(eng):
        c.wait(eng, "T")
        ins = eng.activation(c.SC[:, 4240:4248], c.PS[2][:, 0:8], AF.Identity)
        c.inc(ins, "S")

    for ch in range(16):
        @blk.vector
        def _(eng, ch=ch):
            c.wait(eng, "S", "T")
            thb = c.SC[:, 4240:4248]
            sel = c.SC[:, 4256:4264]
            tB = c.SC[:, 4272:4280]
            nselc = c.SC[:, 4288:4289]
            asg = c.SC[:, 4289:4290]
            mxs = c.SC[:, 4290:4291]
            mxa = c.SC[:, 4291:4292]
            wv = c.SC[:, 4292:4293]
            inda = c.SC[:, 4304:4312]
            indb = c.SC[:, 4312:4320]
            c.pb(eng, eng.tensor_tensor(sel, PR(ch), thb, ALU.is_gt))
            c.pb(eng, eng.tensor_reduce(nselc, sel, AX.X, ALU.add))
            c.pb(eng, eng.tensor_scalar(asg, nselc, 0.5, None, ALU.is_gt))
            c.pb(eng, eng.tensor_scalar(tB, PR(ch), 10000.0, None, ALU.add))
            c.pb(eng, eng.tensor_tensor(tB, tB, sel, ALU.mult))
            c.pb(eng, eng.tensor_scalar(tB, tB, 10000.0, None, ALU.subtract))
            c.pb(eng, eng.tensor_reduce(mxs, tB, AX.X, ALU.max))
            c.pb(eng, eng.tensor_reduce(mxa, PR(ch), AX.X, ALU.max))
            c.pb(eng, eng.tensor_tensor(wv, mxs, mxa, ALU.subtract))
            c.pb(eng, eng.scalar_tensor_tensor(wv, wv, asg, mxa, ALU.mult, ALU.add))
            c.pb(eng, eng.tensor_scalar(inda, tB, mxs, None, ALU.is_equal))
            c.pb(eng, eng.tensor_scalar(indb, PR(ch), mxa, None, ALU.is_equal))
            c.pb(eng, eng.tensor_tensor(inda, inda, indb, ALU.subtract))
            c.pb(eng, eng.scalar_tensor_tensor(inda, inda, asg, indb, ALU.mult, ALU.add))
            ins = eng.tensor_scalar(inda, inda, wv, None, ALU.mult)
            c.inc(ins, "V")

        @blk.tensor
        def _(eng, ch=ch):
            c.wait(eng, "V")
            ins = eng.transpose(c.PS[3][0:8, 0:128], c.SC[:, 4304:4312], c.IDT[:])
            c.inc(ins, "T")

        @blk.vector
        def _(eng, ch=ch):
            c.wait(eng, "T")
            ins = eng.tensor_copy(c.BB[0:8, ch * 128:(ch + 1) * 128],
                                  c.PS[3][0:8, 0:128])
            c.inc(ins, "V")

    if _DBG["on"]:
        @blk.sync
        def _(eng):
            c.wait(eng, "V", "S", "T")
            eng.dma_start(out=dram["dbg_pr"], in_=c.SC[:, 0:128]).then_inc(c.sem["O"], 16)
            eng.dma_start(out=dram["dbg_th"], in_=TH).then_inc(c.sem["O"], 16)
            c.cnt["O"] += 32

        @blk.gpsimd
        def _(eng):
            c.wait(eng, "V")
            ins = eng.dma_start(out=dram["dbg_gate"], in_=c.BB[0:8, 0:2048])
            c.inc(ins, "G", 16)

    wtok = {}
    tok = {"gbt": 0, "gbc": 0, "comb": 0}

    ctr = {"w1": 0, "w2": 0}
    for s_ in range(3):
        tok[("w1u", s_)] = c.cnt["T"]
        tok[("w2u", s_)] = c.cnt["T"]

    def load_w1(e, fc, _slot_unused=None):
        slot = ctr["w1"] % 3
        ctr["w1"] += 1
        wtok[("w1s", e, fc)] = slot
        src = dram["W1_0"][e][:, fc * 128:(fc + 1) * 128]
        src = src.rearrange("(kt p) f -> p kt f", p=128)

        @blk.gpsimd
        def _(eng, slot=slot):
            c.wait_at(eng, "T", tok.get(("w1u", slot), 0))
            ins = eng.dma_start(out=c.WR[:, slot * 1024:(slot + 1) * 1024], in_=src)
            wtok[("w1", e, fc)] = c.inc(ins, "G", 16)

    def load_w2(e, oc, _slot_unused=None):
        slot = ctr["w2"] % 3
        ctr["w2"] += 1
        wtok[("w2s", e, oc)] = slot
        src = dram["W2_0"][e][:, oc * 128:(oc + 1) * 128]
        src = src.rearrange("(fc p) o -> p fc o", p=128)

        @blk.gpsimd
        def _(eng, slot=slot):
            c.wait_at(eng, "T", tok.get(("w2u", slot), 0))
            ins = eng.dma_start(out=c.WB[:, slot * 2048:slot * 2048 + 2048], in_=src)
            wtok[("w2", e, oc)] = c.inc(ins, "G", 16)

    pf_s = {0: c.cnt["S"], 1: c.cnt["S"]}
    pf_v = {0: c.cnt["V"], 1: c.cnt["V"]}
    tok["gbc"] = c.cnt["S"]
    tok["comb"] = c.cnt["V"]
    GATE_NE, GB1 = NE0, 3584
    for half in range(2):
        t0 = half * 1024
        load_w1(0, 0, 0)
        load_w1(0, 1, 1)
        for e in range(NE0):
            for fc in range(16):
                la, lb = (e, fc + 2) if fc + 2 < 16 else (e + 1, fc + 2 - 16)
                if la < NE0:
                    load_w1(la, lb, (fc + 2) % 3)
                pr = fc % 2
                bk0, bk1 = c.PS[2 * pr], c.PS[2 * pr + 1]

                @blk.tensor
                def _(eng, e=e, fc=fc, pr=pr, bk0=bk0, bk1=bk1, t0=t0):
                    c.wait_at(eng, "G", wtok[("w1", e, fc)])
                    c.wait_at(eng, "S", pf_s[pr])
                    c.wait_at(eng, "V", pf_v[pr])
                    slot = wtok[("w1s", e, fc)]
                    for tq, bank in ((0, bk0), (1, bk1)):
                        for kt in range(NDT):
                            ins = eng.matmul(
                                bank[:, 0:TC],
                                c.WR[:, slot * 1024 + kt * 128:slot * 1024 + (kt + 1) * 128],
                                c.XR[:, c.dcols(kt, t0 + tq * TC, t0 + (tq + 1) * TC)],
                                start=(kt == 0), stop=(kt == NDT - 1))
                    tok[("h1t", pr)] = tok[("w1u", slot)] = c.inc(ins, "T")

                @blk.scalar
                def _(eng, fc=fc, pr=pr, bk0=bk0):
                    c.wait_at(eng, "T", tok[("h1t", pr)])
                    ins = eng.activation(c.BA[:, fc * 1024:(fc + 1) * 1024],
                                         c.PSA[:, pr * 1024:(pr + 1) * 1024], AF.Gelu)
                    pf_s[pr] = c.inc(ins, "S")

            load_w2(e, 0, 0)
            load_w2(e, 1, 1)

            @blk.tensor
            def _(eng, e=e, half=half):
                c.wait_at(eng, "S", tok["gbc"])
                eng.matmul(c.PS[6][:, 0:TC],
                           c.BB[0:GATE_NE, 2560 + e * 128:2560 + (e + 1) * 128],
                           c.BB[0:GATE_NE, half * 1024:half * 1024 + TC],
                           start=True, stop=True)
                ins = eng.matmul(c.PS[7][:, 0:TC],
                                 c.BB[0:GATE_NE, 2560 + e * 128:2560 + (e + 1) * 128],
                                 c.BB[0:GATE_NE, half * 1024 + TC:half * 1024 + 1024],
                                 start=True, stop=True)
                tok["gbt"] = c.inc(ins, "T")

            @blk.scalar
            def _(eng):
                c.wait_at(eng, "T", tok["gbt"])
                c.wait_at(eng, "V", tok["comb"])
                eng.activation(c.BB[:, 2048:2560], c.PS[6][:, 0:TC], AF.Identity)
                ins = eng.activation(c.BB[:, GB1:GB1 + 512], c.PS[7][:, 0:TC], AF.Identity)
                tok["gbc"] = c.inc(ins, "S")

            for oc in range(NDT):
                la, lb = (e, oc + 2) if oc + 2 < NDT else (e + 1, oc + 2 - NDT)
                if la < NE0:
                    load_w2(la, lb, (oc + 2) % 3)
                pr = oc % 2
                bk0, bk1 = c.PS[2 * pr], c.PS[2 * pr + 1]

                @blk.tensor
                def _(eng, e=e, oc=oc, pr=pr, bk0=bk0, bk1=bk1, half=half):
                    c.wait_at(eng, "G", wtok[("w2", e, oc)])
                    c.wait_at(eng, "S", max(pf_s[0], pf_s[1]))
                    c.wait_at(eng, "V", pf_v[pr])
                    slot = wtok[("w2s", e, oc)]
                    for tq, bank in ((0, bk0), (1, bk1)):
                        for fc in range(16):
                            ins = eng.matmul(
                                bank[:, 0:TC],
                                c.WB[:, slot * 2048 + fc * 128:slot * 2048 + (fc + 1) * 128],
                                c.BA[:, fc * 1024 + tq * TC:fc * 1024 + (tq + 1) * TC],
                                start=(fc == 0), stop=(fc == 15))
                    tok[("h2t", pr)] = tok[("w2u", slot)] = c.inc(ins, "T")

                @blk.vector
                def _(eng, e=e, oc=oc, pr=pr, bk0=bk0, bk1=bk1):
                    c.wait_at(eng, "T", tok[("h2t", pr)])
                    c.wait_at(eng, "S", tok["gbc"])
                    acc = c.SC[:, oc * 1024:(oc + 1) * 1024]
                    for tq, bank, gc in ((0, bk0, 2048), (1, bk1, 3584)):
                        a_ = acc[:, tq * TC:(tq + 1) * TC]
                        gb = c.BB[:, gc:gc + 512]
                        if e == 0:
                            ins = eng.tensor_tensor(a_, bank[:, 0:TC], gb, ALU.mult)
                        else:
                            tmp = c.BB[:, 4608:5120]
                            eng.tensor_tensor(tmp, bank[:, 0:TC], gb, ALU.mult)
                            ins = eng.tensor_tensor(a_, a_, tmp, ALU.add)
                    pf_v[pr] = tok["comb"] = c.inc(ins, "V")

        for j in range(NDT):
            @blk.vector
            def _(eng, j=j, t0=t0):
                c.wait(eng, "V", "S")
                ins = eng.tensor_tensor(
                    c.XR[:, c.dcols(j, t0, t0 + 1024)],
                    c.XR[:, c.dcols(j, t0, t0 + 1024)],
                    c.SC[:, j * 1024:(j + 1) * 1024], ALU.add)
                c.inc(ins, "V")

    _emit_norm_fm(c, c.XR, c.XR, 1e-5, subtract_mean=True)


def _emit_l1_attn(c, dram):
    nc, blk = c.nc, c.block
    QP = lambda c0, c1: c.BB[:, c0:c1]
    VW0 = 4096
    EXS = 6176
    BTR = 8224
    @blk.sync
    def _(eng):
        c.wait(eng, "V", "S", "T")
        ins = eng.dma_start(out=c.MISC[:, M_GAM:M_GAM + 8], in_=dram["gamma"])
        c.inc(ins, "Y", 16)
        ins = eng.dma_start(out=c.MISC[:, M_BET:M_BET + 8], in_=dram["beta"])
        c.inc(ins, "Y", 16)

    wtok = {}

    def load_wpair(kind, gp, slot):
        src = dram[kind][:, gp * 128:(gp + 1) * 128]
        src = src.rearrange("(kt p) o -> p kt o", p=128)

        @blk.gpsimd
        def _(eng):
            c.wait(eng, "T")
            ins = eng.dma_start(out=c.WR[:, slot * 1024:(slot + 1) * 1024], in_=src)
            wtok[(kind, gp)] = c.inc(ins, "G", 16)

    btok = {}

    def load_bias(g, kc, qc):
        src = dram["biasT"][g][kc * 128:(kc + 1) * 128, qc * TC:(qc + 1) * TC]

        @blk.gpsimd
        def _(eng, kc=kc):
            c.wait_at(eng, "T", bslot_tok.get(kc, 0))
            ins = eng.dma_start(out=c.WB[:, kc * TC:(kc + 1) * TC], in_=src)
            btok[(g, kc, qc)] = c.inc(ins, "G", 16)

    tok = {"qkc": c.cnt["S"], "vv": 0, "pv": 0, "cmb": c.cnt["V"],
           "rcp": c.cnt["V"], "bcc": c.cnt["S"]}
    exp_tok = {0: c.cnt["S"], 1: c.cnt["S"]}
    pvu_tok = {0: c.cnt["T"], 1: c.cnt["T"]}
    bslot_tok = {}
    for kc_ in range(16):
        bslot_tok[kc_] = c.cnt["T"]
    for gp in range(NH // 2):
        for idx, kind in enumerate(("Wq", "Wk", "Wv")):
            load_wpair(kind, gp, idx)
        for idx, kind in enumerate(("Wq", "Wk")):
            @blk.tensor
            def _(eng, kind=kind, idx=idx, gp=gp):
                c.wait_at(eng, "G", wtok[(kind, gp)])
                c.wait_at(eng, "S", tok["qkc"])
                c.wait_at(eng, "S", exp_tok[0])
                c.wait_at(eng, "S", exp_tok[1])
                c.wait_at(eng, "V", tok["cmb"])
                for tc in range(NTC):
                    for kt in range(NDT):
                        ins = eng.matmul(
                            c.PS[tc][:, 0:TC],
                            c.WR[:, idx * 1024 + kt * 128:idx * 1024 + (kt + 1) * 128],
                            c.XR[:, c.dcols(kt, tc * TC, (tc + 1) * TC)],
                            start=(kt == 0), stop=(kt == NDT - 1))
                tok["qkt"] = c.inc(ins, "T")

            @blk.scalar
            def _(eng, idx=idx):
                c.wait_at(eng, "T", tok["qkt"])
                ins = eng.activation(c.BB[:, idx * 2048:(idx + 1) * 2048], c.PSA[:, 0:2048], AF.Identity)
                tok["qkc"] = c.inc(ins, "S")

        @blk.vector
        def _(eng, gp=gp):
            c.wait_at(eng, "S", tok["qkc"])
            for tt in range(16):
                for h in range(2):
                    ins = eng.memset(c.BB[:, VW0 + (tt * 2 + h) * 65 + 64:VW0 + (tt * 2 + h) * 65 + 65], 1.0)
            tok["vz"] = c.inc(ins, "V")

        for tt in range(16):
            bank = c.PS[tt % 2]

            @blk.tensor
            def _(eng, tt=tt, bank=bank, gp=gp):
                c.wait_at(eng, "G", wtok[("Wv", gp)])
                if tt == 0:
                    c.wait_at(eng, "S", tok["qkc"])
                    c.wait_at(eng, "V", tok["vz"])
                else:
                    c.wait_at(eng, "V", tok.get(("vc", tt % 2), 0))
                for kt in range(NDT):
                    ins = eng.matmul(
                        bank[:, 0:128],
                        c.XR[:, c.dcols(kt, tt * 128, (tt + 1) * 128)],
                        c.WR[:, 2 * 1024 + kt * 128:2 * 1024 + (kt + 1) * 128],
                        start=(kt == 0), stop=(kt == NDT - 1))
                tok[("vt", tt % 2)] = c.inc(ins, "T")

            @blk.vector
            def _(eng, tt=tt, bank=bank):
                c.wait_at(eng, "T", tok[("vt", tt % 2)])
                eng.tensor_copy(c.BB[:, VW0 + (tt * 2) * 65:VW0 + (tt * 2) * 65 + 64],
                                bank[:, 0:64])
                ins = eng.tensor_copy(c.BB[:, VW0 + (tt * 2 + 1) * 65:VW0 + (tt * 2 + 1) * 65 + 64],
                                      bank[:, 64:128])
                tok[("vc", tt % 2)] = c.inc(ins, "V")

        for hh in range(2):
            g = gp * 2 + hh
            rb = hh * 64

            @blk.tensor
            def _(eng):
                eng.drain()

            for qc in range(NTC):
                PV = c.PS[4]
                BV = c.PS[5]
                def emit_scores(g2):
                    load_bias(g, g2 * 2, qc)
                    load_bias(g, g2 * 2 + 1, qc)
                    pr = g2 % 2
                    bk0, bk1 = c.PS[2 * pr], c.PS[2 * pr + 1]

                    @blk.tensor
                    def _(eng, g=g, rb=rb, qc=qc, g2=g2, pr=pr, bk0=bk0, bk1=bk1):
                        c.wait_at(eng, "S", exp_tok[pr])
                        if g2 < 2:
                            c.wait_at(eng, "V", tok.get(("vc", 0), 0))
                            c.wait_at(eng, "V", tok.get(("vc", 1), 0))
                        for i, bank in ((0, bk0), (1, bk1)):
                            kc = g2 * 2 + i
                            ins = eng.matmul(
                                bank[:, 0:TC],
                                c.BB[rb:rb + 64, 2048 + kc * 128:2048 + (kc + 1) * 128],
                                c.BB[rb:rb + 64, qc * TC:(qc + 1) * TC],
                                start=True, stop=True)
                        tok[("sct", pr)] = c.inc(ins, "T")

                def emit_exp(g2):
                    pr = g2 % 2

                    @blk.scalar
                    def _(eng, g2=g2, pr=pr):
                        c.wait_at(eng, "T", tok[("sct", pr)])
                        c.wait_at(eng, "T", pvu_tok[pr])
                        ins = eng.activation(
                            c.BB[:, EXS + pr * 1024:EXS + (pr + 1) * 1024],
                            c.PSA[:, pr * 1024:(pr + 1) * 1024], AF.Exp,
                            scale=float(D) ** -0.5)
                        exp_tok[pr] = c.inc(ins, "S")

                def emit_pv(g2):
                    pr = g2 % 2

                    @blk.tensor
                    def _(eng, g=g, qc=qc, g2=g2, pr=pr, PV=PV):
                        c.wait_at(eng, "S", exp_tok[pr])
                        if g2 == 0:
                            c.wait_at(eng, "V", tok["cmb"])
                        for i in range(2):
                            kc = g2 * 2 + i
                            vw = c.BB[:, VW0 + (kc * 2 + g % 2) * 65:VW0 + (kc * 2 + g % 2) * 65 + 65]
                            ins = eng.matmul(PV[0:65, 0:TC], vw,
                                             c.BB[:, EXS + pr * 1024 + i * TC:EXS + pr * 1024 + (i + 1) * TC],
                                             start=(kc == 0), stop=(kc == 15))
                        pvu_tok[pr] = c.inc(ins, "T")

                emit_scores(0)
                for g2 in range(8):
                    emit_exp(g2)
                    if g2 + 1 < 8:
                        emit_scores(g2 + 1)
                    emit_pv(g2)

                @blk.vector
                def _(eng, PV=PV):
                    c.wait_at(eng, "T", c.cnt["T"])
                    with nc.allow_low_precision(reason="f32r recip row for bcast matmul"):
                        ins = eng.reciprocal(c.MR[0:1, 256:768][0:1, 0:TC], PV[64:65, 0:TC])
                    tok["rcp"] = c.inc(ins, "V")

                @blk.tensor
                def _(eng, g=g, qc=qc, BV=BV):
                    for kc in range(16):
                        c.wait_at(eng, "G", btok[(g, kc, qc)])
                        vw = c.BB[:, VW0 + (kc * 2 + g % 2) * 65:VW0 + (kc * 2 + g % 2) * 65 + 65]
                        ins = eng.matmul(BV[0:65, 0:TC], vw,
                                         c.WB[:, kc * TC:(kc + 1) * TC],
                                         start=(kc == 0), stop=(kc == 15))
                        bslot_tok[kc] = c.inc(ins, "T")

                @blk.tensor
                def _(eng):
                    c.wait_at(eng, "V", tok["rcp"])
                    ins = eng.matmul(c.PS[6][0:64, 0:TC], c.MR[0:1, 65:129],
                                     c.MR[0:1, 256:768][0:1, 0:TC], start=True, stop=True)
                    tok["bct"] = c.inc(ins, "T")

                @blk.scalar
                def _(eng):
                    c.wait_at(eng, "T", tok["bct"])
                    c.wait_at(eng, "V", tok["cmb"])
                    ins = eng.activation(c.SC[0:64, 0:TC], c.PS[6][0:64, 0:TC], AF.Identity)
                    tok["bcc"] = c.inc(ins, "S")

                @blk.vector
                def _(eng, g=g, qc=qc, PV=PV, BV=BV):
                    c.wait_at(eng, "S", tok["bcc"])
                    c.wait_at(eng, "T", c.cnt["T"])
                    tmp = c.SC[0:64, 512:1024]
                    eng.tensor_tensor(tmp, PV[0:64, 0:TC], c.SC[0:64, 0:TC], ALU.mult)
                    dst = c.BA[(g % 2) * 64:(g % 2) * 64 + 64,
                               c.dcols(g // 2, qc * TC, (qc + 1) * TC)]
                    ins = eng.tensor_tensor(dst, tmp, BV[0:64, 0:TC], ALU.add)
                    tok["cmb"] = c.inc(ins, "V")

    _emit_norm_fm(c, c.BA, c.BA, 1e-3, subtract_mean=True, gb=True, src_bf16=True)
    for tc in range(NTC):
        @blk.vector
        def _(eng, tc=tc):
            c.wait(eng, "V", "S")
            for j in range(NDT):
                ins = eng.tensor_tensor(c.XR[:, c.dcols(j, tc * TC, (tc + 1) * TC)],
                                        c.XR[:, c.dcols(j, tc * TC, (tc + 1) * TC)],
                                        c.BA[:, c.dcols(j, tc * TC, (tc + 1) * TC)], ALU.add)
            c.inc(ins, "V")
    _emit_norm_fm(c, c.XR, c.XR, 1e-5, subtract_mean=True)


def _emit_l1_ffn(c, dram):
    nc, blk = c.nc, c.block
    P0 = lambda ch: c.SC[:, ch * 16:(ch + 1) * 16]

    @blk.gpsimd
    def _(eng):
        c.wait(eng, "T", "V", "S")
        ins = eng.dma_start(out=c.MR[:, 1280:1408],
                            in_=dram["rw"].rearrange("(kt p) e -> p kt e", p=128))
        c.inc(ins, "G", 16)
        ins = eng.dma_start(out=c.MR[0:1, 1552:1568], in_=dram["rb"].unsqueeze(0))
        c.inc(ins, "G", 16)
        ins = eng.dma_start(out=c.BB[0:16, 2560:4608], in_=dram["OH16"])
        c.inc(ins, "G", 16)
        ins = eng.dma_start(out=c.WB[0:16, 6144:7168], in_=dram["b2"])
        c.inc(ins, "G", 16)

    @blk.tensor
    def _(eng):
        c.wait(eng, "G", "V", "S")
        ins = eng.matmul(c.PS[3][:, 0:16], c.MR[0:1, 1:129], c.MR[0:1, 1552:1568],
                         start=True, stop=True)
        c.inc(ins, "T")

    @blk.scalar
    def _(eng):
        c.wait(eng, "T")
        ins = eng.activation(c.SC[:, 4352:4368], c.PS[3][:, 0:16], AF.Identity)
        c.inc(ins, "S")

    for ch in range(16):
        @blk.tensor
        def _(eng, ch=ch):
            c.wait(eng, "G", "V", "S")
            for kt in range(NDT):
                ins = eng.matmul(c.PS[2][:, 0:16],
                                 c.XR[:, c.dcols(kt, ch * 128, (ch + 1) * 128)],
                                 c.MR[:, 1280 + kt * 16:1280 + (kt + 1) * 16],
                                 start=(kt == 0), stop=(kt == NDT - 1))
            c.inc(ins, "T")

        @blk.vector
        def _(eng, ch=ch):
            c.wait(eng, "T", "S")
            pm = c.SC[:, 4368:4384]
            mx = c.SC[:, 4401:4402]
            eng.tensor_tensor(pm, c.PS[2][:, 0:16], c.SC[:, 4352:4368], ALU.add)
            c.pb(eng, eng.tensor_reduce(mx, pm, AX.X, ALU.max))
            ins = eng.tensor_scalar(pm, pm, mx, None, ALU.subtract)
            c.inc(ins, "V")

        @blk.scalar
        def _(eng, ch=ch):
            c.wait(eng, "V")
            ins = eng.activation(c.SC[:, 4368:4384], c.SC[:, 4368:4384], AF.Exp)
            c.inc(ins, "S")

        @blk.vector
        def _(eng, ch=ch):
            c.wait(eng, "S")
            pm = c.SC[:, 4368:4384]
            sm = c.SC[:, 4401:4402]
            c.pb(eng, eng.tensor_reduce(sm, pm, AX.X, ALU.add))
            c.pb(eng, eng.reciprocal(sm, sm))
            c.pb(eng, eng.tensor_scalar(P0(ch), pm, sm, None, ALU.mult))
            indsum = c.SC[:, 4384:4400]
            wsum = c.SC[:, 4400:4401]
            tmu = c.SC[:, 4402:4418]
            mm_ = c.SC[:, 4418:4419]
            eng.tensor_copy(pm, P0(ch))
            eng.memset(indsum, 0.0)
            c.pb(eng, eng.memset(wsum, 0.0))
            for k4 in range(4):
                c.pb(eng, eng.tensor_reduce(mm_, pm, AX.X, ALU.max))
                c.pb(eng, eng.tensor_scalar(tmu, pm, mm_, None, ALU.is_equal))
                c.pb(eng, eng.tensor_tensor(indsum, indsum, tmu, ALU.add))
                c.pb(eng, eng.tensor_tensor(wsum, wsum, mm_, ALU.add))
                c.pb(eng, eng.tensor_tensor(tmu, tmu, pm, ALU.mult))
                c.pb(eng, eng.tensor_tensor(pm, pm, tmu, ALU.subtract))
            c.pb(eng, eng.reciprocal(wsum, wsum))
            c.pb(eng, eng.tensor_tensor(tmu, P0(ch), indsum, ALU.mult))
            ins = eng.tensor_scalar(tmu, tmu, wsum, None, ALU.mult)
            c.inc(ins, "V")

        @blk.tensor
        def _(eng, ch=ch):
            c.wait(eng, "V")
            ins = eng.transpose(c.PS[3][0:16, 0:128], c.SC[:, 4402:4418], c.IDT[:])
            c.inc(ins, "T")

        @blk.vector
        def _(eng, ch=ch):
            c.wait(eng, "T")
            ins = eng.tensor_copy(c.BB[0:16, ch * 128:(ch + 1) * 128],
                                  c.PS[3][0:16, 0:128])
            c.inc(ins, "V")

    wtok = {}

    tok = {"gbt": 0, "gbc": 0, "comb": 0}

    ctr = {"w1": 0, "w2": 0}
    for s_ in range(3):
        tok[("w1u", s_)] = c.cnt["T"]
        tok[("w2u", s_)] = c.cnt["T"]

    def load_w1(e, fc, _slot_unused=None):
        slot = ctr["w1"] % 3
        ctr["w1"] += 1
        wtok[("w1s", e, fc)] = slot
        src = dram["W1_1"][e][:, fc * 128:(fc + 1) * 128]
        src = src.rearrange("(kt p) f -> p kt f", p=128)

        @blk.gpsimd
        def _(eng, slot=slot):
            c.wait_at(eng, "T", tok.get(("w1u", slot), 0))
            ins = eng.dma_start(out=c.WR[:, slot * 1024:(slot + 1) * 1024], in_=src)
            wtok[("w1", e, fc)] = c.inc(ins, "G", 16)

    def load_w2(e, oc, _slot_unused=None):
        slot = ctr["w2"] % 3
        ctr["w2"] += 1
        wtok[("w2s", e, oc)] = slot
        src = dram["W2_1"][e][:, oc * 128:(oc + 1) * 128]
        src = src.rearrange("(fc p) o -> p fc o", p=128)

        @blk.gpsimd
        def _(eng, slot=slot):
            c.wait_at(eng, "T", tok.get(("w2u", slot), 0))
            ins = eng.dma_start(out=c.WB[:, slot * 2048:slot * 2048 + 2048], in_=src)
            wtok[("w2", e, oc)] = c.inc(ins, "G", 16)

    def load_b1(e):
        @blk.sync
        def _(eng):
            c.wait(eng, "S")
            ins = eng.dma_start(out=c.MISC[:, 104:120], in_=dram["b1c"][e])
            wtok[("b1", e)] = c.inc(ins, "Y", 16)

    pf_s = {0: c.cnt["S"], 1: c.cnt["S"]}
    pf_v = {0: c.cnt["V"], 1: c.cnt["V"]}
    tok["gbc"] = c.cnt["S"]
    tok["comb"] = c.cnt["V"]
    GATE_NE, GB1 = NE1, 4608
    for half in range(2):
        t0 = half * 1024
        load_w1(0, 0, 0)
        load_w1(0, 1, 1)
        for e in range(NE1):
            load_b1(e)
            for fc in range(16):
                la, lb = (e, fc + 2) if fc + 2 < 16 else (e + 1, fc + 2 - 16)
                if la < NE1:
                    load_w1(la, lb, (fc + 2) % 3)
                pr = fc % 2
                bk0, bk1 = c.PS[2 * pr], c.PS[2 * pr + 1]

                @blk.tensor
                def _(eng, e=e, fc=fc, pr=pr, bk0=bk0, bk1=bk1, t0=t0):
                    c.wait_at(eng, "G", wtok[("w1", e, fc)])
                    c.wait_at(eng, "S", pf_s[pr])
                    c.wait_at(eng, "V", pf_v[pr])
                    slot = wtok[("w1s", e, fc)]
                    for tq, bank in ((0, bk0), (1, bk1)):
                        for kt in range(NDT):
                            ins = eng.matmul(
                                bank[:, 0:TC],
                                c.WR[:, slot * 1024 + kt * 128:slot * 1024 + (kt + 1) * 128],
                                c.XR[:, c.dcols(kt, t0 + tq * TC, t0 + (tq + 1) * TC)],
                                start=(kt == 0), stop=(kt == NDT - 1))
                    tok[("h1t", pr)] = tok[("w1u", slot)] = c.inc(ins, "T")

                @blk.scalar
                def _(eng, e=e, fc=fc, pr=pr, bk0=bk0):
                    c.wait_at(eng, "T", tok[("h1t", pr)])
                    c.wait_at(eng, "Y", wtok[("b1", e)])
                    ins = eng.activation(c.BA[:, fc * 1024:(fc + 1) * 1024],
                                         c.PSA[:, pr * 1024:(pr + 1) * 1024], AF.Gelu,
                                         bias=c.mcol(104, fc))
                    pf_s[pr] = c.inc(ins, "S")

            load_w2(e, 0, 0)
            load_w2(e, 1, 1)

            @blk.tensor
            def _(eng, e=e, half=half):
                c.wait_at(eng, "S", tok["gbc"])
                eng.matmul(c.PS[6][:, 0:TC],
                           c.BB[0:GATE_NE, 2560 + e * 128:2560 + (e + 1) * 128],
                           c.BB[0:GATE_NE, half * 1024:half * 1024 + TC],
                           start=True, stop=True)
                ins = eng.matmul(c.PS[7][:, 0:TC],
                                 c.BB[0:GATE_NE, 2560 + e * 128:2560 + (e + 1) * 128],
                                 c.BB[0:GATE_NE, half * 1024 + TC:half * 1024 + 1024],
                                 start=True, stop=True)
                tok["gbt"] = c.inc(ins, "T")

            @blk.scalar
            def _(eng):
                c.wait_at(eng, "T", tok["gbt"])
                c.wait_at(eng, "V", tok["comb"])
                eng.activation(c.BB[:, 2048:2560], c.PS[6][:, 0:TC], AF.Identity)
                ins = eng.activation(c.BB[:, GB1:GB1 + 512], c.PS[7][:, 0:TC], AF.Identity)
                tok["gbc"] = c.inc(ins, "S")

            for oc in range(NDT):
                la, lb = (e, oc + 2) if oc + 2 < NDT else (e + 1, oc + 2 - NDT)
                if la < NE1:
                    load_w2(la, lb, (oc + 2) % 3)
                pr = oc % 2
                bk0, bk1 = c.PS[2 * pr], c.PS[2 * pr + 1]

                @blk.tensor
                def _(eng, e=e, oc=oc, pr=pr, bk0=bk0, bk1=bk1, half=half):
                    c.wait_at(eng, "G", wtok[("w2", e, oc)])
                    c.wait_at(eng, "S", max(pf_s[0], pf_s[1]))
                    c.wait_at(eng, "V", pf_v[pr])
                    slot = wtok[("w2s", e, oc)]
                    for tq, bank in ((0, bk0), (1, bk1)):
                        for fc in range(16):
                            ins = eng.matmul(
                                bank[:, 0:TC],
                                c.WB[:, slot * 2048 + fc * 128:slot * 2048 + (fc + 1) * 128],
                                c.BA[:, fc * 1024 + tq * TC:fc * 1024 + (tq + 1) * TC],
                                start=(fc == 0), stop=(fc == 15))
                    tok[("h2t", pr)] = tok[("w2u", slot)] = c.inc(ins, "T")

                @blk.vector
                def _(eng, e=e, oc=oc, pr=pr, bk0=bk0, bk1=bk1):
                    c.wait_at(eng, "T", tok[("h2t", pr)])
                    c.wait_at(eng, "S", tok["gbc"])
                    acc = c.SC[:, oc * 1024:(oc + 1) * 1024]
                    for tq, bank, gc in ((0, bk0, 2048), (1, bk1, 4608)):
                        a_ = acc[:, tq * TC:(tq + 1) * TC]
                        gb = c.BB[:, gc:gc + 512]
                        if e == 0:
                            ins = eng.tensor_tensor(a_, bank[:, 0:TC], gb, ALU.mult)
                        else:
                            tmp = c.BB[:, 5120:5632]
                            eng.tensor_tensor(tmp, bank[:, 0:TC], gb, ALU.mult)
                            ins = eng.tensor_tensor(a_, a_, tmp, ALU.add)
                    pf_v[pr] = tok["comb"] = c.inc(ins, "V")

        for oc in range(NDT):
            for tq in range(2):
                @blk.tensor
                def _(eng, oc=oc, tq=tq, half=half):
                    c.wait(eng, "V", "S")
                    ins = eng.matmul(c.PS[5][:, 0:TC],
                                     c.WB[0:16, 6144 + oc * 128:6144 + (oc + 1) * 128],
                                     c.BB[0:16, half * 1024 + tq * TC:half * 1024 + (tq + 1) * TC],
                                     start=True, stop=True)
                    c.inc(ins, "T")

                @blk.vector
                def _(eng, oc=oc, tq=tq):
                    c.wait(eng, "T")
                    acc = c.SC[:, oc * 1024 + tq * TC:oc * 1024 + (tq + 1) * TC]
                    ins = eng.tensor_tensor(acc, acc, c.PS[5][:, 0:TC], ALU.add)
                    c.inc(ins, "V")

        for j in range(NDT):
            @blk.vector
            def _(eng, j=j, t0=t0):
                c.wait(eng, "V", "S")
                ins = eng.tensor_tensor(
                    c.XR[:, c.dcols(j, t0, t0 + 1024)],
                    c.XR[:, c.dcols(j, t0, t0 + 1024)],
                    c.SC[:, j * 1024:(j + 1) * 1024], ALU.add)
                c.inc(ins, "V")

    _emit_norm_fm(c, c.XR, c.XR, 1e-5, subtract_mean=True)






def _emit_barrier(c):
    blk = c.block

    @blk.tensor
    def _(eng):
        c.wait(eng, "V", "S", "G", "Y")
        ins = eng.nop()
        c.inc(ins, "T")

    @blk.vector
    def _(eng):
        c.wait(eng, "T", "S", "G", "Y")
        ins = eng.engine_nop()
        c.inc(ins, "V")

    @blk.scalar
    def _(eng):
        c.wait(eng, "T", "V", "G", "Y")
        ins = eng.nop()
        c.inc(ins, "S")

    @blk.gpsimd
    def _(eng):
        c.wait(eng, "T", "V", "S")

    @blk.sync
    def _(eng):
        c.wait(eng, "T", "V", "S")


def _emit_prologue(c, dram):
    blk = c.block

    @blk.sync
    def _(eng):
        ins = eng.dma_start(out=c.IDT[:], in_=dram["ident"])
        c.inc(ins, "Y", 16)

    @blk.gpsimd
    def _(eng):
        ins = eng.dma_start(out=c.MR[:, 0:1], in_=dram["onecol"])
        c.inc(ins, "G", 16)
        ins = eng.dma_start(out=c.MR[0:1, 1:129], in_=dram["onerow"])
        c.inc(ins, "G", 16)

    @blk.vector
    def _(eng):
        eng.memset(c.ONES[:], 1.0)
        eng.memset(c.OB[:], 1.0)
        eng.memset(c.MISC[:, 120:121], -80.0)
        eng.memset(c.MISC[:, 121:122], 1e-6)
        eng.memset(c.MISC[:, 122:123], 1e-5)
        ins = eng.memset(c.MISC[:, 123:124], 1e-3)
        c.inc(ins, "V")

    if c.phases[0] == "l0a":
        for tt in range(T // 128):
            @blk.sync
            def _(eng, tt=tt):
                c.wait(eng, "T")
                ins = eng.dma_start(out=c.SC[:, 0:1024],
                                    in_=dram["x0"][tt * 128:(tt + 1) * 128, :])
                c.inc(ins, "Y", 16)

            vbase = c.cnt["V"]

            @blk.tensor
            def _(eng, tt=tt, vbase=vbase):
                c.wait(eng, "Y")
                for j in range(NDT):
                    if j == 0:
                        c.wait_at(eng, "V", vbase)
                    if j >= 2:
                        c.wait_at(eng, "V", vbase + j - 1)
                    ins = eng.transpose(c.PS[4 + j % 2][:, 0:128],
                                        c.SC[:, j * 128:(j + 1) * 128], c.IDT[:])
                    c.inc(ins, "T")

            @blk.vector
            def _(eng, tt=tt):
                tbase = c.cnt["T"] - NDT
                for j in range(NDT):
                    c.wait_at(eng, "T", tbase + j + 1)
                    ins = eng.tensor_copy(c.XR[:, c.dcols(j, tt * 128, tt * 128 + 128)],
                                          c.PS[4 + j % 2][:, 0:128])
                    c.inc(ins, "V")
    else:
        @blk.gpsimd
        def _(eng):
            ins = eng.dma_start(out=c.XR[:], in_=dram["xin"])
            c.inc(ins, "G", 16)


def _emit_epilogue(c, dram):
    nc, blk = c.nc, c.block
    if c.phases[-1] == "l1f":
        OTM = c.SC[:, 0:1024]
        for tt in range(T // 128):
            vbase = c.cnt["V"]
            otok = c.cnt["O"]

            @blk.tensor
            def _(eng, tt=tt, vbase=vbase):
                c.wait(eng, "G", "S")
                for j in range(NDT):
                    if j == 0:
                        c.wait_at(eng, "V", vbase)
                    if j >= 2:
                        c.wait_at(eng, "V", vbase + j - 1)
                    ins = eng.transpose(c.PS[4 + j % 2][:, 0:128],
                                        c.XR[:, c.dcols(j, tt * 128, tt * 128 + 128)].bitcast(F32),
                                        c.IDT[:])
                    c.inc(ins, "T")

            @blk.vector
            def _(eng, tt=tt, otok=otok):
                tbase = c.cnt["T"] - NDT
                c.wait_at(eng, "O", otok)
                for j in range(NDT):
                    c.wait_at(eng, "T", tbase + j + 1)
                    ins = eng.tensor_copy(OTM[:, j * 128:(j + 1) * 128],
                                          c.PS[4 + j % 2][:, 0:128])
                    c.inc(ins, "V")

            @blk.sync
            def _(eng, tt=tt):
                c.wait(eng, "V")
                ins = eng.dma_start(out=dram["y"][tt * 128:(tt + 1) * 128, :], in_=OTM)
                c.inc(ins, "O", 16)
    else:
        @blk.sync
        def _(eng):
            c.wait(eng, "V", "S", "T", "G")
            ins = eng.dma_start(out=dram["xout"], in_=c.XR[:].bitcast(F32))
            c.inc(ins, "O", 16)

    @blk.sync
    def _(eng):
        c.wait_at(eng, "O", c.cnt["O"])



def build_program(phases):
    nc = bass.Bass("TRN2", target_bir_lowering=False, debug=False)
    dram = {}

    def din(name, shape, dt=F32):
        dram[name] = nc.dram_tensor(name, list(shape), dt, kind="ExternalInput").ap()

    def dout(name, shape, dt=F32):
        dram[name] = nc.dram_tensor(name, list(shape), dt, kind="ExternalOutput").ap()

    if phases[0] == "l0a":
        din("x0", (T, D))
    else:
        din("xin", (128, NDT * T))
    if phases[-1] == "l1f":
        dout("y", (T, D))
    else:
        dout("xout", (128, NDT * T))
    if "l0a" in phases:
        din("wrT", (NL, D, D)); din("wiT", (NL, D, D)); din("woT", (NL, D, D))
        din("cwc", (NL, 128, 32)); din("cb", (NL, 128, 8)); din("wrb", (NL, 128, 8))
        din("wib", (NL, 128, 8)); din("la", (NL, 128, 8)); din("nw", (NL, 128, 8))
        din("now", (128, 8))
    if "l0f" in phases:
        din("Wg", (D, NE0)); din("W1_0", (NE0, D, F0)); din("W2_0", (NE0, F0, D))
        din("G64", (64, 64)); din("OH8", (NE0, NE0 * 128))
        if _DBG["on"]:
            dout("dbg_pr", (128, 128)); dout("dbg_th", (64, 1))
            dout("dbg_gate", (8, 2048)); dout("dbg_acc", (128, 8192))
    if "l0f" in phases:
        dram["cc_in"] = nc.dram_tensor("cc_in", [NE0, T], F32, kind="Internal").ap()
        dram["cc_out"] = nc.dram_tensor("cc_out", [NE0 * B, T], F32, kind="Internal").ap()
    if "l1a" in phases and _DBG["on"]:
        dout("dbg_attn", (128, NDT * T))
    if "l1a" in phases:
        din("Wq", (D, D)); din("Wk", (D, D)); din("Wv", (D, D))
        din("gamma", (128, 8)); din("beta", (128, 8)); din("biasT", (NH, T, T))
    if "l1f" in phases:
        din("rw", (D, NE1)); din("rb", (NE1,))
        din("W1_1", (NE1, D, F1)); din("b1c", (NE1, 128, 16))
        din("W2_1", (NE1, F1, D)); din("b2", (NE1, D))
        din("OH16", (NE1, NE1 * 128))
    din("ident", (128, 128)); din("onecol", (128, 1)); din("onerow", (1, 128))

    es = contextlib.ExitStack()
    with es:
        c = Ctx(nc, es, phases)
        _emit_prologue(c, dram)
        for ph in phases:
            _emit_barrier(c)
            {"l0a": _emit_l0_attn, "l0f": _emit_l0_ffn,
             "l1a": _emit_l1_attn, "l1f": _emit_l1_ffn}[ph](c, dram)
        _emit_barrier(c)
        _emit_epilogue(c, dram)
    return nc


_INPUT_KEYS = {
    "l0a": ["wrT", "wiT", "woT", "cwc", "cb", "wrb", "wib", "la", "nw", "now"],
    "l0f": ["Wg", "W1_0", "W2_0", "G64", "OH8"],
    "l1a": ["Wq", "Wk", "Wv", "gamma", "beta", "biasT"],
    "l1f": ["rw", "rb", "W1_1", "b1c", "W2_1", "b2", "OH16"],
}


def kernel(**inputs):
    host = _prep_host(inputs)
    x0 = np.ascontiguousarray(np.asarray(inputs["L0_attn_x"], dtype=np.float32))
    cur = [np.ascontiguousarray(x0[b]) for b in range(B)]
    for phases in PHASE_GROUPS:
        nc = build_program(phases)
        shared = {"ident": host["ident"], "onecol": host["onecol"],
                  "onerow": host["onerow"]}
        for ph in phases:
            for k in _INPUT_KEYS[ph]:
                shared[k] = host[k]
        in_maps = []
        for b in range(B):
            m = dict(shared)
            if phases[0] == "l0a":
                m["x0"] = cur[b]
            else:
                m["xin"] = cur[b]
            in_maps.append(m)
        res = run_bass_kernel_spmd(nc, in_maps, core_ids=list(range(B)))
        if phases[-1] == "l1f":
            return np.stack([res.results[b]["y"] for b in range(B)], axis=0)
        cur = [np.ascontiguousarray(res.results[b]["xout"]) for b in range(B)]
    raise RuntimeError("phase groups must end with l1f")
